# version 08
# baseline (speedup 1.0000x reference)
import contextlib

import numpy as np
import concourse.bass as bass
import concourse.mybir as mybir
from concourse.bass_utils import run_bass_kernel_spmd

F32 = mybir.dt.float32
I32 = mybir.dt.int32
ALU = mybir.AluOpType
ACT = mybir.ActivationFunctionType
AXL = mybir.AxisListType

B, S, D = 8, 2048, 1024
H, HD = 16, 64
HP = 8
E, I_EXP, SI = 8, 512, 1024
N_ALL = B * S
LAMBDA_SCALE = -4.0 / 64.0
LN_EPS = 1e-5
ATT_EPS = 1e-6
GATE_EPS = 1e-8
ABSMASK = 0x7FFFFFFF

ENGINES = ("sync", "vector", "scalar", "tensor", "gpsimd")


class Prog:

    def __init__(self):
        self.steps = {e: [] for e in ENGINES}
        self.ctr = {}

    def val(self, sem):
        return self.ctr.get(id(sem), 0)

    def bump(self, sem, n):
        self.ctr[id(sem)] = self.ctr.get(id(sem), 0) + n
        return self.ctr[id(sem)]

    def set_dma_queues(self, mapping, boundary_sem):
        self._dmaq = {id(s): q for s, q in mapping.items()}
        self._qbound = {id(s): 0 for s in mapping}

    def wait(self, eng, sem, val=None):
        v = self.val(sem) if val is None else val
        if v <= 0:
            return
        dq = getattr(self, "_dmaq", None)
        if dq is not None and id(sem) in dq and eng != dq[id(sem)]:
            if self._qbound[id(sem)] < v:
                q = dq[id(sem)]
                self.steps[q].append(lambda e, sem=sem, v=v: e.wait_ge(sem, v))
                self._qbound[id(sem)] = v
        self.steps[eng].append(lambda e, sem=sem, v=v: e.wait_ge(sem, v))

    def emit(self, eng, fn, inc=None, inc_n=1):
        if inc is not None:
            self.bump(inc, inc_n)
        self.steps[eng].append(
            lambda e, fn=fn, inc=inc, inc_n=inc_n:
            (fn(e).then_inc(inc, inc_n) if inc is not None else fn(e)))

    def finalize(self, block):
        for name in ENGINES:
            steps = self.steps[name]

            def mk(steps):
                def _(e):
                    for st in steps:
                        st(e)
                return _
            getattr(block, name)(mk(steps))


def build(debug=()):
    nc = bass.Bass("TRN2", target_bir_lowering=False)
    dt = F32

    xo = nc.dram_tensor("xo", [S, D], dt, kind="ExternalInput").ap()
    xf = nc.dram_tensor("xf", [N_ALL, D], dt, kind="ExternalInput").ap()
    qkv_w = nc.dram_tensor("qkv_w", [3 * D, D], dt, kind="ExternalInput").ap()
    qkv_b = nc.dram_tensor("qkv_b", [3 * D, 1], dt, kind="ExternalInput").ap()
    proj_w = nc.dram_tensor("proj_w", [D, D], dt, kind="ExternalInput").ap()
    proj_b = nc.dram_tensor("proj_b", [D, 1], dt, kind="ExternalInput").ap()
    dwc_w = nc.dram_tensor("dwc_w", [D, 3], dt, kind="ExternalInput").ap()
    dwc_b = nc.dram_tensor("dwc_b", [D, 1], dt, kind="ExternalInput").ap()
    gate_w = nc.dram_tensor("gate_w", [E, D], dt, kind="ExternalInput").ap()
    lb_bias = nc.dram_tensor("lb_bias", [E, 1], dt, kind="ExternalInput").ap()
    exp_gw = nc.dram_tensor("exp_gw", [E * I_EXP, D], dt, kind="ExternalInput").ap()
    exp_uw = nc.dram_tensor("exp_uw", [E * I_EXP, D], dt, kind="ExternalInput").ap()
    exp_dw = nc.dram_tensor("exp_dw", [E * D, I_EXP], dt, kind="ExternalInput").ap()
    sh_gw = nc.dram_tensor("sh_gw", [SI, D], dt, kind="ExternalInput").ap()
    sh_uw = nc.dram_tensor("sh_uw", [SI, D], dt, kind="ExternalInput").ap()
    sh_dw = nc.dram_tensor("sh_dw", [D, SI], dt, kind="ExternalInput").ap()
    out = nc.dram_tensor("out", [S, D], dt, kind="ExternalOutput").ap()
    xnT_hbm = nc.dram_tensor("xnT_scr", [D, S], dt).ap()

    dbg = {}
    for name, shape in debug:
        dbg[name] = nc.dram_tensor(name, list(shape), dt, kind="ExternalOutput").ap()

    p = Prog()

    with contextlib.ExitStack() as _st:
        def sb(name, shape):
            return _st.enter_context(nc.sbuf_tensor(name, shape, dt))

        def psum(name, shape):
            return _st.enter_context(nc.psum_tensor(name, shape, dt))

        def sem(name):
            return _st.enter_context(nc.semaphore(name))

        R_t = sb("R", [128, 8 * S])
        ident_t = sb("ident", [128, 128])
        onesrow_t = sb("onesrow", [1, 128])
        onescol_t = sb("onescol", [128, 1])
        masks_t = sb("masks", [128, 16 * 32])
        selwT_t = sb("selwT", [128, 128])
        selown_t = sb("selown", [128, 16 * E])
        wmeanB_t = sb("wmeanB", [128, E])
        negGB_t = sb("negGB", [128, E])
        lbB_t = sb("lbB", [128, E])
        gwT_t = sb("gwT", [128, 8 * E])
        wacc_t = sb("wacc", [128, 2 * E])
        small_t = sb("small", [128, 64])
        projb_t = sb("projb", [128, 8])
        xt0_t = sb("xt0", [128, D])
        xt1_t = sb("xt1", [128, D])
        sqt_t = sb("sqt", [128, D])
        xnt_t = sb("xnt", [128, D])
        xTg_t = sb("xTg", [128, 8 * 128])
        gs_t = sb("gs", [128, 24 * E])
        stat_t = sb("stat", [128, 8])
        stage_t = sb("stage", [128, D])
        wt3_t = sb("wt3", [128, 3 * 8 * 128])
        xnt2_t = sb("xnt2", [128, 8 * 512])
        dbuf_t = sb("dbuf", [128, 7 * 1024])
        nqT2_t = sb("nqT2", [128, S])
        kT2_t = sb("kT2", [128, S])
        vT2_t = sb("vT2", [128, S])
        vaug0_t = sb("vaug0", [128, 32 * 65])
        vaug1_t = sb("vaug1", [128, 32 * 65])
        vtok_t = sb("vtok", [128, 16 * 128])
        localT_t = sb("localT", [128, S])
        ps_sig = psum("ps_sig", [128, 4096])

        class _PsView:

            def __init__(self, col0):
                self.col0 = col0

            def __getitem__(self, idx):
                rows, cols = idx
                c0 = self.col0 + (cols.start or 0)
                c1 = self.col0 + (cols.stop if cols.stop is not None else 512)
                return ps_sig[rows, c0:c1]

        ps_a = _PsView(0)
        ps_b = _PsView(512)
        m_dma = sem("m_dma")
        m_gdma = sem("m_gdma")
        m_T = sem("m_T")
        m_TA = sem("m_TA")
        m_TC = sem("m_TC")
        m_V = sem("m_V")
        m_PV = sem("m_PV")
        m_S = sem("m_S")
        m_PS = sem("m_PS")
        m_SE = sem("m_SE")
        m_G = sem("m_G")
        block = _st.enter_context(nc.Block())

        V, Sc, T, G, Y = "vector", "scalar", "tensor", "gpsimd", "sync"
        p.set_dma_queues({m_dma: Y, m_gdma: G}, m_G)

        def vf(fn):
            p.emit(V, fn, inc=m_V)
            p.wait(V, m_V)

        def sf(fn):
            p.emit(Sc, fn, inc=m_S)
            p.wait(Sc, m_S)

        p.emit(V, lambda e: e.memset(ident_t[:], 0.0))
        p.emit(V, lambda e: e.memset(onesrow_t[:], 1.0))
        p.emit(V, lambda e: e.memset(onescol_t[:], 1.0))
        p.emit(V, lambda e: e.memset(masks_t[:], 0.0))
        p.emit(V, lambda e: e.memset(wacc_t[:], 0.0), inc=m_V)
        p.wait(G, m_V)
        p.emit(G, lambda e: e.affine_select(
            out=ident_t[:], in_=ident_t[:], compare_op=ALU.not_equal,
            fill=1.0, base=0, pattern=[[-1, 128]], channel_multiplier=1))
        for s_ in range(16):
            p.emit(G, lambda e, s_=s_: e.memset(
                masks_t[0:64, 32 * s_ + s_: 32 * s_ + s_ + 1], 1.0))
            p.emit(G, lambda e, s_=s_: e.memset(
                masks_t[64:128, 32 * s_ + 16 + s_: 32 * s_ + 16 + s_ + 1], 1.0),
                inc=(m_G if s_ == 15 else None))

        p.emit(Y, lambda e: e.dma_start(out=gs_t[0:E, 0:1], in_=lb_bias[:, :]), inc=m_dma, inc_n=16)
        for cb in range(8):
            p.emit(Y, lambda e, cb=cb: e.dma_start(
                out=projb_t[:, cb:cb + 1],
                in_=proj_b[cb * 128:(cb + 1) * 128, 0:1]), inc=m_dma, inc_n=16)
        p.emit(Y, lambda e: e.dma_start(out=xTg_t[0:E, 0:D], in_=gate_w[:, :]), inc=m_dma, inc_n=16)
        p.wait(T, m_dma)
        p.wait(T, m_G)
        for dc in range(8):
            p.emit(T, lambda e, dc=dc: e.transpose(
                ps_a[0:128, dc * E:(dc + 1) * E], xTg_t[0:E, dc * 128:(dc + 1) * 128],
                ident_t[0:E, 0:E]), inc=(m_T if dc == 7 else None))
        p.wait(V, m_T)
        p.emit(V, lambda e: e.tensor_copy(out=gwT_t[:, :], in_=ps_a[:, 0:8 * E]))
        vf(lambda e: e.reduce_sum(out=small_t[0:E, 0:1], in_=xTg_t[0:E, 0:D], axis=AXL.X))
        vf(lambda e: e.tensor_scalar(
            out=small_t[0:E, 1:2], in0=small_t[0:E, 0:1], scalar1=-1.0, scalar2=None,
            op0=ALU.mult))
        p.wait(T, m_V)
        p.emit(T, lambda e: e.transpose(ps_b[0:1, 0:E], small_t[0:E, 1:2], ident_t[0:E, 0:E]))
        p.emit(T, lambda e: e.transpose(ps_b[0:1, E:2 * E], gs_t[0:E, 0:1],
                                        ident_t[0:E, 0:E]), inc=m_T)
        p.wait(V, m_T)
        p.emit(V, lambda e: e.tensor_copy(out=small_t[0:1, 8:8 + 2 * E], in_=ps_b[0:1, 0:2 * E]), inc=m_V)
        p.wait(T, m_V)
        p.emit(T, lambda e: e.matmul(
            ps_a[0:128, 0:2 * E], onesrow_t[0:1, :], small_t[0:1, 8:8 + 2 * E],
            start=True, stop=True), inc=m_T)
        p.wait(V, m_T)
        p.emit(V, lambda e: e.tensor_copy(out=negGB_t[:, :], in_=ps_a[:, 0:E]))
        p.emit(V, lambda e: e.tensor_copy(out=lbB_t[:, :], in_=ps_a[:, E:2 * E]), inc=m_V)

        def ln_stats(xt):
            vf(lambda e: e.reduce_sum(out=stat_t[:, 0:1], in_=xt[:, :], axis=AXL.X))
            vf(lambda e: e.tensor_scalar(
                out=stat_t[:, 2:3], in0=stat_t[:, 0:1], scalar1=1.0 / D, scalar2=None,
                op0=ALU.mult))
            p.wait(Sc, m_V)
            p.emit(Sc, lambda e: e.activation(out=sqt_t[:, :], in_=xt[:, :], func=ACT.Square), inc=m_S)
            p.wait(V, m_S)
            vf(lambda e: e.reduce_sum(out=stat_t[:, 1:2], in_=sqt_t[:, :], axis=AXL.X))
            vf(lambda e: e.tensor_scalar(
                out=stat_t[:, 3:4], in0=stat_t[:, 1:2], scalar1=1.0 / D, scalar2=None, op0=ALU.mult))
            vf(lambda e: e.tensor_tensor(
                out=stat_t[:, 4:5], in0=stat_t[:, 2:3], in1=stat_t[:, 2:3], op=ALU.mult))
            vf(lambda e: e.tensor_tensor(
                out=stat_t[:, 5:6], in0=stat_t[:, 3:4], in1=stat_t[:, 4:5], op=ALU.subtract))
            vf(lambda e: e.tensor_scalar(
                out=stat_t[:, 5:6], in0=stat_t[:, 5:6], scalar1=LN_EPS, scalar2=None,
                op0=ALU.add))
            p.wait(Sc, m_V)
            p.emit(Sc, lambda e: e.activation(out=stat_t[:, 6:7], in_=stat_t[:, 5:6], func=ACT.Sqrt), inc=m_S)
            p.wait(V, m_S)
            vf(lambda e: e.reciprocal(out=stat_t[:, 7:8], in_=stat_t[:, 6:7]))

        def transpose_tile_to(xt, evac):
            p.wait(T, m_V)
            for half in range(2):
                psl = ps_a if half == 0 else ps_b
                for q in range(4):
                    dc = half * 4 + q
                    p.emit(T, lambda e, dc=dc, q=q, psl=psl, xt=xt: e.transpose(
                        psl[0:128, q * 128:(q + 1) * 128], xt[:, dc * 128:(dc + 1) * 128],
                        ident_t[:, :]), inc=(m_T if q == 3 else None))
                p.wait(V, m_T)
                evac(half, psl)

        def gate_tile(xt, sel_store=None):
            def evac(half, psl):
                p.emit(V, lambda e, half=half, psl=psl: e.tensor_copy(
                    out=xTg_t[:, half * 512:(half + 1) * 512], in_=psl[:, 0:512]), inc=m_V)
            transpose_tile_to(xt, evac)
            p.wait(T, m_V)
            for dc in range(8):
                p.emit(T, lambda e, dc=dc: e.matmul(
                    ps_a[0:E, 0:128], gwT_t[:, dc * E:(dc + 1) * E],
                    xTg_t[:, dc * 128:(dc + 1) * 128],
                    start=(dc == 0), stop=(dc == 7)), inc=(m_T if dc == 7 else None))
            p.wait(V, m_T)
            p.emit(V, lambda e: e.tensor_copy(out=gs_t[0:E, 8:8 + 128], in_=ps_a[0:E, 0:128]), inc=m_V)
            p.wait(T, m_V)
            p.emit(T, lambda e: e.transpose(
                ps_b[0:128, 0:E], gs_t[0:E, 8:8 + 128], ident_t[0:E, 0:E]), inc=m_T)
            p.wait(V, m_T)
            g0 = 16 * E
            rtok = gs_t[:, g0:g0 + E]
            logit = gs_t[:, g0 + E:g0 + 2 * E]
            scores = gs_t[:, g0 + 2 * E:g0 + 3 * E]
            iseq = gs_t[:, g0 + 3 * E:g0 + 4 * E]
            masked = gs_t[:, g0 + 4 * E:g0 + 5 * E]
            sel = gs_t[:, g0 + 5 * E:g0 + 6 * E]
            wvals = gs_t[:, g0 + 6 * E:g0 + 7 * E]
            vf(lambda e: e.tensor_copy(out=rtok, in_=ps_b[:, 0:E]))
            vf(lambda e: e.scalar_tensor_tensor(
                out=logit, in0=negGB_t[:, :], scalar=stat_t[:, 2:3], in1=rtok,
                op0=ALU.mult, op1=ALU.add))
            vf(lambda e: e.tensor_scalar(
                out=logit, in0=logit, scalar1=stat_t[:, 7:8], scalar2=None, op0=ALU.mult))
            vf(lambda e: e.tensor_tensor(
                out=logit, in0=logit, in1=lbB_t[:, :], op=ALU.add))
            p.wait(Sc, m_V)
            p.emit(Sc, lambda e: e.activation(out=scores, in_=logit, func=ACT.Sigmoid), inc=m_S)
            p.wait(V, m_S)
            vf(lambda e: e.reduce_max(out=stat_t[:, 0:1], in_=scores, axis=AXL.X))
            vf(lambda e: e.tensor_scalar(
                out=iseq, in0=scores, scalar1=stat_t[:, 0:1], scalar2=None, op0=ALU.is_ge))
            vf(lambda e: e.scalar_tensor_tensor(
                out=masked, in0=iseq, scalar=-1e30, in1=scores, op0=ALU.mult, op1=ALU.add))
            vf(lambda e: e.reduce_max(out=stat_t[:, 1:2], in_=masked, axis=AXL.X))
            vf(lambda e: e.tensor_scalar(
                out=sel, in0=scores, scalar1=stat_t[:, 1:2], scalar2=None, op0=ALU.is_ge))
            vf(lambda e: e.tensor_tensor(
                out=stat_t[:, 2:3], in0=stat_t[:, 0:1], in1=stat_t[:, 1:2], op=ALU.add))
            vf(lambda e: e.tensor_scalar(
                out=stat_t[:, 2:3], in0=stat_t[:, 2:3], scalar1=GATE_EPS, scalar2=None, op0=ALU.add))
            vf(lambda e: e.reciprocal(out=stat_t[:, 3:4], in_=stat_t[:, 2:3]))
            vf(lambda e: e.scalar_tensor_tensor(
                out=wvals, in0=scores, scalar=stat_t[:, 3:4], in1=sel,
                op0=ALU.mult, op1=ALU.mult))
            vf(lambda e: e.tensor_tensor(
                out=wacc_t[:, 0:E], in0=wacc_t[:, 0:E], in1=wvals, op=ALU.add))
            vf(lambda e: e.tensor_tensor(
                out=wacc_t[:, E:2 * E], in0=wacc_t[:, E:2 * E], in1=sel, op=ALU.add))
            if sel_store is not None:
                vf(lambda e, sel_store=sel_store: e.tensor_copy(out=sel_store, in_=sel))

        for tg in range(N_ALL // 128):
            xt = xt0_t if tg % 2 == 0 else xt1_t
            p.wait(Y, m_V)
            p.emit(Y, lambda e, tg=tg, xt=xt: e.dma_start(
                out=xt[:, :], in_=xf[tg * 128:(tg + 1) * 128, :]), inc=m_dma, inc_n=16)
            p.wait(V, m_dma)
            p.wait(Sc, m_dma)
            p.wait(T, m_dma)
            ln_stats(xt)
            gate_tile(xt)

        p.wait(T, m_V)
        p.emit(T, lambda e: e.matmul(
            ps_a[0:2 * E, 0:1], wacc_t[:, 0:2 * E], onescol_t[:, :],
            start=True, stop=True), inc=m_T)
        p.wait(V, m_T)
        p.emit(V, lambda e: e.tensor_copy(out=small_t[0:2 * E, 2:3], in_=ps_a[0:2 * E, 0:1]), inc=m_V)
        p.wait(T, m_V)
        p.emit(T, lambda e: e.transpose(
            ps_b[0:1, 0:2 * E], small_t[0:2 * E, 2:3], ident_t[0:2 * E, 0:2 * E]), inc=m_T)
        p.wait(V, m_T)
        vf(lambda e: e.tensor_copy(out=small_t[0:1, 8:8 + 2 * E], in_=ps_b[0:1, 0:2 * E]))
        vf(lambda e: e.tensor_scalar(
            out=small_t[0:1, 24:24 + E], in0=small_t[0:1, 8 + E:8 + 2 * E],
            scalar1=1.0, scalar2=None, op0=ALU.max))
        vf(lambda e: e.reciprocal(out=small_t[0:1, 32:32 + E], in_=small_t[0:1, 24:24 + E]))
        vf(lambda e: e.tensor_tensor(
            out=small_t[0:1, 40:40 + E], in0=small_t[0:1, 8:8 + E],
            in1=small_t[0:1, 32:32 + E], op=ALU.mult))
        p.wait(T, m_V)
        p.emit(T, lambda e: e.matmul(
            ps_a[0:128, 0:E], onesrow_t[0:1, :], small_t[0:1, 40:40 + E],
            start=True, stop=True), inc=m_T)
        p.wait(V, m_T)
        p.emit(V, lambda e: e.tensor_copy(out=wmeanB_t[:, :], in_=ps_a[:, 0:E]), inc=m_V)

        for tb in range(16):
            xt = xt0_t if tb % 2 == 0 else xt1_t
            p.wait(Y, m_V)
            p.emit(Y, lambda e, tb=tb, xt=xt: e.dma_start(
                out=xt[:, :], in_=xo[tb * 128:(tb + 1) * 128, :]), inc=m_dma, inc_n=16)
            p.wait(V, m_dma)
            p.wait(Sc, m_dma)
            p.wait(T, m_dma)
            ln_stats(xt)
            gate_tile(xt, sel_store=selown_t[:, tb * E:(tb + 1) * E])
            ln_stats(xt)
            vf(lambda e, xt=xt: e.tensor_scalar(
                out=xnt_t[:, :], in0=xt[:, :], scalar1=stat_t[:, 2:3], scalar2=stat_t[:, 7:8],
                op0=ALU.subtract, op1=ALU.mult))

            def evac_R(half, psl, tb=tb):
                p.emit(V, lambda e, half=half, psl=psl, tb=tb: e.tensor_copy(
                    out=bass.AP(R_t, (half * 4) * S + tb * 128, [[8 * S, 128], [S, 4], [1, 128]]),
                    in_=bass.AP(ps_sig, psl.col0, [[4096, 128], [128, 4], [1, 128]])), inc=m_V)
            transpose_tile_to(xt, evac_R)

            def evac_stage(half, psl):
                p.emit(V, lambda e, half=half, psl=psl: e.tensor_copy(
                    out=stage_t[:, half * 512:(half + 1) * 512], in_=psl[:, 0:512]), inc=m_V)
            transpose_tile_to(xnt_t, evac_stage)
            p.wait(G, m_V)
            for dc in range(8):
                p.emit(G, lambda e, tb=tb, dc=dc: e.dma_start(
                    out=bass.AP(xnT_hbm.tensor, dc * 128 * S + tb * 128, [[S, 128], [1, 128]]),
                    in_=bass.AP(stage_t, dc * 128, [[1024, 128], [1, 128]])), inc=m_gdma, inc_n=16)
            p.wait(V, m_gdma)

        p.wait(V, m_V)
        for tb in range(16):
            p.emit(V, lambda e, tb=tb: e.tensor_tensor(
                out=selown_t[:, tb * E:(tb + 1) * E], in0=selown_t[:, tb * E:(tb + 1) * E],
                in1=wmeanB_t[:, :], op=ALU.mult), inc=(m_V if tb == 15 else None))
        p.wait(T, m_V)
        for grp in range(4):
            for q in range(4):
                tb = grp * 4 + q
                p.emit(T, lambda e, tb=tb, q=q: e.transpose(
                    ps_a[0:E, q * 128:(q + 1) * 128],
                    selown_t[:, tb * E:(tb + 1) * E], ident_t[:, :]),
                    inc=(m_T if q == 3 else None))
            p.wait(V, m_T)
            p.wait(V, m_gdma)
            p.emit(V, lambda e: e.tensor_copy(
                out=stage_t[0:E, 0:512], in_=ps_a[0:E, 0:512]), inc=m_V)
            p.wait(G, m_V)
            for q in range(4):
                tb = grp * 4 + q
                p.emit(G, lambda e, tb=tb, q=q: e.dma_start(
                    out=selwT_t[E * tb:E * tb + E, 0:128],
                    in_=stage_t[0:E, q * 128:(q + 1) * 128]), inc=m_gdma, inc_n=16)
            p.wait(T, m_V)

        p.wait(V, m_V)
        for cb in range(8):
            p.emit(V, lambda e, cb=cb: e.tensor_scalar(
                out=R_t[:, cb * S:(cb + 1) * S], in0=R_t[:, cb * S:(cb + 1) * S],
                scalar1=projb_t[:, cb:cb + 1], scalar2=None, op0=ALU.add),
                inc=(m_V if cb == 7 else None))
        p.wait(V, m_V)

        if "d_wmean" in dbg:
            p.wait(G, m_V)
            p.emit(G, lambda e: e.dma_start(out=dbg["d_wmean"][:, :], in_=wmeanB_t[0:1, :]), inc=m_gdma, inc_n=16)
        if "d_xnT" in dbg:
            p.wait(G, m_gdma)
            p.emit(G, lambda e: e.dma_start(out=dbg["d_xnT"][:, :], in_=xnT_hbm[:, :]), inc=m_gdma, inc_n=16)


        def barrier():
            for eng in (V, Sc, T, G, Y):
                for s in (m_V, m_S, m_T, m_TA, m_TC, m_SE, m_G, m_dma, m_gdma, m_PS, m_PV):
                    p.wait(eng, s)

        barrier()

        NSLOT = 7
        kern_t = xt0_t
        yT_t = xnt_t
        recB_t = xt1_t
        unit = [0]
        nS_units = [0]
        nV_units = [0]

        def produce_first(base_u, icp, jgs):
            q_ = base_u // 4
            s_list, v_list = [], []
            for r, jg in enumerate(jgs):
                u = base_u + r
                (s_list if (u % 16) < 9 else v_list).append((u, jg))
            qsl = nqT2_t[:, icp * 1024:(icp + 1) * 1024]

            def mm_floor(lst):
                mu = max(u for u, _ in lst)
                return 0 if mu < NSLOT else 8 * ((mu - NSLOT) // 4 + 1)

            if s_list:
                p.wait(Sc, m_TC, mm_floor(s_list))
                for u, jg in s_list:
                    dsl = dbuf_t[:, (u % NSLOT) * 1024:(u % NSLOT) * 1024 + 1024]
                    p.emit(Sc, lambda e, dsl=dsl, qsl=qsl, jg=jg: e.activation(
                        out=dsl, in_=qsl, func=ACT.Abs,
                        bias=kT2_t[:, jg:jg + 1], scale=1.0), inc=m_PS)
                    nS_units[0] += 1
            if v_list:
                p.wait(V, m_TC, mm_floor(v_list))
                for i, (u, jg) in enumerate(v_list):
                    dsl = dbuf_t[:, (u % NSLOT) * 1024:(u % NSLOT) * 1024 + 1024]
                    p.emit(V, lambda e, dsl=dsl, qsl=qsl, jg=jg: e.tensor_scalar(
                        out=dsl, in0=qsl, scalar1=kT2_t[:, jg:jg + 1], scalar2=None,
                        op0=ALU.add), inc=(m_PV if i == len(v_list) - 1 else None))
            return v_list, p.val(m_PV), p.val(m_PS)

        def produce_ands(v_list, sub_count):
            if v_list:
                p.wait(V, m_PV, sub_count)
                for u, jg in v_list:
                    dsl = dbuf_t[:, (u % NSLOT) * 1024:(u % NSLOT) * 1024 + 1024]
                    p.emit(V, lambda e, dsl=dsl: e.tensor_scalar(
                        out=dsl.bitcast(I32), in0=dsl.bitcast(I32), scalar1=ABSMASK,
                        scalar2=None, op0=ALU.bitwise_and), inc=m_PV)
                    nV_units[0] += 1
            return p.val(m_PV)

        for hp in range(HP):
            for mat in range(3):
                p.wait(Y, m_V)
                p.emit(Y, lambda e, mat=mat, hp=hp: e.dma_start(
                    out=sqt_t[:, :],
                    in_=qkv_w[mat * D + hp * 128:(mat * D + hp * 128) + 128, :]),
                    inc=m_dma, inc_n=16)
                p.emit(Y, lambda e, mat=mat, hp=hp: e.dma_start(
                    out=small_t[:, 48 + mat:49 + mat],
                    in_=qkv_b[mat * D + hp * 128:(mat * D + hp * 128) + 128, 0:1]),
                    inc=m_dma, inc_n=16)
                p.wait(T, m_dma)
                p.wait(T, m_V)
                for half in range(2):
                    psl = ps_a if half == 0 else ps_b
                    for q in range(4):
                        dc = half * 4 + q
                        p.emit(T, lambda e, dc=dc, q=q, psl=psl: e.transpose(
                            psl[0:128, q * 128:(q + 1) * 128],
                            sqt_t[:, dc * 128:(dc + 1) * 128], ident_t[:, :]),
                            inc=(m_T if q == 3 else None))
                    p.wait(V, m_T)
                    p.emit(V, lambda e, mat=mat, half=half, psl=psl: e.tensor_copy(
                        out=wt3_t[:, (mat * 8 + half * 4) * 128:(mat * 8 + half * 4 + 4) * 128],
                        in_=psl[:, 0:512]), inc=m_V)
            p.emit(Y, lambda e, hp=hp: e.dma_start(
                out=small_t[:, 52:55], in_=dwc_w[hp * 128:(hp + 1) * 128, :]), inc=m_dma, inc_n=16)
            p.emit(Y, lambda e, hp=hp: e.dma_start(
                out=small_t[:, 55:56], in_=dwc_b[hp * 128:(hp + 1) * 128, 0:1]), inc=m_dma, inc_n=16)

            for tc in range(4):
                p.wait(Y, m_T)
                p.wait(Y, m_gdma)
                for dc in range(8):
                    p.emit(Y, lambda e, tc=tc, dc=dc: e.dma_start(
                        out=bass.AP(xnt2_t, dc * 512, [[4096, 128], [1, 512]]),
                        in_=bass.AP(xnT_hbm.tensor, dc * 128 * S + tc * 512, [[S, 128], [1, 512]])),
                        inc=m_dma, inc_n=16)
                p.wait(T, m_dma)
                for mat in range(3):
                    psl = ps_a if mat % 2 == 0 else ps_b
                    p.wait(T, m_V)
                    for dc in range(8):
                        p.emit(T, lambda e, mat=mat, dc=dc, psl=psl: e.matmul(
                            psl[:, :], wt3_t[:, (mat * 8 + dc) * 128:(mat * 8 + dc + 1) * 128],
                            xnt2_t[:, dc * 512:(dc + 1) * 512],
                            start=(dc == 0), stop=(dc == 7)), inc=(m_T if dc == 7 else None))
                    dstT = (nqT2_t, kT2_t, vT2_t)[mat]
                    p.wait(V, m_T)
                    if mat == 0:
                        p.emit(V, lambda e, dstT=dstT, tc=tc, mat=mat, psl=psl: e.tensor_scalar(
                            out=dstT[:, tc * 512:(tc + 1) * 512], in0=psl[:, 0:512],
                            scalar1=small_t[:, 48 + mat:49 + mat], scalar2=-1.0,
                            op0=ALU.add, op1=ALU.mult), inc=m_V)
                    else:
                        p.emit(V, lambda e, dstT=dstT, tc=tc, mat=mat, psl=psl: e.tensor_scalar(
                            out=dstT[:, tc * 512:(tc + 1) * 512], in0=psl[:, 0:512],
                            scalar1=small_t[:, 48 + mat:49 + mat], scalar2=None, op0=ALU.add), inc=m_V)

            p.wait(V, m_gdma)
            p.wait(T, m_V)
            for grp in range(4):
                psl = ps_a if grp % 2 == 0 else ps_b
                p.wait(T, m_V)
                for q in range(4):
                    tb4 = grp * 4 + q
                    p.emit(T, lambda e, tb4=tb4, q=q, psl=psl: e.transpose(
                        psl[0:128, q * 128:(q + 1) * 128],
                        vT2_t[:, tb4 * 128:(tb4 + 1) * 128], ident_t[:, :]),
                        inc=(m_T if q == 3 else None))
                p.wait(V, m_T)
                p.emit(V, lambda e, grp=grp, psl=psl: e.tensor_copy(
                    out=vtok_t[:, grp * 512:(grp + 1) * 512], in_=psl[:, 0:512]), inc=m_V)
            if hp == 0:
                p.wait(V, m_V)
                p.emit(V, lambda e: e.memset(vaug0_t[:], 0.0))
                p.emit(V, lambda e: e.memset(vaug1_t[:], 0.0), inc=m_V)
                p.wait(V, m_V)
                p.emit(V, lambda e: e.memset(stage_t[0:16, 0:32], 1.0), inc=m_V)
                p.wait(G, m_V)
                for cblk in range(4):
                    p.emit(G, lambda e, cblk=cblk: e.dma_start(
                        out=vaug0_t[32 * cblk:32 * cblk + 16, 64:2080:65],
                        in_=stage_t[0:16, 0:32],
                        allow_slow_non_contiguous=True), inc=m_gdma, inc_n=16)
                    p.emit(G, lambda e, cblk=cblk: e.dma_start(
                        out=vaug1_t[32 * cblk + 16:32 * cblk + 32, 64:2080:65],
                        in_=stage_t[0:16, 0:32],
                        allow_slow_non_contiguous=True), inc=m_gdma, inc_n=16)
            p.wait(G, m_TA)
            p.wait(G, m_V)
            for g in range(32):
                p0 = (g % 2) * 64
                cc = (g // 2) * 128
                for cblk in range(4):
                    p.emit(G, lambda e, g=g, p0=p0, cc=cc, cblk=cblk: e.dma_start(
                        out=vaug0_t[32 * cblk:32 * cblk + 16, g * 65:g * 65 + 64],
                        in_=vtok_t[p0 + 16 * cblk:p0 + 16 * cblk + 16, cc:cc + 64]),
                        inc=m_gdma, inc_n=16)
                    p.emit(G, lambda e, g=g, p0=p0, cc=cc, cblk=cblk: e.dma_start(
                        out=vaug1_t[32 * cblk + 16:32 * cblk + 32, g * 65:g * 65 + 64],
                        in_=vtok_t[p0 + 16 * cblk:p0 + 16 * cblk + 16, cc + 64:cc + 128]),
                        inc=m_gdma, inc_n=16)

            p.wait(V, m_dma)
            vf(lambda e: e.tensor_scalar(
                out=localT_t[:, :], in0=vT2_t[:, :], scalar1=small_t[:, 53:54],
                scalar2=None, op0=ALU.mult))
            vf(lambda e: e.scalar_tensor_tensor(
                out=localT_t[:, 1:S], in0=vT2_t[:, 0:S - 1], scalar=small_t[:, 52:53],
                in1=localT_t[:, 1:S], op0=ALU.mult, op1=ALU.add))
            vf(lambda e: e.scalar_tensor_tensor(
                out=localT_t[:, 0:S - 1], in0=vT2_t[:, 1:S], scalar=small_t[:, 54:55],
                in1=localT_t[:, 0:S - 1], op0=ALU.mult, op1=ALU.add))
            vf(lambda e: e.tensor_scalar(
                out=localT_t[:, :], in0=localT_t[:, :], scalar1=small_t[:, 55:56],
                scalar2=None, op0=ALU.add))

            p.wait(Y, m_V)
            for ob in range(8):
                p.emit(Y, lambda e, hp=hp, ob=ob: e.dma_start(
                    out=bass.AP(xTg_t, ob * 128, [[1024, 128], [1, 128]]),
                    in_=bass.AP(proj_w.tensor, ob * 128 * D + hp * 128, [[D, 128], [1, 128]])),
                    inc=m_dma, inc_n=16)
            p.wait(T, m_dma)
            p.wait(T, m_V)
            for grp in range(2):
                psl = ps_a if grp == 0 else ps_b
                for q in range(4):
                    ob = grp * 4 + q
                    p.emit(T, lambda e, ob=ob, q=q, psl=psl: e.transpose(
                        psl[0:128, q * 128:(q + 1) * 128],
                        xTg_t[:, ob * 128:(ob + 1) * 128], ident_t[:, :]),
                        inc=(m_T if q == 3 else None))
                p.wait(V, m_T)
                p.emit(V, lambda e, grp=grp, psl=psl: e.tensor_copy(
                    out=wt3_t[:, (grp * 4) * 128:(grp * 4 + 4) * 128], in_=psl[:, 0:512]), inc=m_V)

            p.wait(Sc, m_V)
            p.wait(Sc, m_dma)
            p.wait(V, m_T)
            kern_mTA = [0]
            bankV = {b: 0 for b in range(4)}

            def ga_sl(h, m, rows=65):
                t_ = sqt_t if h == 0 else xTg_t
                return t_[0:rows, m * 512:(m + 1) * 512]

            for icp in range(2):
                sig_wait = {0: {k: p.val(m_SE) for k in range(8)}}
                pending = None

                def consume(pend, and_cnt, icp=icp):
                    base_u, g, q4, v_list, sub_cnt, ps_cnt = pend
                    p.wait(T, m_PS, ps_cnt)
                    p.wait(T, m_PV, and_cnt)
                    tc_vals = {}
                    for m in range(2):
                        for r in range(4):
                            t = q4 * 4 + r
                            u = base_u + r
                            slot = u % NSLOT
                            cg = t % 4
                            s_ = t // 4
                            bank = 4 * m + cg
                            if q4 == 0:
                                p.wait(T, m_SE, sig_wait[g][bank])
                                if m == 0:
                                    p.wait(T, m_V, bankV[cg])
                            p.emit(T, lambda e, slot=slot, cg=cg, s_=s_, q4=q4, m=m, bank=bank: e.matmul(
                                ps_sig[32 * cg:32 * cg + 32, 512 * bank:512 * bank + 512],
                                masks_t[:, 32 * s_:32 * s_ + 32],
                                dbuf_t[:, slot * 1024 + m * 512:slot * 1024 + m * 512 + 512],
                                start=(q4 == 0), stop=(q4 == 15),
                                tile_position=(0, 32 * cg)), inc=m_TC)
                            tc_vals[(m, cg)] = p.val(m_TC)
                    if q4 == 15:
                        g_ = g
                        p.wait(Sc, m_TA, kern_mTA[0])
                        nxt = {}
                        for m in range(2):
                            for cg in range(4):
                                bank = 4 * m + cg
                                p.wait(Sc, m_TC, tc_vals[(m, cg)])
                                p.emit(Sc, lambda e, cg=cg, m=m, bank=bank: e.activation(
                                    out=kern_t[32 * cg:32 * cg + 32, m * 512:m * 512 + 512],
                                    in_=ps_sig[32 * cg:32 * cg + 32, 512 * bank:512 * bank + 512],
                                    func=ACT.Exp, bias=0.0, scale=LAMBDA_SCALE), inc=m_SE)
                                nxt[bank] = p.val(m_SE)
                        sig_wait[g_ + 1] = nxt
                        p.wait(T, m_SE)
                        if g_ == 0:
                            p.wait(T, m_gdma)
                        ta_vals = {}
                        for h in range(2):
                            vaug = vaug0_t if h == 0 else vaug1_t
                            for m in range(2):
                                b = 2 * h + m
                                p.wait(T, m_V, bankV[b])
                                p.emit(T, lambda e, h=h, m=m, g_=g_, b=b, vaug=vaug: e.matmul(
                                    ps_sig[0:65, 512 * b:512 * b + 512],
                                    vaug[:, g_ * 65:g_ * 65 + 65],
                                    kern_t[:, m * 512:(m + 1) * 512],
                                    start=True, stop=True), inc=m_TA)
                                ta_vals[b] = p.val(m_TA)
                        kern_mTA[0] = p.val(m_TA)
                        for h in range(2):
                            for m in range(2):
                                b = 2 * h + m
                                p.wait(V, m_TA, ta_vals[b])
                                if g_ == 0:
                                    p.emit(V, lambda e, h=h, m=m, b=b: e.tensor_copy(
                                        out=ga_sl(h, m), in_=ps_sig[0:65, 512 * b:512 * b + 512]),
                                        inc=m_V)
                                else:
                                    p.emit(V, lambda e, h=h, m=m, b=b: e.tensor_tensor(
                                        out=ga_sl(h, m), in0=ga_sl(h, m),
                                        in1=ps_sig[0:65, 512 * b:512 * b + 512], op=ALU.add),
                                        inc=m_V)
                                bankV[b] = p.val(m_V)

                for g in range(32):
                    for q4 in range(16):
                        jgs = []
                        for r in range(4):
                            t = q4 * 4 + r
                            jprime = 16 * (t % 4) + (t // 4)
                            jgs.append(g * 64 + jprime)
                        pa = None
                        if pending is not None:
                            pa = produce_ands(pending[3], pending[4])
                        v_list, sub_cnt, ps_cnt = produce_first(unit[0], icp, jgs)
                        cur = (unit[0], g, q4, v_list, sub_cnt, ps_cnt)
                        unit[0] += 4
                        if pending is not None:
                            consume(pending, pa)
                        pending = cur
                consume(pending, produce_ands(pending[3], pending[4]))
                pending = None

                p.wait(V, m_V)
                for h in range(2):
                    for m in range(2):
                        vf(lambda e, h=h, m=m: e.tensor_scalar(
                            out=stage_t[0:1, 0:512], in0=ga_sl(h, m)[64:65, 0:512],
                            scalar1=ATT_EPS, scalar2=None, op0=ALU.add))
                        vf(lambda e: e.reciprocal(
                            out=stage_t[0:1, 512:1024], in_=stage_t[0:1, 0:512]))
                        p.wait(T, m_V)
                        p.emit(T, lambda e: e.matmul(
                            ps_sig[0:64, 0:512],
                            onesrow_t[0:1, 0:64], stage_t[0:1, 512:1024],
                            start=True, stop=True), inc=m_T)
                        p.wait(V, m_T)
                        vf(lambda e: e.tensor_copy(
                            out=recB_t[0:64, 0:512], in_=ps_sig[0:64, 0:512]))
                        lsl = icp * 1024 + m * 512
                        vf(lambda e, h=h, m=m: e.tensor_tensor(
                            out=yT_t[h * 64:h * 64 + 64, m * 512:(m + 1) * 512],
                            in0=ga_sl(h, m, rows=64), in1=recB_t[0:64, 0:512], op=ALU.mult))
                        vf(lambda e, h=h, m=m, lsl=lsl: e.tensor_tensor(
                            out=yT_t[h * 64:h * 64 + 64, m * 512:(m + 1) * 512],
                            in0=yT_t[h * 64:h * 64 + 64, m * 512:(m + 1) * 512],
                            in1=localT_t[h * 64:h * 64 + 64, lsl:lsl + 512], op=ALU.add))
                p.wait(T, m_V)
                for ob in range(8):
                    for m in range(2):
                        psl = ps_sig[:, 512:1024] if (ob * 2 + m) % 2 == 0 else ps_sig[:, 1024:1536]
                        p.wait(T, m_V)
                        p.emit(T, lambda e, ob=ob, m=m, psl=psl: e.matmul(
                            psl, wt3_t[:, ob * 128:(ob + 1) * 128],
                            yT_t[:, m * 512:(m + 1) * 512], start=True, stop=True), inc=m_T)
                        p.wait(V, m_T)
                        tc = icp * 2 + m
                        p.emit(V, lambda e, ob=ob, tc=tc, psl=psl: e.tensor_tensor(
                            out=R_t[:, ob * S + tc * 512: ob * S + (tc + 1) * 512],
                            in0=R_t[:, ob * S + tc * 512: ob * S + (tc + 1) * 512],
                            in1=psl[:, 0:512], op=ALU.add), inc=m_V)
                for b in range(4):
                    bankV[b] = p.val(m_V)

        barrier()
        prod_t = dbuf_t
        gu_t = kern_t
        selwB_t = recB_t
        ps_g = lambda: ps_sig[:, 2048:2560]
        ps_u = lambda: ps_sig[:, 2560:3072]
        ps_d = lambda: ps_sig[:, 3072:3584]

        def load_transpose_slab(src_dram, row0, ncols, dst_off, nblk):
            p.wait(Y, m_V)
            p.emit(Y, lambda e: e.dma_start(
                out=sqt_t[:, 0:ncols], in_=src_dram[row0:row0 + 128, 0:ncols]),
                inc=m_dma, inc_n=16)
            p.wait(T, m_dma)
            p.wait(T, m_V)
            for grp in range((nblk + 3) // 4):
                psl = ps_a if grp % 2 == 0 else ps_b
                n_in_grp = min(4, nblk - grp * 4)
                for q in range(n_in_grp):
                    dc = grp * 4 + q
                    p.emit(T, lambda e, dc=dc, q=q, psl=psl: e.transpose(
                        psl[0:128, q * 128:(q + 1) * 128],
                        sqt_t[:, dc * 128:(dc + 1) * 128], ident_t[:, :]),
                        inc=(m_T if q == n_in_grp - 1 else None))
                p.wait(V, m_T)
                p.emit(V, lambda e, grp=grp, n_in_grp=n_in_grp, psl=psl, dst_off=dst_off: e.tensor_copy(
                    out=wt3_t[:, dst_off + grp * 512:dst_off + grp * 512 + n_in_grp * 128],
                    in_=psl[:, 0:n_in_grp * 128]), inc=m_V)
                p.wait(T, m_V)

        def ffn_block(gw_d, uw_d, dw_d, gw_row0, dw_row0, n_sib, dw_ncols, selw_e):
            for tq in range(4):
                p.wait(Y, m_T)
                for dc in range(8):
                    p.emit(Y, lambda e, tq=tq, dc=dc: e.dma_start(
                        out=bass.AP(xnt2_t, dc * 512, [[4096, 128], [1, 512]]),
                        in_=bass.AP(xnT_hbm.tensor, dc * 128 * S + tq * 512, [[S, 128], [1, 512]])),
                        inc=m_dma, inc_n=16)
                if selw_e is not None:
                    p.wait(T, m_dma)
                    p.wait(T, m_V)
                    p.wait(T, m_gdma)
                    p.emit(T, lambda e, tq=tq: e.matmul(
                        ps_sig[:, 3584:4096], onesrow_t[0:1, :],
                        localT_t[0:1, tq * 512:(tq + 1) * 512],
                        start=True, stop=True), inc=m_T)
                    p.wait(V, m_T)
                    p.emit(V, lambda e: e.tensor_copy(
                        out=selwB_t[:, 512:1024], in_=ps_sig[:, 3584:4096]), inc=m_V)
                for sib in range(n_sib):
                    load_transpose_slab(gw_d, gw_row0 + sib * 128, D, 0, 8)
                    p.wait(T, m_dma)
                    p.wait(T, m_V)
                    for dc in range(8):
                        p.emit(T, lambda e, dc=dc: e.matmul(
                            ps_g(), wt3_t[:, dc * 128:(dc + 1) * 128],
                            xnt2_t[:, dc * 512:(dc + 1) * 512],
                            start=(dc == 0), stop=(dc == 7)), inc=(m_T if dc == 7 else None))
                    p.wait(Sc, m_T)
                    p.wait(Sc, m_V)
                    p.emit(Sc, lambda e: e.activation(
                        out=stage_t[:, 0:512], in_=ps_g(), func=ACT.Sigmoid), inc=m_S)
                    load_transpose_slab(uw_d, gw_row0 + sib * 128, D, 1024, 8)
                    p.wait(T, m_dma)
                    p.wait(T, m_V)
                    for dc in range(8):
                        p.emit(T, lambda e, dc=dc: e.matmul(
                            ps_u(), wt3_t[:, 1024 + dc * 128:1024 + (dc + 1) * 128],
                            xnt2_t[:, dc * 512:(dc + 1) * 512],
                            start=(dc == 0), stop=(dc == 7)), inc=(m_T if dc == 7 else None))
                    p.wait(V, m_S)
                    p.wait(V, m_T)
                    vf(lambda e: e.tensor_tensor(
                        out=gu_t[:, 0:512], in0=ps_g(), in1=stage_t[:, 0:512], op=ALU.mult))
                    vf(lambda e, sib=sib: e.tensor_tensor(
                        out=prod_t[:, sib * 512:(sib + 1) * 512], in0=gu_t[:, 0:512],
                        in1=ps_u(), op=ALU.mult))
                for cb in range(8):
                    load_transpose_slab(dw_d, dw_row0 + cb * 128, dw_ncols, 2048, n_sib)
                    p.wait(T, m_dma)
                    p.wait(T, m_V)
                    for sic in range(n_sib):
                        p.emit(T, lambda e, sic=sic: e.matmul(
                            ps_d(), wt3_t[:, 2048 + sic * 128:2048 + (sic + 1) * 128],
                            prod_t[:, sic * 512:(sic + 1) * 512],
                            start=(sic == 0), stop=(sic == n_sib - 1)),
                            inc=(m_T if sic == n_sib - 1 else None))
                    p.wait(V, m_T)
                    if selw_e is None:
                        p.emit(V, lambda e, cb=cb, tq=tq: e.tensor_tensor(
                            out=R_t[:, cb * S + tq * 512: cb * S + (tq + 1) * 512],
                            in0=R_t[:, cb * S + tq * 512: cb * S + (tq + 1) * 512],
                            in1=ps_d(), op=ALU.add), inc=m_V)
                    else:
                        vf(lambda e: e.tensor_tensor(
                            out=stage_t[:, 512:1024], in0=ps_d(), in1=selwB_t[:, 512:1024],
                            op=ALU.mult))
                        p.emit(V, lambda e, cb=cb, tq=tq: e.tensor_tensor(
                            out=R_t[:, cb * S + tq * 512: cb * S + (tq + 1) * 512],
                            in0=R_t[:, cb * S + tq * 512: cb * S + (tq + 1) * 512],
                            in1=stage_t[:, 512:1024], op=ALU.add), inc=m_V)

        ffn_block(sh_gw, sh_uw, sh_dw, 0, 0, 8, SI, None)
        for ex in range(E):
            p.wait(G, m_T)
            p.wait(G, m_V)
            for tb16 in range(16):
                p.emit(G, lambda e, ex=ex, tb16=tb16: e.dma_start(
                    out=localT_t[0:1, tb16 * 128:(tb16 + 1) * 128],
                    in_=selwT_t[E * tb16 + ex:E * tb16 + ex + 1, 0:128]), inc=m_gdma, inc_n=16)
            ffn_block(exp_gw, exp_uw, exp_dw, ex * I_EXP, ex * D, 4, I_EXP, ex)

        p.wait(T, m_V)
        for tb in range(16):
            p.wait(V, m_dma, p.val(m_dma))
            for half in range(2):
                psl = ps_a if half == 0 else ps_b
                p.wait(T, m_V)
                for q in range(4):
                    cb = half * 4 + q
                    p.emit(T, lambda e, cb=cb, q=q, tb=tb, psl=psl: e.transpose(
                        psl[0:128, q * 128:(q + 1) * 128],
                        R_t[:, cb * S + tb * 128: cb * S + (tb + 1) * 128],
                        ident_t[:, :]), inc=(m_T if q == 3 else None))
                p.wait(V, m_T)
                p.emit(V, lambda e, half=half, psl=psl: e.tensor_copy(
                    out=stage_t[:, half * 512:(half + 1) * 512], in_=psl[:, 0:512]), inc=m_V)
            p.wait(Y, m_V)
            p.emit(Y, lambda e, tb=tb: e.dma_start(
                out=out[tb * 128:(tb + 1) * 128, :], in_=stage_t[:, :]), inc=m_dma, inc_n=16)
        p.wait(G, m_dma)
        p.wait(G, m_gdma)

        p.finalize(block)

    return nc, dbg


def _inputs_common(L0_attn_qkv_w, L0_attn_qkv_b, L0_attn_proj_w, L0_attn_proj_b,
                   L0_attn_dwc_w, L0_attn_dwc_b, L0_ffn_gate_w, L0_ffn_lb_bias,
                   L0_ffn_expert_gate_w, L0_ffn_expert_up_w, L0_ffn_expert_down_w,
                   L0_ffn_shared_gate_w, L0_ffn_shared_up_w, L0_ffn_shared_down_w):
    f = np.float32
    return {
        "qkv_w": np.ascontiguousarray(L0_attn_qkv_w, dtype=f),
        "qkv_b": np.ascontiguousarray(np.asarray(L0_attn_qkv_b, dtype=f).reshape(3 * D, 1)),
        "proj_w": np.ascontiguousarray(L0_attn_proj_w, dtype=f),
        "proj_b": np.ascontiguousarray(np.asarray(L0_attn_proj_b, dtype=f).reshape(D, 1)),
        "dwc_w": np.ascontiguousarray(np.asarray(L0_attn_dwc_w, dtype=f).reshape(D, 3)),
        "dwc_b": np.ascontiguousarray(np.asarray(L0_attn_dwc_b, dtype=f).reshape(D, 1)),
        "gate_w": np.ascontiguousarray(L0_ffn_gate_w, dtype=f),
        "lb_bias": np.ascontiguousarray(np.asarray(L0_ffn_lb_bias, dtype=f).reshape(E, 1)),
        "exp_gw": np.ascontiguousarray(np.asarray(L0_ffn_expert_gate_w, dtype=f).reshape(E * I_EXP, D)),
        "exp_uw": np.ascontiguousarray(np.asarray(L0_ffn_expert_up_w, dtype=f).reshape(E * I_EXP, D)),
        "exp_dw": np.ascontiguousarray(np.asarray(L0_ffn_expert_down_w, dtype=f).reshape(E * D, I_EXP)),
        "sh_gw": np.ascontiguousarray(L0_ffn_shared_gate_w, dtype=f),
        "sh_uw": np.ascontiguousarray(L0_ffn_shared_up_w, dtype=f),
        "sh_dw": np.ascontiguousarray(L0_ffn_shared_down_w, dtype=f),
    }


def kernel(L0_attn_x, **kw):
    x = np.ascontiguousarray(np.asarray(L0_attn_x, dtype=np.float32))
    common = _inputs_common(**kw)
    common["xf"] = np.ascontiguousarray(x.reshape(N_ALL, D))
    nc, _ = build()
    in_maps = [dict(common, xo=np.ascontiguousarray(x[b])) for b in range(B)]
    res = run_bass_kernel_spmd(nc, in_maps, core_ids=list(range(B)))
    return np.stack([res.results[b]["out"] for b in range(B)], axis=0)


# version 09
# speedup vs baseline: 1.6465x; 1.6465x over previous
import contextlib

import numpy as np
import concourse.bass as bass
import concourse.mybir as mybir
from concourse.bass_utils import run_bass_kernel_spmd

F32 = mybir.dt.float32
I32 = mybir.dt.int32
ALU = mybir.AluOpType
ACT = mybir.ActivationFunctionType
AXL = mybir.AxisListType

B, S, D = 8, 2048, 1024
H, HD = 16, 64
HP = 8
E, I_EXP, SI = 8, 512, 1024
N_ALL = B * S
LAMBDA_SCALE = -4.0 / 64.0
LN_EPS = 1e-5
ATT_EPS = 1e-6
GATE_EPS = 1e-8
ABSMASK = 0x7FFFFFFF

ENGINES = ("sync", "vector", "scalar", "tensor", "gpsimd")


class Prog:

    def __init__(self):
        self.steps = {e: [] for e in ENGINES}
        self.ctr = {}

    def val(self, sem):
        return self.ctr.get(id(sem), 0)

    def bump(self, sem, n):
        self.ctr[id(sem)] = self.ctr.get(id(sem), 0) + n
        return self.ctr[id(sem)]

    def set_dma_queues(self, mapping, boundary_sem):
        self._dmaq = {id(s): q for s, q in mapping.items()}
        self._qbound = {id(s): 0 for s in mapping}

    def wait(self, eng, sem, val=None):
        v = self.val(sem) if val is None else val
        if v <= 0:
            return
        dq = getattr(self, "_dmaq", None)
        if dq is not None and id(sem) in dq and eng != dq[id(sem)]:
            if self._qbound[id(sem)] < v:
                q = dq[id(sem)]
                self.steps[q].append(lambda e, sem=sem, v=v: e.wait_ge(sem, v))
                self._qbound[id(sem)] = v
        self.steps[eng].append(lambda e, sem=sem, v=v: e.wait_ge(sem, v))

    def emit(self, eng, fn, inc=None, inc_n=1):
        if inc is not None:
            self.bump(inc, inc_n)
        self.steps[eng].append(
            lambda e, fn=fn, inc=inc, inc_n=inc_n:
            (fn(e).then_inc(inc, inc_n) if inc is not None else fn(e)))

    def finalize(self, block):
        for name in ENGINES:
            steps = self.steps[name]

            def mk(steps):
                def _(e):
                    for st in steps:
                        st(e)
                return _
            getattr(block, name)(mk(steps))


def build(debug=()):
    nc = bass.Bass("TRN2", target_bir_lowering=False)
    dt = F32

    xo = nc.dram_tensor("xo", [S, D], dt, kind="ExternalInput").ap()
    xf = nc.dram_tensor("xf", [N_ALL, D], dt, kind="ExternalInput").ap()
    qkv_w = nc.dram_tensor("qkv_w", [3 * D, D], dt, kind="ExternalInput").ap()
    qkv_b = nc.dram_tensor("qkv_b", [3 * D, 1], dt, kind="ExternalInput").ap()
    proj_w = nc.dram_tensor("proj_w", [D, D], dt, kind="ExternalInput").ap()
    proj_b = nc.dram_tensor("proj_b", [D, 1], dt, kind="ExternalInput").ap()
    dwc_w = nc.dram_tensor("dwc_w", [D, 3], dt, kind="ExternalInput").ap()
    dwc_b = nc.dram_tensor("dwc_b", [D, 1], dt, kind="ExternalInput").ap()
    gate_w = nc.dram_tensor("gate_w", [E, D], dt, kind="ExternalInput").ap()
    lb_bias = nc.dram_tensor("lb_bias", [E, 1], dt, kind="ExternalInput").ap()
    exp_gw = nc.dram_tensor("exp_gw", [E * I_EXP, D], dt, kind="ExternalInput").ap()
    exp_uw = nc.dram_tensor("exp_uw", [E * I_EXP, D], dt, kind="ExternalInput").ap()
    exp_dw = nc.dram_tensor("exp_dw", [E * D, I_EXP], dt, kind="ExternalInput").ap()
    sh_gw = nc.dram_tensor("sh_gw", [SI, D], dt, kind="ExternalInput").ap()
    sh_uw = nc.dram_tensor("sh_uw", [SI, D], dt, kind="ExternalInput").ap()
    sh_dw = nc.dram_tensor("sh_dw", [D, SI], dt, kind="ExternalInput").ap()
    out = nc.dram_tensor("out", [S, D], dt, kind="ExternalOutput").ap()
    xnT_hbm = nc.dram_tensor("xnT_scr", [D, S], dt).ap()

    dbg = {}
    for name, shape in debug:
        dbg[name] = nc.dram_tensor(name, list(shape), dt, kind="ExternalOutput").ap()

    p = Prog()

    with contextlib.ExitStack() as _st:
        def sb(name, shape):
            return _st.enter_context(nc.sbuf_tensor(name, shape, dt))

        def psum(name, shape):
            return _st.enter_context(nc.psum_tensor(name, shape, dt))

        def sem(name):
            return _st.enter_context(nc.semaphore(name))

        R_t = sb("R", [128, 8 * S])
        ident_t = sb("ident", [128, 128])
        onesrow_t = sb("onesrow", [1, 128])
        onescol_t = sb("onescol", [128, 1])
        masks_t = sb("masks", [128, 16 * 32])
        selwT_t = sb("selwT", [128, 128])
        selown_t = sb("selown", [128, 16 * E])
        wmeanB_t = sb("wmeanB", [128, E])
        negGB_t = sb("negGB", [128, E])
        lbB_t = sb("lbB", [128, E])
        gwT_t = sb("gwT", [128, 8 * E])
        wacc_t = sb("wacc", [128, 2 * E])
        small_t = sb("small", [128, 64])
        projb_t = sb("projb", [128, 8])
        xt0_t = sb("xt0", [128, D])
        xt1_t = sb("xt1", [128, D])
        sqt_t = sb("sqt", [128, D])
        xnt_t = sb("xnt", [128, D])
        xTg_t = sb("xTg", [128, 8 * 128])
        gs_t = sb("gs", [128, 24 * E])
        stat_t = sb("stat", [128, 8])
        stage_t = sb("stage", [128, D])
        wt3_t = sb("wt3", [128, 3 * 8 * 128])
        xnt2_t = sb("xnt2", [128, 8 * 512])
        dbuf_t = sb("dbuf", [128, 7 * 1024])
        nqT2_t = sb("nqT2", [128, S])
        kT2_t = sb("kT2", [128, S])
        vT2_t = sb("vT2", [128, S])
        vaug0_t = sb("vaug0", [128, 32 * 65])
        vaug1_t = sb("vaug1", [128, 32 * 65])
        vtok_t = sb("vtok", [128, 16 * 128])
        localT_t = sb("localT", [128, S])
        ps_sig = psum("ps_sig", [128, 4096])

        class _PsView:

            def __init__(self, col0):
                self.col0 = col0

            def __getitem__(self, idx):
                rows, cols = idx
                c0 = self.col0 + (cols.start or 0)
                c1 = self.col0 + (cols.stop if cols.stop is not None else 512)
                return ps_sig[rows, c0:c1]

        ps_a = _PsView(0)
        ps_b = _PsView(512)
        m_dma = sem("m_dma")
        m_gdma = sem("m_gdma")
        m_T = sem("m_T")
        m_TA = sem("m_TA")
        m_TC = sem("m_TC")
        m_V = sem("m_V")
        m_PV = sem("m_PV")
        m_S = sem("m_S")
        m_PS = sem("m_PS")
        m_SE = sem("m_SE")
        m_G = sem("m_G")
        block = _st.enter_context(nc.Block())

        V, Sc, T, G, Y = "vector", "scalar", "tensor", "gpsimd", "sync"
        p.set_dma_queues({m_dma: Y, m_gdma: G}, m_G)

        def vf(fn):
            p.emit(V, fn, inc=m_V)
            p.wait(V, m_V)

        def sf(fn):
            p.emit(Sc, fn, inc=m_S)
            p.wait(Sc, m_S)

        p.emit(V, lambda e: e.memset(ident_t[:], 0.0))
        p.emit(V, lambda e: e.memset(onesrow_t[:], 1.0))
        p.emit(V, lambda e: e.memset(onescol_t[:], 1.0))
        p.emit(V, lambda e: e.memset(masks_t[:], 0.0))
        p.emit(V, lambda e: e.memset(wacc_t[:], 0.0), inc=m_V)
        p.wait(G, m_V)
        p.emit(G, lambda e: e.affine_select(
            out=ident_t[:], in_=ident_t[:], compare_op=ALU.not_equal,
            fill=1.0, base=0, pattern=[[-1, 128]], channel_multiplier=1))
        for s_ in range(16):
            p.emit(G, lambda e, s_=s_: e.memset(
                masks_t[0:64, 32 * s_ + s_: 32 * s_ + s_ + 1], 1.0))
            p.emit(G, lambda e, s_=s_: e.memset(
                masks_t[64:128, 32 * s_ + 16 + s_: 32 * s_ + 16 + s_ + 1], 1.0),
                inc=(m_G if s_ == 15 else None))

        p.emit(Y, lambda e: e.dma_start(out=gs_t[0:E, 0:1], in_=lb_bias[:, :]), inc=m_dma, inc_n=16)
        for cb in range(8):
            p.emit(Y, lambda e, cb=cb: e.dma_start(
                out=projb_t[:, cb:cb + 1],
                in_=proj_b[cb * 128:(cb + 1) * 128, 0:1]), inc=m_dma, inc_n=16)
        p.emit(Y, lambda e: e.dma_start(out=xTg_t[0:E, 0:D], in_=gate_w[:, :]), inc=m_dma, inc_n=16)
        p.wait(T, m_dma)
        p.wait(T, m_G)
        for dc in range(8):
            p.emit(T, lambda e, dc=dc: e.transpose(
                ps_a[0:128, dc * E:(dc + 1) * E], xTg_t[0:E, dc * 128:(dc + 1) * 128],
                ident_t[0:E, 0:E]), inc=(m_T if dc == 7 else None))
        p.wait(V, m_T)
        p.emit(V, lambda e: e.tensor_copy(out=gwT_t[:, :], in_=ps_a[:, 0:8 * E]))
        vf(lambda e: e.reduce_sum(out=small_t[0:E, 0:1], in_=xTg_t[0:E, 0:D], axis=AXL.X))
        vf(lambda e: e.tensor_scalar(
            out=small_t[0:E, 1:2], in0=small_t[0:E, 0:1], scalar1=-1.0, scalar2=None,
            op0=ALU.mult))
        p.wait(T, m_V)
        p.emit(T, lambda e: e.transpose(ps_b[0:1, 0:E], small_t[0:E, 1:2], ident_t[0:E, 0:E]))
        p.emit(T, lambda e: e.transpose(ps_b[0:1, E:2 * E], gs_t[0:E, 0:1],
                                        ident_t[0:E, 0:E]), inc=m_T)
        p.wait(V, m_T)
        p.emit(V, lambda e: e.tensor_copy(out=small_t[0:1, 8:8 + 2 * E], in_=ps_b[0:1, 0:2 * E]), inc=m_V)
        p.wait(T, m_V)
        p.emit(T, lambda e: e.matmul(
            ps_a[0:128, 0:2 * E], onesrow_t[0:1, :], small_t[0:1, 8:8 + 2 * E],
            start=True, stop=True), inc=m_T)
        p.wait(V, m_T)
        p.emit(V, lambda e: e.tensor_copy(out=negGB_t[:, :], in_=ps_a[:, 0:E]))
        p.emit(V, lambda e: e.tensor_copy(out=lbB_t[:, :], in_=ps_a[:, E:2 * E]), inc=m_V)

        def ln_stats(xt):
            vf(lambda e: e.reduce_sum(out=stat_t[:, 0:1], in_=xt[:, :], axis=AXL.X))
            vf(lambda e: e.tensor_scalar(
                out=stat_t[:, 2:3], in0=stat_t[:, 0:1], scalar1=1.0 / D, scalar2=None,
                op0=ALU.mult))
            p.wait(Sc, m_V)
            p.emit(Sc, lambda e: e.activation(out=sqt_t[:, :], in_=xt[:, :], func=ACT.Square), inc=m_S)
            p.wait(V, m_S)
            vf(lambda e: e.reduce_sum(out=stat_t[:, 1:2], in_=sqt_t[:, :], axis=AXL.X))
            vf(lambda e: e.tensor_scalar(
                out=stat_t[:, 3:4], in0=stat_t[:, 1:2], scalar1=1.0 / D, scalar2=None, op0=ALU.mult))
            vf(lambda e: e.tensor_tensor(
                out=stat_t[:, 4:5], in0=stat_t[:, 2:3], in1=stat_t[:, 2:3], op=ALU.mult))
            vf(lambda e: e.tensor_tensor(
                out=stat_t[:, 5:6], in0=stat_t[:, 3:4], in1=stat_t[:, 4:5], op=ALU.subtract))
            vf(lambda e: e.tensor_scalar(
                out=stat_t[:, 5:6], in0=stat_t[:, 5:6], scalar1=LN_EPS, scalar2=None,
                op0=ALU.add))
            p.wait(Sc, m_V)
            p.emit(Sc, lambda e: e.activation(out=stat_t[:, 6:7], in_=stat_t[:, 5:6], func=ACT.Sqrt), inc=m_S)
            p.wait(V, m_S)
            vf(lambda e: e.reciprocal(out=stat_t[:, 7:8], in_=stat_t[:, 6:7]))

        def transpose_tile_to(xt, evac):
            p.wait(T, m_V)
            for half in range(2):
                psl = ps_a if half == 0 else ps_b
                for q in range(4):
                    dc = half * 4 + q
                    p.emit(T, lambda e, dc=dc, q=q, psl=psl, xt=xt: e.transpose(
                        psl[0:128, q * 128:(q + 1) * 128], xt[:, dc * 128:(dc + 1) * 128],
                        ident_t[:, :]), inc=(m_T if q == 3 else None))
                p.wait(V, m_T)
                evac(half, psl)

        def gate_tile(xt, sel_store=None):
            def evac(half, psl):
                p.emit(V, lambda e, half=half, psl=psl: e.tensor_copy(
                    out=xTg_t[:, half * 512:(half + 1) * 512], in_=psl[:, 0:512]), inc=m_V)
            transpose_tile_to(xt, evac)
            p.wait(T, m_V)
            for dc in range(8):
                p.emit(T, lambda e, dc=dc: e.matmul(
                    ps_a[0:E, 0:128], gwT_t[:, dc * E:(dc + 1) * E],
                    xTg_t[:, dc * 128:(dc + 1) * 128],
                    start=(dc == 0), stop=(dc == 7)), inc=(m_T if dc == 7 else None))
            p.wait(V, m_T)
            p.emit(V, lambda e: e.tensor_copy(out=gs_t[0:E, 8:8 + 128], in_=ps_a[0:E, 0:128]), inc=m_V)
            p.wait(T, m_V)
            p.emit(T, lambda e: e.transpose(
                ps_b[0:128, 0:E], gs_t[0:E, 8:8 + 128], ident_t[0:E, 0:E]), inc=m_T)
            p.wait(V, m_T)
            g0 = 16 * E
            rtok = gs_t[:, g0:g0 + E]
            logit = gs_t[:, g0 + E:g0 + 2 * E]
            scores = gs_t[:, g0 + 2 * E:g0 + 3 * E]
            iseq = gs_t[:, g0 + 3 * E:g0 + 4 * E]
            masked = gs_t[:, g0 + 4 * E:g0 + 5 * E]
            sel = gs_t[:, g0 + 5 * E:g0 + 6 * E]
            wvals = gs_t[:, g0 + 6 * E:g0 + 7 * E]
            vf(lambda e: e.tensor_copy(out=rtok, in_=ps_b[:, 0:E]))
            vf(lambda e: e.scalar_tensor_tensor(
                out=logit, in0=negGB_t[:, :], scalar=stat_t[:, 2:3], in1=rtok,
                op0=ALU.mult, op1=ALU.add))
            vf(lambda e: e.tensor_scalar(
                out=logit, in0=logit, scalar1=stat_t[:, 7:8], scalar2=None, op0=ALU.mult))
            vf(lambda e: e.tensor_tensor(
                out=logit, in0=logit, in1=lbB_t[:, :], op=ALU.add))
            p.wait(Sc, m_V)
            p.emit(Sc, lambda e: e.activation(out=scores, in_=logit, func=ACT.Sigmoid), inc=m_S)
            p.wait(V, m_S)
            vf(lambda e: e.reduce_max(out=stat_t[:, 0:1], in_=scores, axis=AXL.X))
            vf(lambda e: e.tensor_scalar(
                out=iseq, in0=scores, scalar1=stat_t[:, 0:1], scalar2=None, op0=ALU.is_ge))
            vf(lambda e: e.scalar_tensor_tensor(
                out=masked, in0=iseq, scalar=-1e30, in1=scores, op0=ALU.mult, op1=ALU.add))
            vf(lambda e: e.reduce_max(out=stat_t[:, 1:2], in_=masked, axis=AXL.X))
            vf(lambda e: e.tensor_scalar(
                out=sel, in0=scores, scalar1=stat_t[:, 1:2], scalar2=None, op0=ALU.is_ge))
            vf(lambda e: e.tensor_tensor(
                out=stat_t[:, 2:3], in0=stat_t[:, 0:1], in1=stat_t[:, 1:2], op=ALU.add))
            vf(lambda e: e.tensor_scalar(
                out=stat_t[:, 2:3], in0=stat_t[:, 2:3], scalar1=GATE_EPS, scalar2=None, op0=ALU.add))
            vf(lambda e: e.reciprocal(out=stat_t[:, 3:4], in_=stat_t[:, 2:3]))
            vf(lambda e: e.scalar_tensor_tensor(
                out=wvals, in0=scores, scalar=stat_t[:, 3:4], in1=sel,
                op0=ALU.mult, op1=ALU.mult))
            vf(lambda e: e.tensor_tensor(
                out=wacc_t[:, 0:E], in0=wacc_t[:, 0:E], in1=wvals, op=ALU.add))
            vf(lambda e: e.tensor_tensor(
                out=wacc_t[:, E:2 * E], in0=wacc_t[:, E:2 * E], in1=sel, op=ALU.add))
            if sel_store is not None:
                vf(lambda e, sel_store=sel_store: e.tensor_copy(out=sel_store, in_=sel))

        for tg in range(N_ALL // 128):
            xt = xt0_t if tg % 2 == 0 else xt1_t
            p.wait(Y, m_V)
            p.emit(Y, lambda e, tg=tg, xt=xt: e.dma_start(
                out=xt[:, :], in_=xf[tg * 128:(tg + 1) * 128, :]), inc=m_dma, inc_n=16)
            p.wait(V, m_dma)
            p.wait(Sc, m_dma)
            p.wait(T, m_dma)
            ln_stats(xt)
            gate_tile(xt)

        p.wait(T, m_V)
        p.emit(T, lambda e: e.matmul(
            ps_a[0:2 * E, 0:1], wacc_t[:, 0:2 * E], onescol_t[:, :],
            start=True, stop=True), inc=m_T)
        p.wait(V, m_T)
        p.emit(V, lambda e: e.tensor_copy(out=small_t[0:2 * E, 2:3], in_=ps_a[0:2 * E, 0:1]), inc=m_V)
        p.wait(T, m_V)
        p.emit(T, lambda e: e.transpose(
            ps_b[0:1, 0:2 * E], small_t[0:2 * E, 2:3], ident_t[0:2 * E, 0:2 * E]), inc=m_T)
        p.wait(V, m_T)
        vf(lambda e: e.tensor_copy(out=small_t[0:1, 8:8 + 2 * E], in_=ps_b[0:1, 0:2 * E]))
        vf(lambda e: e.tensor_scalar(
            out=small_t[0:1, 24:24 + E], in0=small_t[0:1, 8 + E:8 + 2 * E],
            scalar1=1.0, scalar2=None, op0=ALU.max))
        vf(lambda e: e.reciprocal(out=small_t[0:1, 32:32 + E], in_=small_t[0:1, 24:24 + E]))
        vf(lambda e: e.tensor_tensor(
            out=small_t[0:1, 40:40 + E], in0=small_t[0:1, 8:8 + E],
            in1=small_t[0:1, 32:32 + E], op=ALU.mult))
        p.wait(T, m_V)
        p.emit(T, lambda e: e.matmul(
            ps_a[0:128, 0:E], onesrow_t[0:1, :], small_t[0:1, 40:40 + E],
            start=True, stop=True), inc=m_T)
        p.wait(V, m_T)
        p.emit(V, lambda e: e.tensor_copy(out=wmeanB_t[:, :], in_=ps_a[:, 0:E]), inc=m_V)

        for tb in range(16):
            xt = xt0_t if tb % 2 == 0 else xt1_t
            p.wait(Y, m_V)
            p.emit(Y, lambda e, tb=tb, xt=xt: e.dma_start(
                out=xt[:, :], in_=xo[tb * 128:(tb + 1) * 128, :]), inc=m_dma, inc_n=16)
            p.wait(V, m_dma)
            p.wait(Sc, m_dma)
            p.wait(T, m_dma)
            ln_stats(xt)
            gate_tile(xt, sel_store=selown_t[:, tb * E:(tb + 1) * E])
            ln_stats(xt)
            vf(lambda e, xt=xt: e.tensor_scalar(
                out=xnt_t[:, :], in0=xt[:, :], scalar1=stat_t[:, 2:3], scalar2=stat_t[:, 7:8],
                op0=ALU.subtract, op1=ALU.mult))

            def evac_R(half, psl, tb=tb):
                p.emit(V, lambda e, half=half, psl=psl, tb=tb: e.tensor_copy(
                    out=bass.AP(R_t, (half * 4) * S + tb * 128, [[8 * S, 128], [S, 4], [1, 128]]),
                    in_=bass.AP(ps_sig, psl.col0, [[4096, 128], [128, 4], [1, 128]])), inc=m_V)
            transpose_tile_to(xt, evac_R)

            def evac_stage(half, psl):
                p.emit(V, lambda e, half=half, psl=psl: e.tensor_copy(
                    out=stage_t[:, half * 512:(half + 1) * 512], in_=psl[:, 0:512]), inc=m_V)
            transpose_tile_to(xnt_t, evac_stage)
            p.wait(G, m_V)
            for dc in range(8):
                p.emit(G, lambda e, tb=tb, dc=dc: e.dma_start(
                    out=bass.AP(xnT_hbm.tensor, dc * 128 * S + tb * 128, [[S, 128], [1, 128]]),
                    in_=bass.AP(stage_t, dc * 128, [[1024, 128], [1, 128]])), inc=m_gdma, inc_n=16)
            p.wait(V, m_gdma)

        p.wait(V, m_V)
        for tb in range(16):
            p.emit(V, lambda e, tb=tb: e.tensor_tensor(
                out=selown_t[:, tb * E:(tb + 1) * E], in0=selown_t[:, tb * E:(tb + 1) * E],
                in1=wmeanB_t[:, :], op=ALU.mult), inc=(m_V if tb == 15 else None))
        p.wait(T, m_V)
        for grp in range(4):
            for q in range(4):
                tb = grp * 4 + q
                p.emit(T, lambda e, tb=tb, q=q: e.transpose(
                    ps_a[0:E, q * 128:(q + 1) * 128],
                    selown_t[:, tb * E:(tb + 1) * E], ident_t[:, :]),
                    inc=(m_T if q == 3 else None))
            p.wait(V, m_T)
            p.wait(V, m_gdma)
            p.emit(V, lambda e: e.tensor_copy(
                out=stage_t[0:E, 0:512], in_=ps_a[0:E, 0:512]), inc=m_V)
            p.wait(G, m_V)
            for q in range(4):
                tb = grp * 4 + q
                p.emit(G, lambda e, tb=tb, q=q: e.dma_start(
                    out=selwT_t[E * tb:E * tb + E, 0:128],
                    in_=stage_t[0:E, q * 128:(q + 1) * 128]), inc=m_gdma, inc_n=16)
            p.wait(T, m_V)

        p.wait(V, m_V)
        for cb in range(8):
            p.emit(V, lambda e, cb=cb: e.tensor_scalar(
                out=R_t[:, cb * S:(cb + 1) * S], in0=R_t[:, cb * S:(cb + 1) * S],
                scalar1=projb_t[:, cb:cb + 1], scalar2=None, op0=ALU.add),
                inc=(m_V if cb == 7 else None))
        p.wait(V, m_V)

        if "d_wmean" in dbg:
            p.wait(G, m_V)
            p.emit(G, lambda e: e.dma_start(out=dbg["d_wmean"][:, :], in_=wmeanB_t[0:1, :]), inc=m_gdma, inc_n=16)
        if "d_xnT" in dbg:
            p.wait(G, m_gdma)
            p.emit(G, lambda e: e.dma_start(out=dbg["d_xnT"][:, :], in_=xnT_hbm[:, :]), inc=m_gdma, inc_n=16)


        def barrier():
            for eng in (V, Sc, T, G, Y):
                for s in (m_V, m_S, m_T, m_TA, m_TC, m_SE, m_G, m_dma, m_gdma, m_PS, m_PV):
                    p.wait(eng, s)

        barrier()

        NSLOT = 9
        kbuf = (xt0_t, xt1_t)
        yT_t = xnt_t

        def dslot(u):
            s = u % NSLOT
            if s < 7:
                return dbuf_t[:, s * 1024:(s + 1) * 1024]
            return vtok_t[:, (s - 7) * 1024:(s - 6) * 1024]
        unit = [0]
        nS_units = [0]
        nV_units = [0]

        def produce_first(base_u, icp, jgs):
            q_ = base_u // 4
            s_list, v_list = [], []
            for r, jg in enumerate(jgs):
                u = base_u + r
                (s_list if (u % 16) < 9 else v_list).append((u, jg))
            qsl = nqT2_t[:, icp * 1024:(icp + 1) * 1024]

            def mm_floor(lst):
                mu = max(u for u, _ in lst)
                return 0 if mu < NSLOT else 8 * ((mu - NSLOT) // 4 + 1)

            if s_list:
                p.wait(Sc, m_TC, mm_floor(s_list))
                for u, jg in s_list:
                    dsl = dslot(u)
                    p.emit(Sc, lambda e, dsl=dsl, qsl=qsl, jg=jg: e.activation(
                        out=dsl, in_=qsl, func=ACT.Abs,
                        bias=kT2_t[:, jg:jg + 1], scale=1.0), inc=m_PS)
                    nS_units[0] += 1
            if v_list:
                p.wait(V, m_TC, mm_floor(v_list))
                for i, (u, jg) in enumerate(v_list):
                    dsl = dslot(u)
                    p.emit(V, lambda e, dsl=dsl, qsl=qsl, jg=jg: e.tensor_scalar(
                        out=dsl, in0=qsl, scalar1=kT2_t[:, jg:jg + 1], scalar2=None,
                        op0=ALU.add), inc=(m_PV if i == len(v_list) - 1 else None))
            return v_list, p.val(m_PV), p.val(m_PS)

        def produce_ands(v_list, sub_count):
            if v_list:
                p.wait(V, m_PV, sub_count)
                for u, jg in v_list:
                    dsl = dslot(u)
                    p.emit(V, lambda e, dsl=dsl: e.tensor_scalar(
                        out=dsl.bitcast(I32), in0=dsl.bitcast(I32), scalar1=ABSMASK,
                        scalar2=None, op0=ALU.bitwise_and), inc=m_PV)
                    nV_units[0] += 1
            return p.val(m_PV)

        for hp in range(HP):
            for mat in range(3):
                p.wait(Y, m_V)
                p.emit(Y, lambda e, mat=mat, hp=hp: e.dma_start(
                    out=sqt_t[:, :],
                    in_=qkv_w[mat * D + hp * 128:(mat * D + hp * 128) + 128, :]),
                    inc=m_dma, inc_n=16)
                p.emit(Y, lambda e, mat=mat, hp=hp: e.dma_start(
                    out=small_t[:, 48 + mat:49 + mat],
                    in_=qkv_b[mat * D + hp * 128:(mat * D + hp * 128) + 128, 0:1]),
                    inc=m_dma, inc_n=16)
                p.wait(T, m_dma)
                p.wait(T, m_V)
                for half in range(2):
                    psl = ps_a if half == 0 else ps_b
                    for q in range(4):
                        dc = half * 4 + q
                        p.emit(T, lambda e, dc=dc, q=q, psl=psl: e.transpose(
                            psl[0:128, q * 128:(q + 1) * 128],
                            sqt_t[:, dc * 128:(dc + 1) * 128], ident_t[:, :]),
                            inc=(m_T if q == 3 else None))
                    p.wait(V, m_T)
                    p.emit(V, lambda e, mat=mat, half=half, psl=psl: e.tensor_copy(
                        out=wt3_t[:, (mat * 8 + half * 4) * 128:(mat * 8 + half * 4 + 4) * 128],
                        in_=psl[:, 0:512]), inc=m_V)
            p.emit(Y, lambda e, hp=hp: e.dma_start(
                out=small_t[:, 52:55], in_=dwc_w[hp * 128:(hp + 1) * 128, :]), inc=m_dma, inc_n=16)
            p.emit(Y, lambda e, hp=hp: e.dma_start(
                out=small_t[:, 55:56], in_=dwc_b[hp * 128:(hp + 1) * 128, 0:1]), inc=m_dma, inc_n=16)

            for tc in range(4):
                p.wait(Y, m_T)
                p.wait(Y, m_gdma)
                for dc in range(8):
                    p.emit(Y, lambda e, tc=tc, dc=dc: e.dma_start(
                        out=bass.AP(xnt2_t, dc * 512, [[4096, 128], [1, 512]]),
                        in_=bass.AP(xnT_hbm.tensor, dc * 128 * S + tc * 512, [[S, 128], [1, 512]])),
                        inc=m_dma, inc_n=16)
                p.wait(T, m_dma)
                for mat in range(3):
                    psl = ps_a if mat % 2 == 0 else ps_b
                    p.wait(T, m_V)
                    for dc in range(8):
                        p.emit(T, lambda e, mat=mat, dc=dc, psl=psl: e.matmul(
                            psl[:, :], wt3_t[:, (mat * 8 + dc) * 128:(mat * 8 + dc + 1) * 128],
                            xnt2_t[:, dc * 512:(dc + 1) * 512],
                            start=(dc == 0), stop=(dc == 7)), inc=(m_T if dc == 7 else None))
                    dstT = (nqT2_t, kT2_t, vT2_t)[mat]
                    p.wait(V, m_T)
                    if mat == 0:
                        p.emit(V, lambda e, dstT=dstT, tc=tc, mat=mat, psl=psl: e.tensor_scalar(
                            out=dstT[:, tc * 512:(tc + 1) * 512], in0=psl[:, 0:512],
                            scalar1=small_t[:, 48 + mat:49 + mat], scalar2=-1.0,
                            op0=ALU.add, op1=ALU.mult), inc=m_V)
                    else:
                        p.emit(V, lambda e, dstT=dstT, tc=tc, mat=mat, psl=psl: e.tensor_scalar(
                            out=dstT[:, tc * 512:(tc + 1) * 512], in0=psl[:, 0:512],
                            scalar1=small_t[:, 48 + mat:49 + mat], scalar2=None, op0=ALU.add), inc=m_V)

            p.wait(V, m_gdma)
            p.wait(T, m_V)
            for grp in range(4):
                psl = ps_a if grp % 2 == 0 else ps_b
                p.wait(T, m_V)
                for q in range(4):
                    tb4 = grp * 4 + q
                    p.emit(T, lambda e, tb4=tb4, q=q, psl=psl: e.transpose(
                        psl[0:128, q * 128:(q + 1) * 128],
                        vT2_t[:, tb4 * 128:(tb4 + 1) * 128], ident_t[:, :]),
                        inc=(m_T if q == 3 else None))
                p.wait(V, m_T)
                p.emit(V, lambda e, grp=grp, psl=psl: e.tensor_copy(
                    out=vtok_t[:, grp * 512:(grp + 1) * 512], in_=psl[:, 0:512]), inc=m_V)
            if hp == 0:
                p.wait(V, m_V)
                p.emit(V, lambda e: e.memset(vaug0_t[:], 0.0))
                p.emit(V, lambda e: e.memset(vaug1_t[:], 0.0), inc=m_V)
                p.wait(V, m_V)
                p.emit(V, lambda e: e.memset(stage_t[0:16, 0:32], 1.0), inc=m_V)
                p.wait(G, m_V)
                for cblk in range(4):
                    p.emit(G, lambda e, cblk=cblk: e.dma_start(
                        out=vaug0_t[32 * cblk:32 * cblk + 16, 64:2080:65],
                        in_=stage_t[0:16, 0:32],
                        allow_slow_non_contiguous=True), inc=m_gdma, inc_n=16)
                    p.emit(G, lambda e, cblk=cblk: e.dma_start(
                        out=vaug1_t[32 * cblk + 16:32 * cblk + 32, 64:2080:65],
                        in_=stage_t[0:16, 0:32],
                        allow_slow_non_contiguous=True), inc=m_gdma, inc_n=16)
            p.wait(G, m_TA)
            p.wait(G, m_V)
            for g in range(32):
                p0 = (g % 2) * 64
                cc = (g // 2) * 128
                for cblk in range(4):
                    p.emit(G, lambda e, g=g, p0=p0, cc=cc, cblk=cblk: e.dma_start(
                        out=vaug0_t[32 * cblk:32 * cblk + 16, g * 65:g * 65 + 64],
                        in_=vtok_t[p0 + 16 * cblk:p0 + 16 * cblk + 16, cc:cc + 64]),
                        inc=m_gdma, inc_n=16)
                    p.emit(G, lambda e, g=g, p0=p0, cc=cc, cblk=cblk: e.dma_start(
                        out=vaug1_t[32 * cblk + 16:32 * cblk + 32, g * 65:g * 65 + 64],
                        in_=vtok_t[p0 + 16 * cblk:p0 + 16 * cblk + 16, cc + 64:cc + 128]),
                        inc=m_gdma, inc_n=16)

            p.wait(V, m_dma)
            vf(lambda e: e.tensor_scalar(
                out=localT_t[:, :], in0=vT2_t[:, :], scalar1=small_t[:, 53:54],
                scalar2=None, op0=ALU.mult))
            vf(lambda e: e.scalar_tensor_tensor(
                out=localT_t[:, 1:S], in0=vT2_t[:, 0:S - 1], scalar=small_t[:, 52:53],
                in1=localT_t[:, 1:S], op0=ALU.mult, op1=ALU.add))
            vf(lambda e: e.scalar_tensor_tensor(
                out=localT_t[:, 0:S - 1], in0=vT2_t[:, 1:S], scalar=small_t[:, 54:55],
                in1=localT_t[:, 0:S - 1], op0=ALU.mult, op1=ALU.add))
            vf(lambda e: e.tensor_scalar(
                out=localT_t[:, :], in0=localT_t[:, :], scalar1=small_t[:, 55:56],
                scalar2=None, op0=ALU.add))

            p.wait(Y, m_V)
            for ob in range(8):
                p.emit(Y, lambda e, hp=hp, ob=ob: e.dma_start(
                    out=bass.AP(xTg_t, ob * 128, [[1024, 128], [1, 128]]),
                    in_=bass.AP(proj_w.tensor, ob * 128 * D + hp * 128, [[D, 128], [1, 128]])),
                    inc=m_dma, inc_n=16)
            p.wait(T, m_dma)
            p.wait(T, m_V)
            for grp in range(2):
                psl = ps_a if grp == 0 else ps_b
                for q in range(4):
                    ob = grp * 4 + q
                    p.emit(T, lambda e, ob=ob, q=q, psl=psl: e.transpose(
                        psl[0:128, q * 128:(q + 1) * 128],
                        xTg_t[:, ob * 128:(ob + 1) * 128], ident_t[:, :]),
                        inc=(m_T if q == 3 else None))
                p.wait(V, m_T)
                p.emit(V, lambda e, grp=grp, psl=psl: e.tensor_copy(
                    out=wt3_t[:, (grp * 4) * 128:(grp * 4 + 4) * 128], in_=psl[:, 0:512]), inc=m_V)

            p.wait(Sc, m_V)
            p.wait(Sc, m_dma)
            p.wait(Sc, m_gdma)
            p.wait(V, m_gdma)
            p.wait(V, m_T)
            kern_mTA = [0, 0]
            bankV = {b: 0 for b in range(4)}

            def ga_sl(h, m, rows=65):
                t_ = sqt_t if h == 0 else xTg_t
                return t_[0:rows, m * 512:(m + 1) * 512]

            for icp in range(2):
                sig_wait = {0: {k: p.val(m_SE) for k in range(8)}}
                pending = None

                def consume(pend, and_cnt, icp=icp):
                    base_u, g, q4, v_list, sub_cnt, ps_cnt = pend
                    p.wait(T, m_PS, ps_cnt)
                    p.wait(T, m_PV, and_cnt)
                    tc_vals = {}
                    for m in range(2):
                        for r in range(4):
                            t = q4 * 4 + r
                            u = base_u + r
                            slot = u % NSLOT
                            cg = t % 4
                            s_ = t // 4
                            bank = 4 * m + cg
                            if q4 == 0:
                                p.wait(T, m_SE, sig_wait[g][bank])
                                if m == 0:
                                    p.wait(T, m_V, bankV[cg])
                            dsl_m = dslot(u)[:, m * 512:(m + 1) * 512]
                            p.emit(T, lambda e, dsl_m=dsl_m, cg=cg, s_=s_, q4=q4, bank=bank: e.matmul(
                                ps_sig[32 * cg:32 * cg + 32, 512 * bank:512 * bank + 512],
                                masks_t[:, 32 * s_:32 * s_ + 32],
                                dsl_m,
                                start=(q4 == 0), stop=(q4 == 15),
                                tile_position=(0, 32 * cg)), inc=m_TC)
                            tc_vals[(m, cg)] = p.val(m_TC)
                    if q4 == 15:
                        g_ = g
                        kslot = g_ % 2
                        p.wait(Sc, m_TA, kern_mTA[kslot])
                        nxt = {}
                        for m in range(2):
                            for cg in range(4):
                                bank = 4 * m + cg
                                p.wait(Sc, m_TC, tc_vals[(m, cg)])
                                p.emit(Sc, lambda e, cg=cg, m=m, bank=bank, kslot=kslot: e.activation(
                                    out=kbuf[kslot][32 * cg:32 * cg + 32, m * 512:m * 512 + 512],
                                    in_=ps_sig[32 * cg:32 * cg + 32, 512 * bank:512 * bank + 512],
                                    func=ACT.Exp, bias=0.0, scale=LAMBDA_SCALE), inc=m_SE)
                                nxt[bank] = p.val(m_SE)
                        sig_wait[g_ + 1] = nxt
                        p.wait(T, m_SE)
                        if g_ == 0:
                            p.wait(T, m_gdma)
                        ta_vals = {}
                        for h in range(2):
                            vaug = vaug0_t if h == 0 else vaug1_t
                            for m in range(2):
                                b = 2 * h + m
                                p.wait(T, m_V, bankV[b])
                                p.emit(T, lambda e, h=h, m=m, g_=g_, b=b, vaug=vaug, kslot=kslot: e.matmul(
                                    ps_sig[0:65, 512 * b:512 * b + 512],
                                    vaug[:, g_ * 65:g_ * 65 + 65],
                                    kbuf[kslot][:, m * 512:(m + 1) * 512],
                                    start=True, stop=True), inc=m_TA)
                                ta_vals[b] = p.val(m_TA)
                        kern_mTA[kslot] = p.val(m_TA)
                        for h in range(2):
                            for m in range(2):
                                b = 2 * h + m
                                p.wait(V, m_TA, ta_vals[b])
                                if g_ == 0:
                                    p.emit(V, lambda e, h=h, m=m, b=b: e.tensor_copy(
                                        out=ga_sl(h, m), in_=ps_sig[0:65, 512 * b:512 * b + 512]),
                                        inc=m_V)
                                else:
                                    p.emit(V, lambda e, h=h, m=m, b=b: e.tensor_tensor(
                                        out=ga_sl(h, m), in0=ga_sl(h, m),
                                        in1=ps_sig[0:65, 512 * b:512 * b + 512], op=ALU.add),
                                        inc=m_V)
                                bankV[b] = p.val(m_V)

                for g in range(32):
                    for q4 in range(16):
                        jgs = []
                        for r in range(4):
                            t = q4 * 4 + r
                            jprime = 16 * (t % 4) + (t // 4)
                            jgs.append(g * 64 + jprime)
                        pa = None
                        if pending is not None:
                            pa = produce_ands(pending[3], pending[4])
                        v_list, sub_cnt, ps_cnt = produce_first(unit[0], icp, jgs)
                        cur = (unit[0], g, q4, v_list, sub_cnt, ps_cnt)
                        unit[0] += 4
                        if pending is not None:
                            consume(pending, pa)
                        pending = cur
                consume(pending, produce_ands(pending[3], pending[4]))
                pending = None

                p.wait(V, m_V)
                for h in range(2):
                    for m in range(2):
                        vf(lambda e, h=h, m=m: e.tensor_scalar(
                            out=stage_t[0:1, 0:512], in0=ga_sl(h, m)[64:65, 0:512],
                            scalar1=ATT_EPS, scalar2=None, op0=ALU.add))
                        vf(lambda e: e.reciprocal(
                            out=stage_t[0:1, 0:512], in_=stage_t[0:1, 0:512]))
                        p.wait(T, m_V)
                        p.emit(T, lambda e: e.matmul(
                            ps_sig[0:64, 0:512],
                            onesrow_t[0:1, 0:64], stage_t[0:1, 0:512],
                            start=True, stop=True), inc=m_T)
                        p.wait(V, m_T)
                        vf(lambda e: e.tensor_copy(
                            out=stage_t[0:64, 512:1024], in_=ps_sig[0:64, 0:512]))
                        lsl = icp * 1024 + m * 512
                        vf(lambda e, h=h, m=m: e.tensor_tensor(
                            out=yT_t[h * 64:h * 64 + 64, m * 512:(m + 1) * 512],
                            in0=ga_sl(h, m, rows=64), in1=stage_t[0:64, 512:1024], op=ALU.mult))
                        vf(lambda e, h=h, m=m, lsl=lsl: e.tensor_tensor(
                            out=yT_t[h * 64:h * 64 + 64, m * 512:(m + 1) * 512],
                            in0=yT_t[h * 64:h * 64 + 64, m * 512:(m + 1) * 512],
                            in1=localT_t[h * 64:h * 64 + 64, lsl:lsl + 512], op=ALU.add))
                p.wait(T, m_V)
                for ob in range(8):
                    for m in range(2):
                        psl = ps_sig[:, 512:1024] if (ob * 2 + m) % 2 == 0 else ps_sig[:, 1024:1536]
                        p.wait(T, m_V)
                        p.emit(T, lambda e, ob=ob, m=m, psl=psl: e.matmul(
                            psl, wt3_t[:, ob * 128:(ob + 1) * 128],
                            yT_t[:, m * 512:(m + 1) * 512], start=True, stop=True), inc=m_T)
                        p.wait(V, m_T)
                        tc = icp * 2 + m
                        p.emit(V, lambda e, ob=ob, tc=tc, psl=psl: e.tensor_tensor(
                            out=R_t[:, ob * S + tc * 512: ob * S + (tc + 1) * 512],
                            in0=R_t[:, ob * S + tc * 512: ob * S + (tc + 1) * 512],
                            in1=psl[:, 0:512], op=ALU.add), inc=m_V)
                for b in range(4):
                    bankV[b] = p.val(m_V)

        barrier()
        prod_t = dbuf_t
        gu_t = xt0_t
        selwB_t = xt1_t
        ps_g = lambda: ps_sig[:, 2048:2560]
        ps_u = lambda: ps_sig[:, 2560:3072]
        ps_d = lambda: ps_sig[:, 3072:3584]

        def load_transpose_slab(src_dram, row0, ncols, dst_off, nblk):
            p.wait(Y, m_V)
            p.emit(Y, lambda e: e.dma_start(
                out=sqt_t[:, 0:ncols], in_=src_dram[row0:row0 + 128, 0:ncols]),
                inc=m_dma, inc_n=16)
            p.wait(T, m_dma)
            p.wait(T, m_V)
            for grp in range((nblk + 3) // 4):
                psl = ps_a if grp % 2 == 0 else ps_b
                n_in_grp = min(4, nblk - grp * 4)
                for q in range(n_in_grp):
                    dc = grp * 4 + q
                    p.emit(T, lambda e, dc=dc, q=q, psl=psl: e.transpose(
                        psl[0:128, q * 128:(q + 1) * 128],
                        sqt_t[:, dc * 128:(dc + 1) * 128], ident_t[:, :]),
                        inc=(m_T if q == n_in_grp - 1 else None))
                p.wait(V, m_T)
                p.emit(V, lambda e, grp=grp, n_in_grp=n_in_grp, psl=psl, dst_off=dst_off: e.tensor_copy(
                    out=wt3_t[:, dst_off + grp * 512:dst_off + grp * 512 + n_in_grp * 128],
                    in_=psl[:, 0:n_in_grp * 128]), inc=m_V)
                p.wait(T, m_V)

        def ffn_block(gw_d, uw_d, dw_d, gw_row0, dw_row0, n_sib, dw_ncols, selw_e):
            for tq in range(4):
                p.wait(Y, m_T)
                for dc in range(8):
                    p.emit(Y, lambda e, tq=tq, dc=dc: e.dma_start(
                        out=bass.AP(xnt2_t, dc * 512, [[4096, 128], [1, 512]]),
                        in_=bass.AP(xnT_hbm.tensor, dc * 128 * S + tq * 512, [[S, 128], [1, 512]])),
                        inc=m_dma, inc_n=16)
                if selw_e is not None:
                    p.wait(T, m_dma)
                    p.wait(T, m_V)
                    p.wait(T, m_gdma)
                    p.emit(T, lambda e, tq=tq: e.matmul(
                        ps_sig[:, 3584:4096], onesrow_t[0:1, :],
                        localT_t[0:1, tq * 512:(tq + 1) * 512],
                        start=True, stop=True), inc=m_T)
                    p.wait(V, m_T)
                    p.emit(V, lambda e: e.tensor_copy(
                        out=selwB_t[:, 512:1024], in_=ps_sig[:, 3584:4096]), inc=m_V)
                for sib in range(n_sib):
                    load_transpose_slab(gw_d, gw_row0 + sib * 128, D, 0, 8)
                    p.wait(T, m_dma)
                    p.wait(T, m_V)
                    for dc in range(8):
                        p.emit(T, lambda e, dc=dc: e.matmul(
                            ps_g(), wt3_t[:, dc * 128:(dc + 1) * 128],
                            xnt2_t[:, dc * 512:(dc + 1) * 512],
                            start=(dc == 0), stop=(dc == 7)), inc=(m_T if dc == 7 else None))
                    p.wait(Sc, m_T)
                    p.wait(Sc, m_V)
                    p.emit(Sc, lambda e: e.activation(
                        out=stage_t[:, 0:512], in_=ps_g(), func=ACT.Sigmoid), inc=m_S)
                    load_transpose_slab(uw_d, gw_row0 + sib * 128, D, 1024, 8)
                    p.wait(T, m_dma)
                    p.wait(T, m_V)
                    for dc in range(8):
                        p.emit(T, lambda e, dc=dc: e.matmul(
                            ps_u(), wt3_t[:, 1024 + dc * 128:1024 + (dc + 1) * 128],
                            xnt2_t[:, dc * 512:(dc + 1) * 512],
                            start=(dc == 0), stop=(dc == 7)), inc=(m_T if dc == 7 else None))
                    p.wait(V, m_S)
                    p.wait(V, m_T)
                    vf(lambda e: e.tensor_tensor(
                        out=gu_t[:, 0:512], in0=ps_g(), in1=stage_t[:, 0:512], op=ALU.mult))
                    vf(lambda e, sib=sib: e.tensor_tensor(
                        out=prod_t[:, sib * 512:(sib + 1) * 512], in0=gu_t[:, 0:512],
                        in1=ps_u(), op=ALU.mult))
                for cb in range(8):
                    load_transpose_slab(dw_d, dw_row0 + cb * 128, dw_ncols, 2048, n_sib)
                    p.wait(T, m_dma)
                    p.wait(T, m_V)
                    for sic in range(n_sib):
                        p.emit(T, lambda e, sic=sic: e.matmul(
                            ps_d(), wt3_t[:, 2048 + sic * 128:2048 + (sic + 1) * 128],
                            prod_t[:, sic * 512:(sic + 1) * 512],
                            start=(sic == 0), stop=(sic == n_sib - 1)),
                            inc=(m_T if sic == n_sib - 1 else None))
                    p.wait(V, m_T)
                    if selw_e is None:
                        p.emit(V, lambda e, cb=cb, tq=tq: e.tensor_tensor(
                            out=R_t[:, cb * S + tq * 512: cb * S + (tq + 1) * 512],
                            in0=R_t[:, cb * S + tq * 512: cb * S + (tq + 1) * 512],
                            in1=ps_d(), op=ALU.add), inc=m_V)
                    else:
                        vf(lambda e: e.tensor_tensor(
                            out=stage_t[:, 512:1024], in0=ps_d(), in1=selwB_t[:, 512:1024],
                            op=ALU.mult))
                        p.emit(V, lambda e, cb=cb, tq=tq: e.tensor_tensor(
                            out=R_t[:, cb * S + tq * 512: cb * S + (tq + 1) * 512],
                            in0=R_t[:, cb * S + tq * 512: cb * S + (tq + 1) * 512],
                            in1=stage_t[:, 512:1024], op=ALU.add), inc=m_V)

        ffn_block(sh_gw, sh_uw, sh_dw, 0, 0, 8, SI, None)
        for ex in range(E):
            p.wait(G, m_T)
            p.wait(G, m_V)
            for tb16 in range(16):
                p.emit(G, lambda e, ex=ex, tb16=tb16: e.dma_start(
                    out=localT_t[0:1, tb16 * 128:(tb16 + 1) * 128],
                    in_=selwT_t[E * tb16 + ex:E * tb16 + ex + 1, 0:128]), inc=m_gdma, inc_n=16)
            ffn_block(exp_gw, exp_uw, exp_dw, ex * I_EXP, ex * D, 4, I_EXP, ex)

        p.wait(T, m_V)
        for tb in range(16):
            p.wait(V, m_dma, p.val(m_dma))
            for half in range(2):
                psl = ps_a if half == 0 else ps_b
                p.wait(T, m_V)
                for q in range(4):
                    cb = half * 4 + q
                    p.emit(T, lambda e, cb=cb, q=q, tb=tb, psl=psl: e.transpose(
                        psl[0:128, q * 128:(q + 1) * 128],
                        R_t[:, cb * S + tb * 128: cb * S + (tb + 1) * 128],
                        ident_t[:, :]), inc=(m_T if q == 3 else None))
                p.wait(V, m_T)
                p.emit(V, lambda e, half=half, psl=psl: e.tensor_copy(
                    out=stage_t[:, half * 512:(half + 1) * 512], in_=psl[:, 0:512]), inc=m_V)
            p.wait(Y, m_V)
            p.emit(Y, lambda e, tb=tb: e.dma_start(
                out=out[tb * 128:(tb + 1) * 128, :], in_=stage_t[:, :]), inc=m_dma, inc_n=16)
        p.wait(G, m_dma)
        p.wait(G, m_gdma)

        p.finalize(block)

    return nc, dbg


def _inputs_common(L0_attn_qkv_w, L0_attn_qkv_b, L0_attn_proj_w, L0_attn_proj_b,
                   L0_attn_dwc_w, L0_attn_dwc_b, L0_ffn_gate_w, L0_ffn_lb_bias,
                   L0_ffn_expert_gate_w, L0_ffn_expert_up_w, L0_ffn_expert_down_w,
                   L0_ffn_shared_gate_w, L0_ffn_shared_up_w, L0_ffn_shared_down_w):
    f = np.float32
    return {
        "qkv_w": np.ascontiguousarray(L0_attn_qkv_w, dtype=f),
        "qkv_b": np.ascontiguousarray(np.asarray(L0_attn_qkv_b, dtype=f).reshape(3 * D, 1)),
        "proj_w": np.ascontiguousarray(L0_attn_proj_w, dtype=f),
        "proj_b": np.ascontiguousarray(np.asarray(L0_attn_proj_b, dtype=f).reshape(D, 1)),
        "dwc_w": np.ascontiguousarray(np.asarray(L0_attn_dwc_w, dtype=f).reshape(D, 3)),
        "dwc_b": np.ascontiguousarray(np.asarray(L0_attn_dwc_b, dtype=f).reshape(D, 1)),
        "gate_w": np.ascontiguousarray(L0_ffn_gate_w, dtype=f),
        "lb_bias": np.ascontiguousarray(np.asarray(L0_ffn_lb_bias, dtype=f).reshape(E, 1)),
        "exp_gw": np.ascontiguousarray(np.asarray(L0_ffn_expert_gate_w, dtype=f).reshape(E * I_EXP, D)),
        "exp_uw": np.ascontiguousarray(np.asarray(L0_ffn_expert_up_w, dtype=f).reshape(E * I_EXP, D)),
        "exp_dw": np.ascontiguousarray(np.asarray(L0_ffn_expert_down_w, dtype=f).reshape(E * D, I_EXP)),
        "sh_gw": np.ascontiguousarray(L0_ffn_shared_gate_w, dtype=f),
        "sh_uw": np.ascontiguousarray(L0_ffn_shared_up_w, dtype=f),
        "sh_dw": np.ascontiguousarray(L0_ffn_shared_down_w, dtype=f),
    }


def kernel(L0_attn_x, **kw):
    x = np.ascontiguousarray(np.asarray(L0_attn_x, dtype=np.float32))
    common = _inputs_common(**kw)
    common["xf"] = np.ascontiguousarray(x.reshape(N_ALL, D))
    nc, _ = build()
    in_maps = [dict(common, xo=np.ascontiguousarray(x[b])) for b in range(B)]
    res = run_bass_kernel_spmd(nc, in_maps, core_ids=list(range(B)))
    return np.stack([res.results[b]["out"] for b in range(B)], axis=0)


# version 10
# speedup vs baseline: 1.8340x; 1.1139x over previous
import contextlib

import numpy as np
import concourse.bass as bass
import concourse.mybir as mybir
from concourse.bass_utils import run_bass_kernel_spmd

F32 = mybir.dt.float32
I32 = mybir.dt.int32
ALU = mybir.AluOpType
ACT = mybir.ActivationFunctionType
AXL = mybir.AxisListType

B, S, D = 8, 2048, 1024
H, HD = 16, 64
HP = 8
E, I_EXP, SI = 8, 512, 1024
N_ALL = B * S
LAMBDA_SCALE = -4.0 / 64.0
LN_EPS = 1e-5
ATT_EPS = 1e-6
GATE_EPS = 1e-8
ABSMASK = 0x7FFFFFFF

ENGINES = ("sync", "vector", "scalar", "tensor", "gpsimd")


class Prog:

    def __init__(self):
        self.steps = {e: [] for e in ENGINES}
        self.ctr = {}

    def val(self, sem):
        return self.ctr.get(id(sem), 0)

    def bump(self, sem, n):
        self.ctr[id(sem)] = self.ctr.get(id(sem), 0) + n
        return self.ctr[id(sem)]

    def set_dma_queues(self, mapping, boundary_sem):
        self._dmaq = {id(s): q for s, q in mapping.items()}
        self._qbound = {id(s): 0 for s in mapping}

    def wait(self, eng, sem, val=None):
        v = self.val(sem) if val is None else val
        if v <= 0:
            return
        dq = getattr(self, "_dmaq", None)
        if dq is not None and id(sem) in dq and eng != dq[id(sem)]:
            if self._qbound[id(sem)] < v:
                q = dq[id(sem)]
                self.steps[q].append(lambda e, sem=sem, v=v: e.wait_ge(sem, v))
                self._qbound[id(sem)] = v
        self.steps[eng].append(lambda e, sem=sem, v=v: e.wait_ge(sem, v))

    def emit(self, eng, fn, inc=None, inc_n=1):
        if inc is not None:
            self.bump(inc, inc_n)
        self.steps[eng].append(
            lambda e, fn=fn, inc=inc, inc_n=inc_n:
            (fn(e).then_inc(inc, inc_n) if inc is not None else fn(e)))

    def finalize(self, block):
        for name in ENGINES:
            steps = self.steps[name]

            def mk(steps):
                def _(e):
                    for st in steps:
                        st(e)
                return _
            getattr(block, name)(mk(steps))


def build(debug=()):
    nc = bass.Bass("TRN2", target_bir_lowering=False)
    dt = F32

    xo = nc.dram_tensor("xo", [S, D], dt, kind="ExternalInput").ap()
    xf = nc.dram_tensor("xf", [N_ALL, D], dt, kind="ExternalInput").ap()
    qkv_w = nc.dram_tensor("qkv_w", [3 * D, D], dt, kind="ExternalInput").ap()
    qkv_b = nc.dram_tensor("qkv_b", [3 * D, 1], dt, kind="ExternalInput").ap()
    proj_w = nc.dram_tensor("proj_w", [D, D], dt, kind="ExternalInput").ap()
    proj_b = nc.dram_tensor("proj_b", [D, 1], dt, kind="ExternalInput").ap()
    dwc_w = nc.dram_tensor("dwc_w", [D, 3], dt, kind="ExternalInput").ap()
    dwc_b = nc.dram_tensor("dwc_b", [D, 1], dt, kind="ExternalInput").ap()
    gate_w = nc.dram_tensor("gate_w", [E, D], dt, kind="ExternalInput").ap()
    lb_bias = nc.dram_tensor("lb_bias", [E, 1], dt, kind="ExternalInput").ap()
    exp_gw = nc.dram_tensor("exp_gw", [E * I_EXP, D], dt, kind="ExternalInput").ap()
    exp_uw = nc.dram_tensor("exp_uw", [E * I_EXP, D], dt, kind="ExternalInput").ap()
    exp_dw = nc.dram_tensor("exp_dw", [E * D, I_EXP], dt, kind="ExternalInput").ap()
    sh_gw = nc.dram_tensor("sh_gw", [SI, D], dt, kind="ExternalInput").ap()
    sh_uw = nc.dram_tensor("sh_uw", [SI, D], dt, kind="ExternalInput").ap()
    sh_dw = nc.dram_tensor("sh_dw", [D, SI], dt, kind="ExternalInput").ap()
    out = nc.dram_tensor("out", [S, D], dt, kind="ExternalOutput").ap()
    xnT_hbm = nc.dram_tensor("xnT_scr", [D, S], dt).ap()

    dbg = {}
    for name, shape in debug:
        dbg[name] = nc.dram_tensor(name, list(shape), dt, kind="ExternalOutput").ap()

    p = Prog()

    with contextlib.ExitStack() as _st:
        def sb(name, shape):
            return _st.enter_context(nc.sbuf_tensor(name, shape, dt))

        def psum(name, shape):
            return _st.enter_context(nc.psum_tensor(name, shape, dt))

        def sem(name):
            return _st.enter_context(nc.semaphore(name))

        R_t = sb("R", [128, 8 * S])
        ident_t = sb("ident", [128, 128])
        onesrow_t = sb("onesrow", [1, 128])
        onescol_t = sb("onescol", [128, 1])
        masks_t = sb("masks", [128, 16 * 32])
        selwT_t = sb("selwT", [128, 128])
        selown_t = sb("selown", [128, 16 * E])
        wmeanB_t = sb("wmeanB", [128, E])
        negGB_t = sb("negGB", [128, E])
        lbB_t = sb("lbB", [128, E])
        gwT_t = sb("gwT", [128, 8 * E])
        wacc_t = sb("wacc", [128, 2 * E])
        small_t = sb("small", [128, 64])
        projb_t = sb("projb", [128, 8])
        xt0_t = sb("xt0", [128, D])
        xt1_t = sb("xt1", [128, D])
        sqt_t = sb("sqt", [128, D])
        xnt_t = sb("xnt", [128, D])
        xTg_t = sb("xTg", [128, 8 * 128])
        gs_t = sb("gs", [128, 24 * E])
        stat_t = sb("stat", [128, 8])
        stage_t = sb("stage", [128, D])
        wt3_t = sb("wt3", [128, 3 * 8 * 128])
        xnt2_t = sb("xnt2", [128, 8 * 512])
        dbuf_t = sb("dbuf", [128, 14 * 512])
        nqT2_t = sb("nqT2", [128, S])
        kT2_t = sb("kT2", [128, S])
        vT2_t = sb("vT2", [128, S])
        vaug0_t = sb("vaug0", [128, 32 * 65])
        vaug1_t = sb("vaug1", [128, 32 * 65])
        vtok_t = sb("vtok", [128, 16 * 128])
        localT_t = sb("localT", [128, S])
        ps_a = psum("ps_a", [128, 512])
        ps_b = psum("ps_b", [128, 512])
        ps_sig = psum("ps_sig", [128, 2048])
        ps_ga = psum("ps_ga", [66, 1024])
        m_dma = sem("m_dma")
        m_gdma = sem("m_gdma")
        m_T = sem("m_T")
        m_TA = sem("m_TA")
        m_TC = sem("m_TC")
        m_V = sem("m_V")
        m_PV = sem("m_PV")
        m_S = sem("m_S")
        m_PS = sem("m_PS")
        m_SE = sem("m_SE")
        m_G = sem("m_G")
        block = _st.enter_context(nc.Block())

        V, Sc, T, G, Y = "vector", "scalar", "tensor", "gpsimd", "sync"
        p.set_dma_queues({m_dma: Y, m_gdma: G}, m_G)

        def vf(fn):
            p.emit(V, fn, inc=m_V)
            p.wait(V, m_V)

        def sf(fn):
            p.emit(Sc, fn, inc=m_S)
            p.wait(Sc, m_S)

        p.emit(V, lambda e: e.memset(ident_t[:], 0.0))
        p.emit(V, lambda e: e.memset(onesrow_t[:], 1.0))
        p.emit(V, lambda e: e.memset(onescol_t[:], 1.0))
        p.emit(V, lambda e: e.memset(masks_t[:], 0.0))
        p.emit(V, lambda e: e.memset(wacc_t[:], 0.0), inc=m_V)
        p.wait(G, m_V)
        p.emit(G, lambda e: e.affine_select(
            out=ident_t[:], in_=ident_t[:], compare_op=ALU.not_equal,
            fill=1.0, base=0, pattern=[[-1, 128]], channel_multiplier=1))
        for s_ in range(16):
            p.emit(G, lambda e, s_=s_: e.memset(
                masks_t[0:64, 32 * s_ + s_: 32 * s_ + s_ + 1], 1.0))
            p.emit(G, lambda e, s_=s_: e.memset(
                masks_t[64:128, 32 * s_ + 16 + s_: 32 * s_ + 16 + s_ + 1], 1.0),
                inc=(m_G if s_ == 15 else None))

        p.emit(Y, lambda e: e.dma_start(out=gs_t[0:E, 0:1], in_=lb_bias[:, :]), inc=m_dma, inc_n=16)
        for cb in range(8):
            p.emit(Y, lambda e, cb=cb: e.dma_start(
                out=projb_t[:, cb:cb + 1],
                in_=proj_b[cb * 128:(cb + 1) * 128, 0:1]), inc=m_dma, inc_n=16)
        p.emit(Y, lambda e: e.dma_start(out=xTg_t[0:E, 0:D], in_=gate_w[:, :]), inc=m_dma, inc_n=16)
        p.wait(T, m_dma)
        p.wait(T, m_G)
        for dc in range(8):
            p.emit(T, lambda e, dc=dc: e.transpose(
                ps_a[0:128, dc * E:(dc + 1) * E], xTg_t[0:E, dc * 128:(dc + 1) * 128],
                ident_t[0:E, 0:E]), inc=(m_T if dc == 7 else None))
        p.wait(V, m_T)
        p.emit(V, lambda e: e.tensor_copy(out=gwT_t[:, :], in_=ps_a[:, 0:8 * E]))
        vf(lambda e: e.reduce_sum(out=small_t[0:E, 0:1], in_=xTg_t[0:E, 0:D], axis=AXL.X))
        vf(lambda e: e.tensor_scalar(
            out=small_t[0:E, 1:2], in0=small_t[0:E, 0:1], scalar1=-1.0, scalar2=None,
            op0=ALU.mult))
        p.wait(T, m_V)
        p.emit(T, lambda e: e.transpose(ps_b[0:1, 0:E], small_t[0:E, 1:2], ident_t[0:E, 0:E]))
        p.emit(T, lambda e: e.transpose(ps_b[0:1, E:2 * E], gs_t[0:E, 0:1],
                                        ident_t[0:E, 0:E]), inc=m_T)
        p.wait(V, m_T)
        p.emit(V, lambda e: e.tensor_copy(out=small_t[0:1, 8:8 + 2 * E], in_=ps_b[0:1, 0:2 * E]), inc=m_V)
        p.wait(T, m_V)
        p.emit(T, lambda e: e.matmul(
            ps_a[0:128, 0:2 * E], onesrow_t[0:1, :], small_t[0:1, 8:8 + 2 * E],
            start=True, stop=True), inc=m_T)
        p.wait(V, m_T)
        p.emit(V, lambda e: e.tensor_copy(out=negGB_t[:, :], in_=ps_a[:, 0:E]))
        p.emit(V, lambda e: e.tensor_copy(out=lbB_t[:, :], in_=ps_a[:, E:2 * E]), inc=m_V)

        def ln_stats(xt):
            vf(lambda e: e.reduce_sum(out=stat_t[:, 0:1], in_=xt[:, :], axis=AXL.X))
            vf(lambda e: e.tensor_scalar(
                out=stat_t[:, 2:3], in0=stat_t[:, 0:1], scalar1=1.0 / D, scalar2=None,
                op0=ALU.mult))
            p.wait(Sc, m_V)
            p.emit(Sc, lambda e: e.activation(out=sqt_t[:, :], in_=xt[:, :], func=ACT.Square), inc=m_S)
            p.wait(V, m_S)
            vf(lambda e: e.reduce_sum(out=stat_t[:, 1:2], in_=sqt_t[:, :], axis=AXL.X))
            vf(lambda e: e.tensor_scalar(
                out=stat_t[:, 3:4], in0=stat_t[:, 1:2], scalar1=1.0 / D, scalar2=None, op0=ALU.mult))
            vf(lambda e: e.tensor_tensor(
                out=stat_t[:, 4:5], in0=stat_t[:, 2:3], in1=stat_t[:, 2:3], op=ALU.mult))
            vf(lambda e: e.tensor_tensor(
                out=stat_t[:, 5:6], in0=stat_t[:, 3:4], in1=stat_t[:, 4:5], op=ALU.subtract))
            vf(lambda e: e.tensor_scalar(
                out=stat_t[:, 5:6], in0=stat_t[:, 5:6], scalar1=LN_EPS, scalar2=None,
                op0=ALU.add))
            p.wait(Sc, m_V)
            p.emit(Sc, lambda e: e.activation(out=stat_t[:, 6:7], in_=stat_t[:, 5:6], func=ACT.Sqrt), inc=m_S)
            p.wait(V, m_S)
            vf(lambda e: e.reciprocal(out=stat_t[:, 7:8], in_=stat_t[:, 6:7]))

        def transpose_tile_to(xt, evac):
            p.wait(T, m_V)
            for half in range(2):
                psl = ps_a if half == 0 else ps_b
                for q in range(4):
                    dc = half * 4 + q
                    p.emit(T, lambda e, dc=dc, q=q, psl=psl, xt=xt: e.transpose(
                        psl[0:128, q * 128:(q + 1) * 128], xt[:, dc * 128:(dc + 1) * 128],
                        ident_t[:, :]), inc=(m_T if q == 3 else None))
                p.wait(V, m_T)
                evac(half, psl)

        def gate_tile(xt, sel_store=None):
            def evac(half, psl):
                p.emit(V, lambda e, half=half, psl=psl: e.tensor_copy(
                    out=xTg_t[:, half * 512:(half + 1) * 512], in_=psl[:, 0:512]), inc=m_V)
            transpose_tile_to(xt, evac)
            p.wait(T, m_V)
            for dc in range(8):
                p.emit(T, lambda e, dc=dc: e.matmul(
                    ps_a[0:E, 0:128], gwT_t[:, dc * E:(dc + 1) * E],
                    xTg_t[:, dc * 128:(dc + 1) * 128],
                    start=(dc == 0), stop=(dc == 7)), inc=(m_T if dc == 7 else None))
            p.wait(V, m_T)
            p.emit(V, lambda e: e.tensor_copy(out=gs_t[0:E, 8:8 + 128], in_=ps_a[0:E, 0:128]), inc=m_V)
            p.wait(T, m_V)
            p.emit(T, lambda e: e.transpose(
                ps_b[0:128, 0:E], gs_t[0:E, 8:8 + 128], ident_t[0:E, 0:E]), inc=m_T)
            p.wait(V, m_T)
            g0 = 16 * E
            rtok = gs_t[:, g0:g0 + E]
            logit = gs_t[:, g0 + E:g0 + 2 * E]
            scores = gs_t[:, g0 + 2 * E:g0 + 3 * E]
            iseq = gs_t[:, g0 + 3 * E:g0 + 4 * E]
            masked = gs_t[:, g0 + 4 * E:g0 + 5 * E]
            sel = gs_t[:, g0 + 5 * E:g0 + 6 * E]
            wvals = gs_t[:, g0 + 6 * E:g0 + 7 * E]
            vf(lambda e: e.tensor_copy(out=rtok, in_=ps_b[:, 0:E]))
            vf(lambda e: e.scalar_tensor_tensor(
                out=logit, in0=negGB_t[:, :], scalar=stat_t[:, 2:3], in1=rtok,
                op0=ALU.mult, op1=ALU.add))
            vf(lambda e: e.tensor_scalar(
                out=logit, in0=logit, scalar1=stat_t[:, 7:8], scalar2=None, op0=ALU.mult))
            vf(lambda e: e.tensor_tensor(
                out=logit, in0=logit, in1=lbB_t[:, :], op=ALU.add))
            p.wait(Sc, m_V)
            p.emit(Sc, lambda e: e.activation(out=scores, in_=logit, func=ACT.Sigmoid), inc=m_S)
            p.wait(V, m_S)
            vf(lambda e: e.reduce_max(out=stat_t[:, 0:1], in_=scores, axis=AXL.X))
            vf(lambda e: e.tensor_scalar(
                out=iseq, in0=scores, scalar1=stat_t[:, 0:1], scalar2=None, op0=ALU.is_ge))
            vf(lambda e: e.scalar_tensor_tensor(
                out=masked, in0=iseq, scalar=-1e30, in1=scores, op0=ALU.mult, op1=ALU.add))
            vf(lambda e: e.reduce_max(out=stat_t[:, 1:2], in_=masked, axis=AXL.X))
            vf(lambda e: e.tensor_scalar(
                out=sel, in0=scores, scalar1=stat_t[:, 1:2], scalar2=None, op0=ALU.is_ge))
            vf(lambda e: e.tensor_tensor(
                out=stat_t[:, 2:3], in0=stat_t[:, 0:1], in1=stat_t[:, 1:2], op=ALU.add))
            vf(lambda e: e.tensor_scalar(
                out=stat_t[:, 2:3], in0=stat_t[:, 2:3], scalar1=GATE_EPS, scalar2=None, op0=ALU.add))
            vf(lambda e: e.reciprocal(out=stat_t[:, 3:4], in_=stat_t[:, 2:3]))
            vf(lambda e: e.scalar_tensor_tensor(
                out=wvals, in0=scores, scalar=stat_t[:, 3:4], in1=sel,
                op0=ALU.mult, op1=ALU.mult))
            vf(lambda e: e.tensor_tensor(
                out=wacc_t[:, 0:E], in0=wacc_t[:, 0:E], in1=wvals, op=ALU.add))
            vf(lambda e: e.tensor_tensor(
                out=wacc_t[:, E:2 * E], in0=wacc_t[:, E:2 * E], in1=sel, op=ALU.add))
            if sel_store is not None:
                vf(lambda e, sel_store=sel_store: e.tensor_copy(out=sel_store, in_=sel))

        for tg in range(N_ALL // 128):
            xt = xt0_t if tg % 2 == 0 else xt1_t
            p.wait(Y, m_V)
            p.emit(Y, lambda e, tg=tg, xt=xt: e.dma_start(
                out=xt[:, :], in_=xf[tg * 128:(tg + 1) * 128, :]), inc=m_dma, inc_n=16)
            p.wait(V, m_dma)
            p.wait(Sc, m_dma)
            p.wait(T, m_dma)
            ln_stats(xt)
            gate_tile(xt)

        p.wait(T, m_V)
        p.emit(T, lambda e: e.matmul(
            ps_a[0:2 * E, 0:1], wacc_t[:, 0:2 * E], onescol_t[:, :],
            start=True, stop=True), inc=m_T)
        p.wait(V, m_T)
        p.emit(V, lambda e: e.tensor_copy(out=small_t[0:2 * E, 2:3], in_=ps_a[0:2 * E, 0:1]), inc=m_V)
        p.wait(T, m_V)
        p.emit(T, lambda e: e.transpose(
            ps_b[0:1, 0:2 * E], small_t[0:2 * E, 2:3], ident_t[0:2 * E, 0:2 * E]), inc=m_T)
        p.wait(V, m_T)
        vf(lambda e: e.tensor_copy(out=small_t[0:1, 8:8 + 2 * E], in_=ps_b[0:1, 0:2 * E]))
        vf(lambda e: e.tensor_scalar(
            out=small_t[0:1, 24:24 + E], in0=small_t[0:1, 8 + E:8 + 2 * E],
            scalar1=1.0, scalar2=None, op0=ALU.max))
        vf(lambda e: e.reciprocal(out=small_t[0:1, 32:32 + E], in_=small_t[0:1, 24:24 + E]))
        vf(lambda e: e.tensor_tensor(
            out=small_t[0:1, 40:40 + E], in0=small_t[0:1, 8:8 + E],
            in1=small_t[0:1, 32:32 + E], op=ALU.mult))
        p.wait(T, m_V)
        p.emit(T, lambda e: e.matmul(
            ps_a[0:128, 0:E], onesrow_t[0:1, :], small_t[0:1, 40:40 + E],
            start=True, stop=True), inc=m_T)
        p.wait(V, m_T)
        p.emit(V, lambda e: e.tensor_copy(out=wmeanB_t[:, :], in_=ps_a[:, 0:E]), inc=m_V)

        for tb in range(16):
            xt = xt0_t if tb % 2 == 0 else xt1_t
            p.wait(Y, m_V)
            p.emit(Y, lambda e, tb=tb, xt=xt: e.dma_start(
                out=xt[:, :], in_=xo[tb * 128:(tb + 1) * 128, :]), inc=m_dma, inc_n=16)
            p.wait(V, m_dma)
            p.wait(Sc, m_dma)
            p.wait(T, m_dma)
            ln_stats(xt)
            gate_tile(xt, sel_store=selown_t[:, tb * E:(tb + 1) * E])
            ln_stats(xt)
            vf(lambda e, xt=xt: e.tensor_scalar(
                out=xnt_t[:, :], in0=xt[:, :], scalar1=stat_t[:, 2:3], scalar2=stat_t[:, 7:8],
                op0=ALU.subtract, op1=ALU.mult))

            def evac_R(half, psl, tb=tb):
                p.emit(V, lambda e, half=half, psl=psl, tb=tb: e.tensor_copy(
                    out=bass.AP(R_t, (half * 4) * S + tb * 128, [[8 * S, 128], [S, 4], [1, 128]]),
                    in_=bass.AP(psl, 0, [[512, 128], [128, 4], [1, 128]])), inc=m_V)
            transpose_tile_to(xt, evac_R)

            def evac_stage(half, psl):
                p.emit(V, lambda e, half=half, psl=psl: e.tensor_copy(
                    out=stage_t[:, half * 512:(half + 1) * 512], in_=psl[:, 0:512]), inc=m_V)
            transpose_tile_to(xnt_t, evac_stage)
            p.wait(G, m_V)
            for dc in range(8):
                p.emit(G, lambda e, tb=tb, dc=dc: e.dma_start(
                    out=bass.AP(xnT_hbm.tensor, dc * 128 * S + tb * 128, [[S, 128], [1, 128]]),
                    in_=bass.AP(stage_t, dc * 128, [[1024, 128], [1, 128]])), inc=m_gdma, inc_n=16)
            p.wait(V, m_gdma)

        p.wait(V, m_V)
        for tb in range(16):
            p.emit(V, lambda e, tb=tb: e.tensor_tensor(
                out=selown_t[:, tb * E:(tb + 1) * E], in0=selown_t[:, tb * E:(tb + 1) * E],
                in1=wmeanB_t[:, :], op=ALU.mult), inc=(m_V if tb == 15 else None))
        p.wait(T, m_V)
        for grp in range(4):
            for q in range(4):
                tb = grp * 4 + q
                p.emit(T, lambda e, tb=tb, q=q: e.transpose(
                    ps_a[0:E, q * 128:(q + 1) * 128],
                    selown_t[:, tb * E:(tb + 1) * E], ident_t[:, :]),
                    inc=(m_T if q == 3 else None))
            p.wait(V, m_T)
            p.wait(V, m_gdma)
            p.emit(V, lambda e: e.tensor_copy(
                out=stage_t[0:E, 0:512], in_=ps_a[0:E, 0:512]), inc=m_V)
            p.wait(G, m_V)
            for q in range(4):
                tb = grp * 4 + q
                p.emit(G, lambda e, tb=tb, q=q: e.dma_start(
                    out=selwT_t[E * tb:E * tb + E, 0:128],
                    in_=stage_t[0:E, q * 128:(q + 1) * 128]), inc=m_gdma, inc_n=16)
            p.wait(T, m_V)

        p.wait(V, m_V)
        for cb in range(8):
            p.emit(V, lambda e, cb=cb: e.tensor_scalar(
                out=R_t[:, cb * S:(cb + 1) * S], in0=R_t[:, cb * S:(cb + 1) * S],
                scalar1=projb_t[:, cb:cb + 1], scalar2=None, op0=ALU.add),
                inc=(m_V if cb == 7 else None))
        p.wait(V, m_V)

        if "d_wmean" in dbg:
            p.wait(G, m_V)
            p.emit(G, lambda e: e.dma_start(out=dbg["d_wmean"][:, :], in_=wmeanB_t[0:1, :]), inc=m_gdma, inc_n=16)
        if "d_xnT" in dbg:
            p.wait(G, m_gdma)
            p.emit(G, lambda e: e.dma_start(out=dbg["d_xnT"][:, :], in_=xnT_hbm[:, :]), inc=m_gdma, inc_n=16)


        def barrier():
            for eng in (V, Sc, T, G, Y):
                for s in (m_V, m_S, m_T, m_TA, m_TC, m_SE, m_G, m_dma, m_gdma, m_PS, m_PV):
                    p.wait(eng, s)

        barrier()

        NSLOT = 14
        kern_t = xt0_t
        yT_t = xnt_t
        recB_t = xt1_t
        unit = [0]
        nS_units = [0]
        nV_units = [0]

        def produce_first(base_u, ic, jgs):
            s_list, v_list = [], []
            for r, jg in enumerate(jgs):
                u = base_u + r
                (s_list if (u % 16) < 9 else v_list).append((u, jg))
            qsl = nqT2_t[:, ic * 512:(ic + 1) * 512]
            if s_list:
                p.wait(Sc, m_TC, max(u for u, _ in s_list) - (NSLOT - 1))
                for u, jg in s_list:
                    dsl = dbuf_t[:, (u % NSLOT) * 512:(u % NSLOT) * 512 + 512]
                    p.emit(Sc, lambda e, dsl=dsl, qsl=qsl, jg=jg: e.activation(
                        out=dsl, in_=qsl, func=ACT.Abs,
                        bias=kT2_t[:, jg:jg + 1], scale=1.0), inc=m_PS)
                    nS_units[0] += 1
            if v_list:
                p.wait(V, m_TC, max(u for u, _ in v_list) - (NSLOT - 1))
                for i, (u, jg) in enumerate(v_list):
                    dsl = dbuf_t[:, (u % NSLOT) * 512:(u % NSLOT) * 512 + 512]
                    p.emit(V, lambda e, dsl=dsl, qsl=qsl, jg=jg: e.tensor_scalar(
                        out=dsl, in0=qsl, scalar1=kT2_t[:, jg:jg + 1], scalar2=None,
                        op0=ALU.add), inc=(m_PV if i == len(v_list) - 1 else None))
            return v_list, p.val(m_PV), p.val(m_PS)

        def produce_ands(v_list, sub_count):
            if v_list:
                p.wait(V, m_PV, sub_count)
                for u, jg in v_list:
                    dsl = dbuf_t[:, (u % NSLOT) * 512:(u % NSLOT) * 512 + 512]
                    p.emit(V, lambda e, dsl=dsl: e.tensor_scalar(
                        out=dsl.bitcast(I32), in0=dsl.bitcast(I32), scalar1=ABSMASK,
                        scalar2=None, op0=ALU.bitwise_and), inc=m_PV)
                    nV_units[0] += 1
            return p.val(m_PV)

        for hp in range(HP):
            for mat in range(3):
                p.wait(Y, m_V)
                p.emit(Y, lambda e, mat=mat, hp=hp: e.dma_start(
                    out=sqt_t[:, :],
                    in_=qkv_w[mat * D + hp * 128:(mat * D + hp * 128) + 128, :]),
                    inc=m_dma, inc_n=16)
                p.emit(Y, lambda e, mat=mat, hp=hp: e.dma_start(
                    out=small_t[:, 48 + mat:49 + mat],
                    in_=qkv_b[mat * D + hp * 128:(mat * D + hp * 128) + 128, 0:1]),
                    inc=m_dma, inc_n=16)
                p.wait(T, m_dma)
                p.wait(T, m_V)
                for half in range(2):
                    psl = ps_a if half == 0 else ps_b
                    for q in range(4):
                        dc = half * 4 + q
                        p.emit(T, lambda e, dc=dc, q=q, psl=psl: e.transpose(
                            psl[0:128, q * 128:(q + 1) * 128],
                            sqt_t[:, dc * 128:(dc + 1) * 128], ident_t[:, :]),
                            inc=(m_T if q == 3 else None))
                    p.wait(V, m_T)
                    p.emit(V, lambda e, mat=mat, half=half, psl=psl: e.tensor_copy(
                        out=wt3_t[:, (mat * 8 + half * 4) * 128:(mat * 8 + half * 4 + 4) * 128],
                        in_=psl[:, 0:512]), inc=m_V)
            p.emit(Y, lambda e, hp=hp: e.dma_start(
                out=small_t[:, 52:55], in_=dwc_w[hp * 128:(hp + 1) * 128, :]), inc=m_dma, inc_n=16)
            p.emit(Y, lambda e, hp=hp: e.dma_start(
                out=small_t[:, 55:56], in_=dwc_b[hp * 128:(hp + 1) * 128, 0:1]), inc=m_dma, inc_n=16)

            for tc in range(4):
                p.wait(Y, m_T)
                p.wait(Y, m_gdma)
                for dc in range(8):
                    p.emit(Y, lambda e, tc=tc, dc=dc: e.dma_start(
                        out=bass.AP(xnt2_t, dc * 512, [[4096, 128], [1, 512]]),
                        in_=bass.AP(xnT_hbm.tensor, dc * 128 * S + tc * 512, [[S, 128], [1, 512]])),
                        inc=m_dma, inc_n=16)
                p.wait(T, m_dma)
                for mat in range(3):
                    psl = ps_a if mat % 2 == 0 else ps_b
                    p.wait(T, m_V)
                    for dc in range(8):
                        p.emit(T, lambda e, mat=mat, dc=dc, psl=psl: e.matmul(
                            psl[:, :], wt3_t[:, (mat * 8 + dc) * 128:(mat * 8 + dc + 1) * 128],
                            xnt2_t[:, dc * 512:(dc + 1) * 512],
                            start=(dc == 0), stop=(dc == 7)), inc=(m_T if dc == 7 else None))
                    dstT = (nqT2_t, kT2_t, vT2_t)[mat]
                    p.wait(V, m_T)
                    if mat == 0:
                        p.emit(V, lambda e, dstT=dstT, tc=tc, mat=mat, psl=psl: e.tensor_scalar(
                            out=dstT[:, tc * 512:(tc + 1) * 512], in0=psl[:, 0:512],
                            scalar1=small_t[:, 48 + mat:49 + mat], scalar2=-1.0,
                            op0=ALU.add, op1=ALU.mult), inc=m_V)
                    else:
                        p.emit(V, lambda e, dstT=dstT, tc=tc, mat=mat, psl=psl: e.tensor_scalar(
                            out=dstT[:, tc * 512:(tc + 1) * 512], in0=psl[:, 0:512],
                            scalar1=small_t[:, 48 + mat:49 + mat], scalar2=None, op0=ALU.add), inc=m_V)

            p.wait(V, m_gdma)
            p.wait(T, m_V)
            for grp in range(4):
                psl = ps_a if grp % 2 == 0 else ps_b
                p.wait(T, m_V)
                for q in range(4):
                    tb4 = grp * 4 + q
                    p.emit(T, lambda e, tb4=tb4, q=q, psl=psl: e.transpose(
                        psl[0:128, q * 128:(q + 1) * 128],
                        vT2_t[:, tb4 * 128:(tb4 + 1) * 128], ident_t[:, :]),
                        inc=(m_T if q == 3 else None))
                p.wait(V, m_T)
                p.emit(V, lambda e, grp=grp, psl=psl: e.tensor_copy(
                    out=vtok_t[:, grp * 512:(grp + 1) * 512], in_=psl[:, 0:512]), inc=m_V)
            if hp == 0:
                p.wait(V, m_V)
                p.emit(V, lambda e: e.memset(vaug0_t[:], 0.0))
                p.emit(V, lambda e: e.memset(vaug1_t[:], 0.0), inc=m_V)
                p.wait(V, m_V)
                p.emit(V, lambda e: e.memset(stage_t[0:16, 0:32], 1.0), inc=m_V)
                p.wait(G, m_V)
                for cblk in range(4):
                    p.emit(G, lambda e, cblk=cblk: e.dma_start(
                        out=vaug0_t[32 * cblk:32 * cblk + 16, 64:2080:65],
                        in_=stage_t[0:16, 0:32],
                        allow_slow_non_contiguous=True), inc=m_gdma, inc_n=16)
                    p.emit(G, lambda e, cblk=cblk: e.dma_start(
                        out=vaug1_t[32 * cblk + 16:32 * cblk + 32, 64:2080:65],
                        in_=stage_t[0:16, 0:32],
                        allow_slow_non_contiguous=True), inc=m_gdma, inc_n=16)
            p.wait(G, m_TA)
            p.wait(G, m_V)
            for g in range(32):
                p0 = (g % 2) * 64
                cc = (g // 2) * 128
                for cblk in range(4):
                    p.emit(G, lambda e, g=g, p0=p0, cc=cc, cblk=cblk: e.dma_start(
                        out=vaug0_t[32 * cblk:32 * cblk + 16, g * 65:g * 65 + 64],
                        in_=vtok_t[p0 + 16 * cblk:p0 + 16 * cblk + 16, cc:cc + 64]),
                        inc=m_gdma, inc_n=16)
                    p.emit(G, lambda e, g=g, p0=p0, cc=cc, cblk=cblk: e.dma_start(
                        out=vaug1_t[32 * cblk + 16:32 * cblk + 32, g * 65:g * 65 + 64],
                        in_=vtok_t[p0 + 16 * cblk:p0 + 16 * cblk + 16, cc + 64:cc + 128]),
                        inc=m_gdma, inc_n=16)

            p.wait(V, m_dma)
            vf(lambda e: e.tensor_scalar(
                out=localT_t[:, :], in0=vT2_t[:, :], scalar1=small_t[:, 53:54],
                scalar2=None, op0=ALU.mult))
            vf(lambda e: e.scalar_tensor_tensor(
                out=localT_t[:, 1:S], in0=vT2_t[:, 0:S - 1], scalar=small_t[:, 52:53],
                in1=localT_t[:, 1:S], op0=ALU.mult, op1=ALU.add))
            vf(lambda e: e.scalar_tensor_tensor(
                out=localT_t[:, 0:S - 1], in0=vT2_t[:, 1:S], scalar=small_t[:, 54:55],
                in1=localT_t[:, 0:S - 1], op0=ALU.mult, op1=ALU.add))
            vf(lambda e: e.tensor_scalar(
                out=localT_t[:, :], in0=localT_t[:, :], scalar1=small_t[:, 55:56],
                scalar2=None, op0=ALU.add))

            p.wait(Y, m_V)
            for ob in range(8):
                p.emit(Y, lambda e, hp=hp, ob=ob: e.dma_start(
                    out=bass.AP(xTg_t, ob * 128, [[1024, 128], [1, 128]]),
                    in_=bass.AP(proj_w.tensor, ob * 128 * D + hp * 128, [[D, 128], [1, 128]])),
                    inc=m_dma, inc_n=16)
            p.wait(T, m_dma)
            p.wait(T, m_V)
            for grp in range(2):
                psl = ps_a if grp == 0 else ps_b
                for q in range(4):
                    ob = grp * 4 + q
                    p.emit(T, lambda e, ob=ob, q=q, psl=psl: e.transpose(
                        psl[0:128, q * 128:(q + 1) * 128],
                        xTg_t[:, ob * 128:(ob + 1) * 128], ident_t[:, :]),
                        inc=(m_T if q == 3 else None))
                p.wait(V, m_T)
                p.emit(V, lambda e, grp=grp, psl=psl: e.tensor_copy(
                    out=wt3_t[:, (grp * 4) * 128:(grp * 4 + 4) * 128], in_=psl[:, 0:512]), inc=m_V)

            p.wait(Sc, m_V)
            p.wait(Sc, m_dma)
            kern_mTA = [0, 0]
            phase_steps = [[]]
            for ic in range(4):
                sig_wait = {0: {cg: p.val(m_SE) for cg in range(4)}}
                pending = None

                def consume(pend):
                    base_u, g, q4, v_list, sub_cnt, ps_cnt = pend
                    and_cnt = produce_ands(v_list, sub_cnt)
                    p.wait(T, m_PS, ps_cnt)
                    p.wait(T, m_PV, and_cnt)
                    tc_vals = []
                    for r in range(4):
                        t = q4 * 4 + r
                        u = base_u + r
                        slot = u % NSLOT
                        cg = t % 4
                        s_ = t // 4
                        if q4 == 0:
                            p.wait(T, m_SE, sig_wait[g][cg])
                        p.emit(T, lambda e, slot=slot, cg=cg, s_=s_, q4=q4: e.matmul(
                            ps_sig[32 * cg:32 * cg + 32, 512 * cg:512 * cg + 512],
                            masks_t[:, 32 * s_:32 * s_ + 32],
                            dbuf_t[:, slot * 512:slot * 512 + 512],
                            start=(q4 == 0), stop=(q4 == 15),
                            tile_position=(0, 32 * cg)), inc=m_TC)
                        tc_vals.append(p.val(m_TC))
                    if q4 == 15:
                        g_ = g
                        kslot = g_ % 2
                        p.wait(Sc, m_TA, kern_mTA[kslot])
                        nxt = {}
                        for cg in range(4):
                            p.wait(Sc, m_TC, tc_vals[cg])
                            p.emit(Sc, lambda e, cg=cg, kslot=kslot: e.activation(
                                out=kern_t[32 * cg:32 * cg + 32, kslot * 512:kslot * 512 + 512],
                                in_=ps_sig[32 * cg:32 * cg + 32, 512 * cg:512 * cg + 512],
                                func=ACT.Exp, bias=0.0, scale=LAMBDA_SCALE), inc=m_SE)
                            nxt[cg] = p.val(m_SE)
                        sig_wait[g_ + 1] = nxt
                        p.wait(T, m_SE)
                        if g_ == 0:
                            p.wait(T, m_V)
                            p.wait(T, m_gdma)
                        for h in range(2):
                            vaug = vaug0_t if h == 0 else vaug1_t
                            p.emit(T, lambda e, h=h, g_=g_, vaug=vaug, kslot=kslot: e.matmul(
                                ps_ga[0:65, h * 512:(h + 1) * 512],
                                vaug[:, g_ * 65:g_ * 65 + 65],
                                kern_t[:, kslot * 512:kslot * 512 + 512],
                                start=(g_ == 0), stop=(g_ == 31)),
                                inc=(m_TA if h == 1 else None))
                        kern_mTA[kslot] = p.val(m_TA)

                for g in range(32):
                    for q4 in range(16):
                        jgs = []
                        for r in range(4):
                            t = q4 * 4 + r
                            jprime = 16 * (t % 4) + (t // 4)
                            jgs.append(g * 64 + jprime)
                        v_list, sub_cnt, ps_cnt = produce_first(unit[0], ic, jgs)
                        cur = (unit[0], g, q4, v_list, sub_cnt, ps_cnt)
                        unit[0] += 4
                        if pending is not None:
                            consume(pending)
                            for _ in range(3):
                                if phase_steps[0]:
                                    phase_steps[0].pop(0)()
                        pending = cur
                consume(pending)
                pending = None
                def make_phase_steps(ic_, hp_):
                    steps = []
                    def s_wait_ga(e=None):
                        p.wait(V, m_TA)
                    steps.append(s_wait_ga)
                    for h in range(2):
                        def s_norm1(h=h):
                            vf(lambda e: e.tensor_scalar(
                                out=stage_t[0:1, 0:512], in0=ps_ga[64:65, h * 512:(h + 1) * 512],
                                scalar1=ATT_EPS, scalar2=None, op0=ALU.add))
                            vf(lambda e: e.reciprocal(
                                out=stage_t[0:1, 512:1024], in_=stage_t[0:1, 0:512]))
                        def s_bcast(h=h):
                            p.wait(T, m_V)
                            p.emit(T, lambda e: e.matmul(
                                ps_b[0:64, 0:512],
                                onesrow_t[0:1, 0:64], stage_t[0:1, 512:1024],
                                start=True, stop=True), inc=m_T)
                            p.wait(V, m_T)
                            vf(lambda e: e.tensor_copy(
                                out=recB_t[0:64, 0:512], in_=ps_b[0:64, 0:512]))
                        def s_y(h=h, ic_=ic_):
                            lsl = ic_ * 512
                            vf(lambda e, h=h: e.tensor_tensor(
                                out=yT_t[h * 64:h * 64 + 64, 0:512],
                                in0=ps_ga[0:64, h * 512:(h + 1) * 512],
                                in1=recB_t[0:64, 0:512], op=ALU.mult))
                            vf(lambda e, h=h, lsl=lsl: e.tensor_tensor(
                                out=yT_t[h * 64:h * 64 + 64, 0:512],
                                in0=yT_t[h * 64:h * 64 + 64, 0:512],
                                in1=localT_t[h * 64:h * 64 + 64, lsl:lsl + 512], op=ALU.add))
                        steps += [s_norm1, s_bcast, s_y]
                    for ob in range(8):
                        def s_proj(ob=ob, ic_=ic_):
                            p.wait(T, m_V)
                            p.emit(T, lambda e, ob=ob: e.matmul(
                                ps_a[:, :], wt3_t[:, ob * 128:(ob + 1) * 128],
                                yT_t[:, 0:512], start=True, stop=True), inc=m_T)
                            p.wait(V, m_T)
                            p.emit(V, lambda e, ob=ob, ic_=ic_: e.tensor_tensor(
                                out=R_t[:, ob * S + ic_ * 512: ob * S + (ic_ + 1) * 512],
                                in0=R_t[:, ob * S + ic_ * 512: ob * S + (ic_ + 1) * 512],
                                in1=ps_a[:, 0:512], op=ALU.add), inc=m_V)
                        steps.append(s_proj)
                    return steps

                phase_steps[0] = make_phase_steps(ic, hp)
                if ic == 3:
                    while phase_steps[0]:
                        phase_steps[0].pop(0)()


        barrier()
        prod_t = dbuf_t
        gu_t = kern_t
        selwB_t = recB_t
        ps_g = lambda: ps_sig[:, 0:512]
        ps_u = lambda: ps_sig[:, 512:1024]
        ps_d = lambda: ps_sig[:, 1024:1536]

        FFN_STAGE = (sqt_t, xTg_t)

        def slab_dma(buf, src_dram, row0, ncols):
            p.wait(Y, m_T)
            p.emit(Y, lambda e, buf=buf, src_dram=src_dram, row0=row0, ncols=ncols: e.dma_start(
                out=buf[:, 0:ncols], in_=src_dram[row0:row0 + 128, 0:ncols]),
                inc=m_dma, inc_n=16)
            dv = p.val(m_dma)
            p.wait(Y, m_dma, dv)
            p._qbound[id(m_dma)] = dv
            return dv

        def slab_transpose(buf, dst_off, nblk, dma_val):
            p.wait(T, m_dma, dma_val)
            p.wait(T, m_V)
            for grp in range((nblk + 3) // 4):
                psl = ps_a if grp % 2 == 0 else ps_b
                n_in_grp = min(4, nblk - grp * 4)
                for q in range(n_in_grp):
                    dc = grp * 4 + q
                    p.emit(T, lambda e, dc=dc, q=q, psl=psl, buf=buf: e.transpose(
                        psl[0:128, q * 128:(q + 1) * 128],
                        buf[:, dc * 128:(dc + 1) * 128], ident_t[:, :]),
                        inc=(m_T if q == n_in_grp - 1 else None))
                p.wait(V, m_T)
                p.emit(V, lambda e, grp=grp, n_in_grp=n_in_grp, psl=psl, dst_off=dst_off: e.tensor_copy(
                    out=wt3_t[:, dst_off + grp * 512:dst_off + grp * 512 + n_in_grp * 128],
                    in_=psl[:, 0:n_in_grp * 128]), inc=m_V)
                p.wait(T, m_V)

        def ffn_block(gw_d, uw_d, dw_d, gw_row0, dw_row0, n_sib, dw_ncols, selw_e):
            for tq in range(4):
                p.wait(Y, m_T)
                for dc in range(8):
                    p.emit(Y, lambda e, tq=tq, dc=dc: e.dma_start(
                        out=bass.AP(xnt2_t, dc * 512, [[4096, 128], [1, 512]]),
                        in_=bass.AP(xnT_hbm.tensor, dc * 128 * S + tq * 512, [[S, 128], [1, 512]])),
                        inc=m_dma, inc_n=16)
                if selw_e is not None:
                    p.wait(T, m_dma)
                    p.wait(T, m_V)
                    p.wait(T, m_gdma)
                    p.emit(T, lambda e, tq=tq: e.matmul(
                        ps_sig[:, 1536:2048], onesrow_t[0:1, :],
                        localT_t[0:1, tq * 512:(tq + 1) * 512],
                        start=True, stop=True), inc=m_T)
                    p.wait(V, m_T)
                    p.emit(V, lambda e: e.tensor_copy(
                        out=selwB_t[:, 512:1024], in_=ps_sig[:, 1536:2048]), inc=m_V)
                sched = []
                for sib in range(n_sib):
                    sched.append(("g", gw_d, gw_row0 + sib * 128, D, 8, 0, sib))
                    sched.append(("u", uw_d, gw_row0 + sib * 128, D, 8, 1024, sib))
                for cb in range(8):
                    sched.append(("d", dw_d, dw_row0 + cb * 128, dw_ncols, n_sib, 2048, cb))
                dv = slab_dma(FFN_STAGE[0], sched[0][1], sched[0][2], sched[0][3])
                for k, (kind, dram_, row0, ncols, nblk, dst, idx) in enumerate(sched):
                    dv_cur = dv
                    if k + 1 < len(sched):
                        nx = sched[k + 1]
                        dv = slab_dma(FFN_STAGE[(k + 1) % 2], nx[1], nx[2], nx[3])
                    slab_transpose(FFN_STAGE[k % 2], dst, nblk, dv_cur)
                    if kind == "g":
                        p.wait(T, m_V)
                        for dc in range(8):
                            p.emit(T, lambda e, dc=dc: e.matmul(
                                ps_g(), wt3_t[:, dc * 128:(dc + 1) * 128],
                                xnt2_t[:, dc * 512:(dc + 1) * 512],
                                start=(dc == 0), stop=(dc == 7)), inc=(m_T if dc == 7 else None))
                        p.wait(Sc, m_T)
                        p.wait(Sc, m_V)
                        p.emit(Sc, lambda e: e.activation(
                            out=stage_t[:, 0:512], in_=ps_g(), func=ACT.Sigmoid), inc=m_S)
                    elif kind == "u":
                        p.wait(T, m_V)
                        for dc in range(8):
                            p.emit(T, lambda e, dc=dc: e.matmul(
                                ps_u(), wt3_t[:, 1024 + dc * 128:1024 + (dc + 1) * 128],
                                xnt2_t[:, dc * 512:(dc + 1) * 512],
                                start=(dc == 0), stop=(dc == 7)), inc=(m_T if dc == 7 else None))
                        p.wait(V, m_S)
                        p.wait(V, m_T)
                        vf(lambda e: e.tensor_tensor(
                            out=gu_t[:, 0:512], in0=ps_g(), in1=stage_t[:, 0:512], op=ALU.mult))
                        vf(lambda e, sib=idx: e.tensor_tensor(
                            out=prod_t[:, sib * 512:(sib + 1) * 512], in0=gu_t[:, 0:512],
                            in1=ps_u(), op=ALU.mult))
                    else:
                        p.wait(T, m_V)
                        for sic in range(n_sib):
                            p.emit(T, lambda e, sic=sic: e.matmul(
                                ps_d(), wt3_t[:, 2048 + sic * 128:2048 + (sic + 1) * 128],
                                prod_t[:, sic * 512:(sic + 1) * 512],
                                start=(sic == 0), stop=(sic == n_sib - 1)),
                                inc=(m_T if sic == n_sib - 1 else None))
                        p.wait(V, m_T)
                        cb = idx
                        if selw_e is None:
                            p.emit(V, lambda e, cb=cb, tq=tq: e.tensor_tensor(
                                out=R_t[:, cb * S + tq * 512: cb * S + (tq + 1) * 512],
                                in0=R_t[:, cb * S + tq * 512: cb * S + (tq + 1) * 512],
                                in1=ps_d(), op=ALU.add), inc=m_V)
                        else:
                            vf(lambda e: e.tensor_tensor(
                                out=stage_t[:, 512:1024], in0=ps_d(), in1=selwB_t[:, 512:1024],
                                op=ALU.mult))
                            p.emit(V, lambda e, cb=cb, tq=tq: e.tensor_tensor(
                                out=R_t[:, cb * S + tq * 512: cb * S + (tq + 1) * 512],
                                in0=R_t[:, cb * S + tq * 512: cb * S + (tq + 1) * 512],
                                in1=stage_t[:, 512:1024], op=ALU.add), inc=m_V)

        ffn_block(sh_gw, sh_uw, sh_dw, 0, 0, 8, SI, None)
        for ex in range(E):
            p.wait(G, m_T)
            p.wait(G, m_V)
            for tb16 in range(16):
                p.emit(G, lambda e, ex=ex, tb16=tb16: e.dma_start(
                    out=localT_t[0:1, tb16 * 128:(tb16 + 1) * 128],
                    in_=selwT_t[E * tb16 + ex:E * tb16 + ex + 1, 0:128]), inc=m_gdma, inc_n=16)
            ffn_block(exp_gw, exp_uw, exp_dw, ex * I_EXP, ex * D, 4, I_EXP, ex)

        p.wait(T, m_V)
        for tb in range(16):
            p.wait(V, m_dma, p.val(m_dma))
            for half in range(2):
                psl = ps_a if half == 0 else ps_b
                p.wait(T, m_V)
                for q in range(4):
                    cb = half * 4 + q
                    p.emit(T, lambda e, cb=cb, q=q, tb=tb, psl=psl: e.transpose(
                        psl[0:128, q * 128:(q + 1) * 128],
                        R_t[:, cb * S + tb * 128: cb * S + (tb + 1) * 128],
                        ident_t[:, :]), inc=(m_T if q == 3 else None))
                p.wait(V, m_T)
                p.emit(V, lambda e, half=half, psl=psl: e.tensor_copy(
                    out=stage_t[:, half * 512:(half + 1) * 512], in_=psl[:, 0:512]), inc=m_V)
            p.wait(Y, m_V)
            p.emit(Y, lambda e, tb=tb: e.dma_start(
                out=out[tb * 128:(tb + 1) * 128, :], in_=stage_t[:, :]), inc=m_dma, inc_n=16)
        p.wait(G, m_dma)
        p.wait(G, m_gdma)

        p.finalize(block)

    return nc, dbg


def _inputs_common(L0_attn_qkv_w, L0_attn_qkv_b, L0_attn_proj_w, L0_attn_proj_b,
                   L0_attn_dwc_w, L0_attn_dwc_b, L0_ffn_gate_w, L0_ffn_lb_bias,
                   L0_ffn_expert_gate_w, L0_ffn_expert_up_w, L0_ffn_expert_down_w,
                   L0_ffn_shared_gate_w, L0_ffn_shared_up_w, L0_ffn_shared_down_w):
    f = np.float32
    return {
        "qkv_w": np.ascontiguousarray(L0_attn_qkv_w, dtype=f),
        "qkv_b": np.ascontiguousarray(np.asarray(L0_attn_qkv_b, dtype=f).reshape(3 * D, 1)),
        "proj_w": np.ascontiguousarray(L0_attn_proj_w, dtype=f),
        "proj_b": np.ascontiguousarray(np.asarray(L0_attn_proj_b, dtype=f).reshape(D, 1)),
        "dwc_w": np.ascontiguousarray(np.asarray(L0_attn_dwc_w, dtype=f).reshape(D, 3)),
        "dwc_b": np.ascontiguousarray(np.asarray(L0_attn_dwc_b, dtype=f).reshape(D, 1)),
        "gate_w": np.ascontiguousarray(L0_ffn_gate_w, dtype=f),
        "lb_bias": np.ascontiguousarray(np.asarray(L0_ffn_lb_bias, dtype=f).reshape(E, 1)),
        "exp_gw": np.ascontiguousarray(np.asarray(L0_ffn_expert_gate_w, dtype=f).reshape(E * I_EXP, D)),
        "exp_uw": np.ascontiguousarray(np.asarray(L0_ffn_expert_up_w, dtype=f).reshape(E * I_EXP, D)),
        "exp_dw": np.ascontiguousarray(np.asarray(L0_ffn_expert_down_w, dtype=f).reshape(E * D, I_EXP)),
        "sh_gw": np.ascontiguousarray(L0_ffn_shared_gate_w, dtype=f),
        "sh_uw": np.ascontiguousarray(L0_ffn_shared_up_w, dtype=f),
        "sh_dw": np.ascontiguousarray(L0_ffn_shared_down_w, dtype=f),
    }


def kernel(L0_attn_x, **kw):
    x = np.ascontiguousarray(np.asarray(L0_attn_x, dtype=np.float32))
    common = _inputs_common(**kw)
    common["xf"] = np.ascontiguousarray(x.reshape(N_ALL, D))
    nc, _ = build()
    in_maps = [dict(common, xo=np.ascontiguousarray(x[b])) for b in range(B)]
    res = run_bass_kernel_spmd(nc, in_maps, core_ids=list(range(B)))
    return np.stack([res.results[b]["out"] for b in range(B)], axis=0)


# version 11 (final)
# speedup vs baseline: 1.8346x; 1.0003x over previous
import contextlib

import numpy as np
import concourse.bass as bass
import concourse.mybir as mybir
from concourse.bass_utils import run_bass_kernel_spmd

F32 = mybir.dt.float32
I32 = mybir.dt.int32
ALU = mybir.AluOpType
ACT = mybir.ActivationFunctionType
AXL = mybir.AxisListType

B, S, D = 8, 2048, 1024
H, HD = 16, 64
HP = 8
E, I_EXP, SI = 8, 512, 1024
N_ALL = B * S
LAMBDA_SCALE = -4.0 / 64.0
LN_EPS = 1e-5
ATT_EPS = 1e-6
GATE_EPS = 1e-8
ABSMASK = 0x7FFFFFFF

ENGINES = ("sync", "vector", "scalar", "tensor", "gpsimd")


class Prog:

    def __init__(self):
        self.steps = {e: [] for e in ENGINES}
        self.ctr = {}

    def val(self, sem):
        return self.ctr.get(id(sem), 0)

    def bump(self, sem, n):
        self.ctr[id(sem)] = self.ctr.get(id(sem), 0) + n
        return self.ctr[id(sem)]

    def set_dma_queues(self, mapping, boundary_sem):
        self._dmaq = {id(s): q for s, q in mapping.items()}
        self._qbound = {id(s): 0 for s in mapping}

    def wait(self, eng, sem, val=None):
        v = self.val(sem) if val is None else val
        if v <= 0:
            return
        dq = getattr(self, "_dmaq", None)
        if dq is not None and id(sem) in dq and eng != dq[id(sem)]:
            if self._qbound[id(sem)] < v:
                q = dq[id(sem)]
                self.steps[q].append(lambda e, sem=sem, v=v: e.wait_ge(sem, v))
                self._qbound[id(sem)] = v
        self.steps[eng].append(lambda e, sem=sem, v=v: e.wait_ge(sem, v))

    def emit(self, eng, fn, inc=None, inc_n=1):
        if inc is not None:
            self.bump(inc, inc_n)
        self.steps[eng].append(
            lambda e, fn=fn, inc=inc, inc_n=inc_n:
            (fn(e).then_inc(inc, inc_n) if inc is not None else fn(e)))

    def finalize(self, block):
        for name in ENGINES:
            steps = self.steps[name]

            def mk(steps):
                def _(e):
                    for st in steps:
                        st(e)
                return _
            getattr(block, name)(mk(steps))


def build(debug=()):
    nc = bass.Bass("TRN2", target_bir_lowering=False)
    dt = F32

    xo = nc.dram_tensor("xo", [S, D], dt, kind="ExternalInput").ap()
    xf = nc.dram_tensor("xf", [N_ALL, D], dt, kind="ExternalInput").ap()
    qkv_w = nc.dram_tensor("qkv_w", [3 * D, D], dt, kind="ExternalInput").ap()
    qkv_b = nc.dram_tensor("qkv_b", [3 * D, 1], dt, kind="ExternalInput").ap()
    proj_w = nc.dram_tensor("proj_w", [D, D], dt, kind="ExternalInput").ap()
    proj_b = nc.dram_tensor("proj_b", [D, 1], dt, kind="ExternalInput").ap()
    dwc_w = nc.dram_tensor("dwc_w", [D, 3], dt, kind="ExternalInput").ap()
    dwc_b = nc.dram_tensor("dwc_b", [D, 1], dt, kind="ExternalInput").ap()
    gate_w = nc.dram_tensor("gate_w", [E, D], dt, kind="ExternalInput").ap()
    lb_bias = nc.dram_tensor("lb_bias", [E, 1], dt, kind="ExternalInput").ap()
    exp_gw = nc.dram_tensor("exp_gw", [E * I_EXP, D], dt, kind="ExternalInput").ap()
    exp_uw = nc.dram_tensor("exp_uw", [E * I_EXP, D], dt, kind="ExternalInput").ap()
    exp_dw = nc.dram_tensor("exp_dw", [E * D, I_EXP], dt, kind="ExternalInput").ap()
    sh_gw = nc.dram_tensor("sh_gw", [SI, D], dt, kind="ExternalInput").ap()
    sh_uw = nc.dram_tensor("sh_uw", [SI, D], dt, kind="ExternalInput").ap()
    sh_dw = nc.dram_tensor("sh_dw", [D, SI], dt, kind="ExternalInput").ap()
    out = nc.dram_tensor("out", [S, D], dt, kind="ExternalOutput").ap()
    xnT_hbm = nc.dram_tensor("xnT_scr", [D, S], dt).ap()

    dbg = {}
    for name, shape in debug:
        dbg[name] = nc.dram_tensor(name, list(shape), dt, kind="ExternalOutput").ap()

    p = Prog()

    with contextlib.ExitStack() as _st:
        def sb(name, shape):
            return _st.enter_context(nc.sbuf_tensor(name, shape, dt))

        def psum(name, shape):
            return _st.enter_context(nc.psum_tensor(name, shape, dt))

        def sem(name):
            return _st.enter_context(nc.semaphore(name))

        R_t = sb("R", [128, 8 * S])
        ident_t = sb("ident", [128, 128])
        onesrow_t = sb("onesrow", [1, 128])
        onescol_t = sb("onescol", [128, 1])
        masks_t = sb("masks", [128, 16 * 32])
        selwT_t = sb("selwT", [128, 128])
        selown_t = sb("selown", [128, 16 * E])
        wmeanB_t = sb("wmeanB", [128, E])
        negGB_t = sb("negGB", [128, E])
        lbB_t = sb("lbB", [128, E])
        gwT_t = sb("gwT", [128, 8 * E])
        wacc_t = sb("wacc", [128, 2 * E])
        small_t = sb("small", [128, 64])
        projb_t = sb("projb", [128, 8])
        xt0_t = sb("xt0", [128, D])
        xt1_t = sb("xt1", [128, D])
        sqt_t = sb("sqt", [128, D])
        xnt_t = sb("xnt", [128, D])
        xTg_t = sb("xTg", [128, 8 * 128])
        gs_t = sb("gs", [128, 24 * E])
        stat_t = sb("stat", [128, 8])
        stage_t = sb("stage", [128, D])
        wt3_t = sb("wt3", [128, 3 * 8 * 128])
        xnt2_t = sb("xnt2", [128, 8 * 512])
        dbuf_t = sb("dbuf", [128, 14 * 512])
        nqT2_t = sb("nqT2", [128, S])
        kT2_t = sb("kT2", [128, S])
        vT2_t = sb("vT2", [128, S])
        vaug0_t = sb("vaug0", [128, 32 * 65])
        vaug1_t = sb("vaug1", [128, 32 * 65])
        vtok_t = sb("vtok", [128, 16 * 128])
        localT_t = sb("localT", [128, S])
        ps_a = psum("ps_a", [128, 512])
        ps_b = psum("ps_b", [128, 512])
        ps_sig = psum("ps_sig", [128, 2048])
        ps_ga = psum("ps_ga", [66, 1024])
        m_dma = sem("m_dma")
        m_gdma = sem("m_gdma")
        m_T = sem("m_T")
        m_TA = sem("m_TA")
        m_TC = sem("m_TC")
        m_V = sem("m_V")
        m_PV = sem("m_PV")
        m_S = sem("m_S")
        m_PS = sem("m_PS")
        m_SE = sem("m_SE")
        m_G = sem("m_G")
        block = _st.enter_context(nc.Block())

        V, Sc, T, G, Y = "vector", "scalar", "tensor", "gpsimd", "sync"
        p.set_dma_queues({m_dma: Y, m_gdma: G}, m_G)

        def vf(fn):
            p.emit(V, fn, inc=m_V)
            p.wait(V, m_V)

        def sf(fn):
            p.emit(Sc, fn, inc=m_S)
            p.wait(Sc, m_S)

        p.emit(V, lambda e: e.memset(ident_t[:], 0.0))
        p.emit(V, lambda e: e.memset(onesrow_t[:], 1.0))
        p.emit(V, lambda e: e.memset(onescol_t[:], 1.0))
        p.emit(V, lambda e: e.memset(masks_t[:], 0.0))
        p.emit(V, lambda e: e.memset(wacc_t[:], 0.0), inc=m_V)
        p.wait(G, m_V)
        p.emit(G, lambda e: e.affine_select(
            out=ident_t[:], in_=ident_t[:], compare_op=ALU.not_equal,
            fill=1.0, base=0, pattern=[[-1, 128]], channel_multiplier=1))
        for s_ in range(16):
            p.emit(G, lambda e, s_=s_: e.memset(
                masks_t[0:64, 32 * s_ + s_: 32 * s_ + s_ + 1], 1.0))
            p.emit(G, lambda e, s_=s_: e.memset(
                masks_t[64:128, 32 * s_ + 16 + s_: 32 * s_ + 16 + s_ + 1], 1.0),
                inc=(m_G if s_ == 15 else None))

        p.emit(Y, lambda e: e.dma_start(out=gs_t[0:E, 0:1], in_=lb_bias[:, :]), inc=m_dma, inc_n=16)
        for cb in range(8):
            p.emit(Y, lambda e, cb=cb: e.dma_start(
                out=projb_t[:, cb:cb + 1],
                in_=proj_b[cb * 128:(cb + 1) * 128, 0:1]), inc=m_dma, inc_n=16)
        p.emit(Y, lambda e: e.dma_start(out=xTg_t[0:E, 0:D], in_=gate_w[:, :]), inc=m_dma, inc_n=16)
        p.wait(T, m_dma)
        p.wait(T, m_G)
        for dc in range(8):
            p.emit(T, lambda e, dc=dc: e.transpose(
                ps_a[0:128, dc * E:(dc + 1) * E], xTg_t[0:E, dc * 128:(dc + 1) * 128],
                ident_t[0:E, 0:E]), inc=(m_T if dc == 7 else None))
        p.wait(V, m_T)
        p.emit(V, lambda e: e.tensor_copy(out=gwT_t[:, :], in_=ps_a[:, 0:8 * E]))
        vf(lambda e: e.reduce_sum(out=small_t[0:E, 0:1], in_=xTg_t[0:E, 0:D], axis=AXL.X))
        vf(lambda e: e.tensor_scalar(
            out=small_t[0:E, 1:2], in0=small_t[0:E, 0:1], scalar1=-1.0, scalar2=None,
            op0=ALU.mult))
        p.wait(T, m_V)
        p.emit(T, lambda e: e.transpose(ps_b[0:1, 0:E], small_t[0:E, 1:2], ident_t[0:E, 0:E]))
        p.emit(T, lambda e: e.transpose(ps_b[0:1, E:2 * E], gs_t[0:E, 0:1],
                                        ident_t[0:E, 0:E]), inc=m_T)
        p.wait(V, m_T)
        p.emit(V, lambda e: e.tensor_copy(out=small_t[0:1, 8:8 + 2 * E], in_=ps_b[0:1, 0:2 * E]), inc=m_V)
        p.wait(T, m_V)
        p.emit(T, lambda e: e.matmul(
            ps_a[0:128, 0:2 * E], onesrow_t[0:1, :], small_t[0:1, 8:8 + 2 * E],
            start=True, stop=True), inc=m_T)
        p.wait(V, m_T)
        p.emit(V, lambda e: e.tensor_copy(out=negGB_t[:, :], in_=ps_a[:, 0:E]))
        p.emit(V, lambda e: e.tensor_copy(out=lbB_t[:, :], in_=ps_a[:, E:2 * E]), inc=m_V)

        def ln_stats(xt):
            vf(lambda e: e.reduce_sum(out=stat_t[:, 0:1], in_=xt[:, :], axis=AXL.X))
            vf(lambda e: e.tensor_scalar(
                out=stat_t[:, 2:3], in0=stat_t[:, 0:1], scalar1=1.0 / D, scalar2=None,
                op0=ALU.mult))
            p.wait(Sc, m_V)
            p.emit(Sc, lambda e: e.activation(out=sqt_t[:, :], in_=xt[:, :], func=ACT.Square), inc=m_S)
            p.wait(V, m_S)
            vf(lambda e: e.reduce_sum(out=stat_t[:, 1:2], in_=sqt_t[:, :], axis=AXL.X))
            vf(lambda e: e.tensor_scalar(
                out=stat_t[:, 3:4], in0=stat_t[:, 1:2], scalar1=1.0 / D, scalar2=None, op0=ALU.mult))
            vf(lambda e: e.tensor_tensor(
                out=stat_t[:, 4:5], in0=stat_t[:, 2:3], in1=stat_t[:, 2:3], op=ALU.mult))
            vf(lambda e: e.tensor_tensor(
                out=stat_t[:, 5:6], in0=stat_t[:, 3:4], in1=stat_t[:, 4:5], op=ALU.subtract))
            vf(lambda e: e.tensor_scalar(
                out=stat_t[:, 5:6], in0=stat_t[:, 5:6], scalar1=LN_EPS, scalar2=None,
                op0=ALU.add))
            p.wait(Sc, m_V)
            p.emit(Sc, lambda e: e.activation(out=stat_t[:, 6:7], in_=stat_t[:, 5:6], func=ACT.Sqrt), inc=m_S)
            p.wait(V, m_S)
            vf(lambda e: e.reciprocal(out=stat_t[:, 7:8], in_=stat_t[:, 6:7]))

        def transpose_tile_to(xt, evac):
            p.wait(T, m_V)
            for half in range(2):
                psl = ps_a if half == 0 else ps_b
                for q in range(4):
                    dc = half * 4 + q
                    p.emit(T, lambda e, dc=dc, q=q, psl=psl, xt=xt: e.transpose(
                        psl[0:128, q * 128:(q + 1) * 128], xt[:, dc * 128:(dc + 1) * 128],
                        ident_t[:, :]), inc=(m_T if q == 3 else None))
                p.wait(V, m_T)
                evac(half, psl)

        def gate_tile(xt, sel_store=None):
            def evac(half, psl):
                p.emit(V, lambda e, half=half, psl=psl: e.tensor_copy(
                    out=xTg_t[:, half * 512:(half + 1) * 512], in_=psl[:, 0:512]), inc=m_V)
            transpose_tile_to(xt, evac)
            p.wait(T, m_V)
            for dc in range(8):
                p.emit(T, lambda e, dc=dc: e.matmul(
                    ps_a[0:E, 0:128], gwT_t[:, dc * E:(dc + 1) * E],
                    xTg_t[:, dc * 128:(dc + 1) * 128],
                    start=(dc == 0), stop=(dc == 7)), inc=(m_T if dc == 7 else None))
            p.wait(V, m_T)
            p.emit(V, lambda e: e.tensor_copy(out=gs_t[0:E, 8:8 + 128], in_=ps_a[0:E, 0:128]), inc=m_V)
            p.wait(T, m_V)
            p.emit(T, lambda e: e.transpose(
                ps_b[0:128, 0:E], gs_t[0:E, 8:8 + 128], ident_t[0:E, 0:E]), inc=m_T)
            p.wait(V, m_T)
            g0 = 16 * E
            rtok = gs_t[:, g0:g0 + E]
            logit = gs_t[:, g0 + E:g0 + 2 * E]
            scores = gs_t[:, g0 + 2 * E:g0 + 3 * E]
            iseq = gs_t[:, g0 + 3 * E:g0 + 4 * E]
            masked = gs_t[:, g0 + 4 * E:g0 + 5 * E]
            sel = gs_t[:, g0 + 5 * E:g0 + 6 * E]
            wvals = gs_t[:, g0 + 6 * E:g0 + 7 * E]
            vf(lambda e: e.tensor_copy(out=rtok, in_=ps_b[:, 0:E]))
            vf(lambda e: e.scalar_tensor_tensor(
                out=logit, in0=negGB_t[:, :], scalar=stat_t[:, 2:3], in1=rtok,
                op0=ALU.mult, op1=ALU.add))
            vf(lambda e: e.tensor_scalar(
                out=logit, in0=logit, scalar1=stat_t[:, 7:8], scalar2=None, op0=ALU.mult))
            vf(lambda e: e.tensor_tensor(
                out=logit, in0=logit, in1=lbB_t[:, :], op=ALU.add))
            p.wait(Sc, m_V)
            p.emit(Sc, lambda e: e.activation(out=scores, in_=logit, func=ACT.Sigmoid), inc=m_S)
            p.wait(V, m_S)
            vf(lambda e: e.reduce_max(out=stat_t[:, 0:1], in_=scores, axis=AXL.X))
            vf(lambda e: e.tensor_scalar(
                out=iseq, in0=scores, scalar1=stat_t[:, 0:1], scalar2=None, op0=ALU.is_ge))
            vf(lambda e: e.scalar_tensor_tensor(
                out=masked, in0=iseq, scalar=-1e30, in1=scores, op0=ALU.mult, op1=ALU.add))
            vf(lambda e: e.reduce_max(out=stat_t[:, 1:2], in_=masked, axis=AXL.X))
            vf(lambda e: e.tensor_scalar(
                out=sel, in0=scores, scalar1=stat_t[:, 1:2], scalar2=None, op0=ALU.is_ge))
            vf(lambda e: e.tensor_tensor(
                out=stat_t[:, 2:3], in0=stat_t[:, 0:1], in1=stat_t[:, 1:2], op=ALU.add))
            vf(lambda e: e.tensor_scalar(
                out=stat_t[:, 2:3], in0=stat_t[:, 2:3], scalar1=GATE_EPS, scalar2=None, op0=ALU.add))
            vf(lambda e: e.reciprocal(out=stat_t[:, 3:4], in_=stat_t[:, 2:3]))
            vf(lambda e: e.scalar_tensor_tensor(
                out=wvals, in0=scores, scalar=stat_t[:, 3:4], in1=sel,
                op0=ALU.mult, op1=ALU.mult))
            vf(lambda e: e.tensor_tensor(
                out=wacc_t[:, 0:E], in0=wacc_t[:, 0:E], in1=wvals, op=ALU.add))
            vf(lambda e: e.tensor_tensor(
                out=wacc_t[:, E:2 * E], in0=wacc_t[:, E:2 * E], in1=sel, op=ALU.add))
            if sel_store is not None:
                vf(lambda e, sel_store=sel_store: e.tensor_copy(out=sel_store, in_=sel))

        for tg in range(N_ALL // 128):
            xt = xt0_t if tg % 2 == 0 else xt1_t
            p.wait(Y, m_V)
            p.emit(Y, lambda e, tg=tg, xt=xt: e.dma_start(
                out=xt[:, :], in_=xf[tg * 128:(tg + 1) * 128, :]), inc=m_dma, inc_n=16)
            p.wait(V, m_dma)
            p.wait(Sc, m_dma)
            p.wait(T, m_dma)
            ln_stats(xt)
            gate_tile(xt)

        p.wait(T, m_V)
        p.emit(T, lambda e: e.matmul(
            ps_a[0:2 * E, 0:1], wacc_t[:, 0:2 * E], onescol_t[:, :],
            start=True, stop=True), inc=m_T)
        p.wait(V, m_T)
        p.emit(V, lambda e: e.tensor_copy(out=small_t[0:2 * E, 2:3], in_=ps_a[0:2 * E, 0:1]), inc=m_V)
        p.wait(T, m_V)
        p.emit(T, lambda e: e.transpose(
            ps_b[0:1, 0:2 * E], small_t[0:2 * E, 2:3], ident_t[0:2 * E, 0:2 * E]), inc=m_T)
        p.wait(V, m_T)
        vf(lambda e: e.tensor_copy(out=small_t[0:1, 8:8 + 2 * E], in_=ps_b[0:1, 0:2 * E]))
        vf(lambda e: e.tensor_scalar(
            out=small_t[0:1, 24:24 + E], in0=small_t[0:1, 8 + E:8 + 2 * E],
            scalar1=1.0, scalar2=None, op0=ALU.max))
        vf(lambda e: e.reciprocal(out=small_t[0:1, 32:32 + E], in_=small_t[0:1, 24:24 + E]))
        vf(lambda e: e.tensor_tensor(
            out=small_t[0:1, 40:40 + E], in0=small_t[0:1, 8:8 + E],
            in1=small_t[0:1, 32:32 + E], op=ALU.mult))
        p.wait(T, m_V)
        p.emit(T, lambda e: e.matmul(
            ps_a[0:128, 0:E], onesrow_t[0:1, :], small_t[0:1, 40:40 + E],
            start=True, stop=True), inc=m_T)
        p.wait(V, m_T)
        p.emit(V, lambda e: e.tensor_copy(out=wmeanB_t[:, :], in_=ps_a[:, 0:E]), inc=m_V)

        for tb in range(16):
            xt = xt0_t if tb % 2 == 0 else xt1_t
            p.wait(Y, m_V)
            p.emit(Y, lambda e, tb=tb, xt=xt: e.dma_start(
                out=xt[:, :], in_=xo[tb * 128:(tb + 1) * 128, :]), inc=m_dma, inc_n=16)
            p.wait(V, m_dma)
            p.wait(Sc, m_dma)
            p.wait(T, m_dma)
            ln_stats(xt)
            gate_tile(xt, sel_store=selown_t[:, tb * E:(tb + 1) * E])
            ln_stats(xt)
            vf(lambda e, xt=xt: e.tensor_scalar(
                out=xnt_t[:, :], in0=xt[:, :], scalar1=stat_t[:, 2:3], scalar2=stat_t[:, 7:8],
                op0=ALU.subtract, op1=ALU.mult))

            def evac_R(half, psl, tb=tb):
                p.emit(V, lambda e, half=half, psl=psl, tb=tb: e.tensor_copy(
                    out=bass.AP(R_t, (half * 4) * S + tb * 128, [[8 * S, 128], [S, 4], [1, 128]]),
                    in_=bass.AP(psl, 0, [[512, 128], [128, 4], [1, 128]])), inc=m_V)
            transpose_tile_to(xt, evac_R)

            def evac_stage(half, psl):
                p.emit(V, lambda e, half=half, psl=psl: e.tensor_copy(
                    out=stage_t[:, half * 512:(half + 1) * 512], in_=psl[:, 0:512]), inc=m_V)
            transpose_tile_to(xnt_t, evac_stage)
            p.wait(G, m_V)
            for dc in range(8):
                p.emit(G, lambda e, tb=tb, dc=dc: e.dma_start(
                    out=bass.AP(xnT_hbm.tensor, dc * 128 * S + tb * 128, [[S, 128], [1, 128]]),
                    in_=bass.AP(stage_t, dc * 128, [[1024, 128], [1, 128]])), inc=m_gdma, inc_n=16)
            p.wait(V, m_gdma)

        p.wait(V, m_V)
        for tb in range(16):
            p.emit(V, lambda e, tb=tb: e.tensor_tensor(
                out=selown_t[:, tb * E:(tb + 1) * E], in0=selown_t[:, tb * E:(tb + 1) * E],
                in1=wmeanB_t[:, :], op=ALU.mult), inc=(m_V if tb == 15 else None))
        p.wait(T, m_V)
        for grp in range(4):
            for q in range(4):
                tb = grp * 4 + q
                p.emit(T, lambda e, tb=tb, q=q: e.transpose(
                    ps_a[0:E, q * 128:(q + 1) * 128],
                    selown_t[:, tb * E:(tb + 1) * E], ident_t[:, :]),
                    inc=(m_T if q == 3 else None))
            p.wait(V, m_T)
            p.wait(V, m_gdma)
            p.emit(V, lambda e: e.tensor_copy(
                out=stage_t[0:E, 0:512], in_=ps_a[0:E, 0:512]), inc=m_V)
            p.wait(G, m_V)
            for q in range(4):
                tb = grp * 4 + q
                p.emit(G, lambda e, tb=tb, q=q: e.dma_start(
                    out=selwT_t[E * tb:E * tb + E, 0:128],
                    in_=stage_t[0:E, q * 128:(q + 1) * 128]), inc=m_gdma, inc_n=16)
            p.wait(T, m_V)

        p.wait(V, m_V)
        for cb in range(8):
            p.emit(V, lambda e, cb=cb: e.tensor_scalar(
                out=R_t[:, cb * S:(cb + 1) * S], in0=R_t[:, cb * S:(cb + 1) * S],
                scalar1=projb_t[:, cb:cb + 1], scalar2=None, op0=ALU.add),
                inc=(m_V if cb == 7 else None))
        p.wait(V, m_V)

        if "d_wmean" in dbg:
            p.wait(G, m_V)
            p.emit(G, lambda e: e.dma_start(out=dbg["d_wmean"][:, :], in_=wmeanB_t[0:1, :]), inc=m_gdma, inc_n=16)
        if "d_xnT" in dbg:
            p.wait(G, m_gdma)
            p.emit(G, lambda e: e.dma_start(out=dbg["d_xnT"][:, :], in_=xnT_hbm[:, :]), inc=m_gdma, inc_n=16)


        def barrier():
            for eng in (V, Sc, T, G, Y):
                for s in (m_V, m_S, m_T, m_TA, m_TC, m_SE, m_G, m_dma, m_gdma, m_PS, m_PV):
                    p.wait(eng, s)

        barrier()

        NSLOT = 14
        kern_t = xt0_t
        yT_t = xnt_t
        recB_t = xt1_t
        unit = [0]
        nS_units = [0]
        nV_units = [0]

        def produce_first(base_u, ic, jgs):
            s_list, v_list = [], []
            for r, jg in enumerate(jgs):
                u = base_u + r
                (s_list if (u % 16) < 9 else v_list).append((u, jg))
            qsl = nqT2_t[:, ic * 512:(ic + 1) * 512]
            if s_list:
                p.wait(Sc, m_TC, max(u for u, _ in s_list) - (NSLOT - 1))
                for u, jg in s_list:
                    dsl = dbuf_t[:, (u % NSLOT) * 512:(u % NSLOT) * 512 + 512]
                    p.emit(Sc, lambda e, dsl=dsl, qsl=qsl, jg=jg: e.activation(
                        out=dsl, in_=qsl, func=ACT.Abs,
                        bias=kT2_t[:, jg:jg + 1], scale=1.0), inc=m_PS)
                    nS_units[0] += 1
            if v_list:
                p.wait(V, m_TC, max(u for u, _ in v_list) - (NSLOT - 1))
                for i, (u, jg) in enumerate(v_list):
                    dsl = dbuf_t[:, (u % NSLOT) * 512:(u % NSLOT) * 512 + 512]
                    p.emit(V, lambda e, dsl=dsl, qsl=qsl, jg=jg: e.tensor_scalar(
                        out=dsl, in0=qsl, scalar1=kT2_t[:, jg:jg + 1], scalar2=None,
                        op0=ALU.add), inc=(m_PV if i == len(v_list) - 1 else None))
            return v_list, p.val(m_PV), p.val(m_PS)

        def produce_ands(v_list, sub_count):
            if v_list:
                p.wait(V, m_PV, sub_count)
                for u, jg in v_list:
                    dsl = dbuf_t[:, (u % NSLOT) * 512:(u % NSLOT) * 512 + 512]
                    p.emit(V, lambda e, dsl=dsl: e.tensor_scalar(
                        out=dsl.bitcast(I32), in0=dsl.bitcast(I32), scalar1=ABSMASK,
                        scalar2=None, op0=ALU.bitwise_and), inc=m_PV)
                    nV_units[0] += 1
            return p.val(m_PV)

        for hp in range(HP):
            for mat in range(3):
                p.wait(Y, m_V)
                p.emit(Y, lambda e, mat=mat, hp=hp: e.dma_start(
                    out=sqt_t[:, :],
                    in_=qkv_w[mat * D + hp * 128:(mat * D + hp * 128) + 128, :]),
                    inc=m_dma, inc_n=16)
                p.emit(Y, lambda e, mat=mat, hp=hp: e.dma_start(
                    out=small_t[:, 48 + mat:49 + mat],
                    in_=qkv_b[mat * D + hp * 128:(mat * D + hp * 128) + 128, 0:1]),
                    inc=m_dma, inc_n=16)
                p.wait(T, m_dma)
                p.wait(T, m_V)
                for half in range(2):
                    psl = ps_a if half == 0 else ps_b
                    for q in range(4):
                        dc = half * 4 + q
                        p.emit(T, lambda e, dc=dc, q=q, psl=psl: e.transpose(
                            psl[0:128, q * 128:(q + 1) * 128],
                            sqt_t[:, dc * 128:(dc + 1) * 128], ident_t[:, :]),
                            inc=(m_T if q == 3 else None))
                    p.wait(V, m_T)
                    p.emit(V, lambda e, mat=mat, half=half, psl=psl: e.tensor_copy(
                        out=wt3_t[:, (mat * 8 + half * 4) * 128:(mat * 8 + half * 4 + 4) * 128],
                        in_=psl[:, 0:512]), inc=m_V)
            p.emit(Y, lambda e, hp=hp: e.dma_start(
                out=small_t[:, 52:55], in_=dwc_w[hp * 128:(hp + 1) * 128, :]), inc=m_dma, inc_n=16)
            p.emit(Y, lambda e, hp=hp: e.dma_start(
                out=small_t[:, 55:56], in_=dwc_b[hp * 128:(hp + 1) * 128, 0:1]), inc=m_dma, inc_n=16)

            for tc in range(4):
                p.wait(Y, m_T)
                p.wait(Y, m_gdma)
                for dc in range(8):
                    p.emit(Y, lambda e, tc=tc, dc=dc: e.dma_start(
                        out=bass.AP(xnt2_t, dc * 512, [[4096, 128], [1, 512]]),
                        in_=bass.AP(xnT_hbm.tensor, dc * 128 * S + tc * 512, [[S, 128], [1, 512]])),
                        inc=m_dma, inc_n=16)
                p.wait(T, m_dma)
                for mat in range(3):
                    psl = ps_a if mat % 2 == 0 else ps_b
                    p.wait(T, m_V)
                    for dc in range(8):
                        p.emit(T, lambda e, mat=mat, dc=dc, psl=psl: e.matmul(
                            psl[:, :], wt3_t[:, (mat * 8 + dc) * 128:(mat * 8 + dc + 1) * 128],
                            xnt2_t[:, dc * 512:(dc + 1) * 512],
                            start=(dc == 0), stop=(dc == 7)), inc=(m_T if dc == 7 else None))
                    dstT = (nqT2_t, kT2_t, vT2_t)[mat]
                    p.wait(V, m_T)
                    if mat == 0:
                        p.emit(V, lambda e, dstT=dstT, tc=tc, mat=mat, psl=psl: e.tensor_scalar(
                            out=dstT[:, tc * 512:(tc + 1) * 512], in0=psl[:, 0:512],
                            scalar1=small_t[:, 48 + mat:49 + mat], scalar2=-1.0,
                            op0=ALU.add, op1=ALU.mult), inc=m_V)
                    else:
                        p.emit(V, lambda e, dstT=dstT, tc=tc, mat=mat, psl=psl: e.tensor_scalar(
                            out=dstT[:, tc * 512:(tc + 1) * 512], in0=psl[:, 0:512],
                            scalar1=small_t[:, 48 + mat:49 + mat], scalar2=None, op0=ALU.add), inc=m_V)

            p.wait(V, m_gdma)
            p.wait(T, m_V)
            for grp in range(4):
                psl = ps_a if grp % 2 == 0 else ps_b
                p.wait(T, m_V)
                for q in range(4):
                    tb4 = grp * 4 + q
                    p.emit(T, lambda e, tb4=tb4, q=q, psl=psl: e.transpose(
                        psl[0:128, q * 128:(q + 1) * 128],
                        vT2_t[:, tb4 * 128:(tb4 + 1) * 128], ident_t[:, :]),
                        inc=(m_T if q == 3 else None))
                p.wait(V, m_T)
                p.emit(V, lambda e, grp=grp, psl=psl: e.tensor_copy(
                    out=vtok_t[:, grp * 512:(grp + 1) * 512], in_=psl[:, 0:512]), inc=m_V)
            if hp == 0:
                p.wait(V, m_V)
                p.emit(V, lambda e: e.memset(vaug0_t[:], 0.0))
                p.emit(V, lambda e: e.memset(vaug1_t[:], 0.0), inc=m_V)
                p.wait(V, m_V)
                p.emit(V, lambda e: e.memset(stage_t[0:16, 0:32], 1.0), inc=m_V)
                p.wait(G, m_V)
                for cblk in range(4):
                    p.emit(G, lambda e, cblk=cblk: e.dma_start(
                        out=vaug0_t[32 * cblk:32 * cblk + 16, 64:2080:65],
                        in_=stage_t[0:16, 0:32],
                        allow_slow_non_contiguous=True), inc=m_gdma, inc_n=16)
                    p.emit(G, lambda e, cblk=cblk: e.dma_start(
                        out=vaug1_t[32 * cblk + 16:32 * cblk + 32, 64:2080:65],
                        in_=stage_t[0:16, 0:32],
                        allow_slow_non_contiguous=True), inc=m_gdma, inc_n=16)
            p.wait(G, m_TA)
            p.wait(G, m_V)
            for g in range(32):
                p0 = (g % 2) * 64
                cc = (g // 2) * 128
                for cblk in range(4):
                    p.emit(G, lambda e, g=g, p0=p0, cc=cc, cblk=cblk: e.dma_start(
                        out=vaug0_t[32 * cblk:32 * cblk + 16, g * 65:g * 65 + 64],
                        in_=vtok_t[p0 + 16 * cblk:p0 + 16 * cblk + 16, cc:cc + 64]),
                        inc=m_gdma, inc_n=16)
                    p.emit(G, lambda e, g=g, p0=p0, cc=cc, cblk=cblk: e.dma_start(
                        out=vaug1_t[32 * cblk + 16:32 * cblk + 32, g * 65:g * 65 + 64],
                        in_=vtok_t[p0 + 16 * cblk:p0 + 16 * cblk + 16, cc + 64:cc + 128]),
                        inc=m_gdma, inc_n=16)

            p.wait(V, m_dma)
            vf(lambda e: e.tensor_scalar(
                out=localT_t[:, :], in0=vT2_t[:, :], scalar1=small_t[:, 53:54],
                scalar2=None, op0=ALU.mult))
            vf(lambda e: e.scalar_tensor_tensor(
                out=localT_t[:, 1:S], in0=vT2_t[:, 0:S - 1], scalar=small_t[:, 52:53],
                in1=localT_t[:, 1:S], op0=ALU.mult, op1=ALU.add))
            vf(lambda e: e.scalar_tensor_tensor(
                out=localT_t[:, 0:S - 1], in0=vT2_t[:, 1:S], scalar=small_t[:, 54:55],
                in1=localT_t[:, 0:S - 1], op0=ALU.mult, op1=ALU.add))
            vf(lambda e: e.tensor_scalar(
                out=localT_t[:, :], in0=localT_t[:, :], scalar1=small_t[:, 55:56],
                scalar2=None, op0=ALU.add))

            p.wait(Y, m_V)
            for ob in range(8):
                p.emit(Y, lambda e, hp=hp, ob=ob: e.dma_start(
                    out=bass.AP(xTg_t, ob * 128, [[1024, 128], [1, 128]]),
                    in_=bass.AP(proj_w.tensor, ob * 128 * D + hp * 128, [[D, 128], [1, 128]])),
                    inc=m_dma, inc_n=16)
            p.wait(T, m_dma)
            p.wait(T, m_V)
            for grp in range(2):
                psl = ps_a if grp == 0 else ps_b
                for q in range(4):
                    ob = grp * 4 + q
                    p.emit(T, lambda e, ob=ob, q=q, psl=psl: e.transpose(
                        psl[0:128, q * 128:(q + 1) * 128],
                        xTg_t[:, ob * 128:(ob + 1) * 128], ident_t[:, :]),
                        inc=(m_T if q == 3 else None))
                p.wait(V, m_T)
                p.emit(V, lambda e, grp=grp, psl=psl: e.tensor_copy(
                    out=wt3_t[:, (grp * 4) * 128:(grp * 4 + 4) * 128], in_=psl[:, 0:512]), inc=m_V)

            p.wait(Sc, m_V)
            p.wait(Sc, m_dma)
            kern_mTA = [0, 0]
            phase_steps = [[]]
            for ic in range(4):
                sig_wait = {0: {cg: p.val(m_SE) for cg in range(4)}}
                pending = None

                def consume(pend):
                    base_u, g, q4, v_list, sub_cnt, ps_cnt = pend
                    and_cnt = produce_ands(v_list, sub_cnt)
                    p.wait(T, m_PS, ps_cnt)
                    p.wait(T, m_PV, and_cnt)
                    tc_vals = []
                    for r in range(4):
                        t = q4 * 4 + r
                        u = base_u + r
                        slot = u % NSLOT
                        cg = t % 4
                        s_ = t // 4
                        if q4 == 0:
                            p.wait(T, m_SE, sig_wait[g][cg])
                        p.emit(T, lambda e, slot=slot, cg=cg, s_=s_, q4=q4: e.matmul(
                            ps_sig[32 * cg:32 * cg + 32, 512 * cg:512 * cg + 512],
                            masks_t[:, 32 * s_:32 * s_ + 32],
                            dbuf_t[:, slot * 512:slot * 512 + 512],
                            start=(q4 == 0), stop=(q4 == 15),
                            tile_position=(0, 32 * cg)), inc=m_TC)
                        tc_vals.append(p.val(m_TC))
                    if q4 == 15:
                        g_ = g
                        kslot = g_ % 2
                        p.wait(Sc, m_TA, kern_mTA[kslot])
                        nxt = {}
                        for cg in range(4):
                            p.wait(Sc, m_TC, tc_vals[cg])
                            p.emit(Sc, lambda e, cg=cg, kslot=kslot: e.activation(
                                out=kern_t[32 * cg:32 * cg + 32, kslot * 512:kslot * 512 + 512],
                                in_=ps_sig[32 * cg:32 * cg + 32, 512 * cg:512 * cg + 512],
                                func=ACT.Exp, bias=0.0, scale=LAMBDA_SCALE), inc=m_SE)
                            nxt[cg] = p.val(m_SE)
                        sig_wait[g_ + 1] = nxt
                        p.wait(T, m_SE)
                        if g_ == 0:
                            p.wait(T, m_V)
                            p.wait(T, m_gdma)
                        for h in range(2):
                            vaug = vaug0_t if h == 0 else vaug1_t
                            p.emit(T, lambda e, h=h, g_=g_, vaug=vaug, kslot=kslot: e.matmul(
                                ps_ga[0:65, h * 512:(h + 1) * 512],
                                vaug[:, g_ * 65:g_ * 65 + 65],
                                kern_t[:, kslot * 512:kslot * 512 + 512],
                                start=(g_ == 0), stop=(g_ == 31)),
                                inc=(m_TA if h == 1 else None))
                        kern_mTA[kslot] = p.val(m_TA)

                for g in range(32):
                    for q4 in range(16):
                        jgs = []
                        for r in range(4):
                            t = q4 * 4 + r
                            jprime = 16 * (t % 4) + (t // 4)
                            jgs.append(g * 64 + jprime)
                        v_list, sub_cnt, ps_cnt = produce_first(unit[0], ic, jgs)
                        cur = (unit[0], g, q4, v_list, sub_cnt, ps_cnt)
                        unit[0] += 4
                        if pending is not None:
                            consume(pending)
                            for _ in range(3):
                                if phase_steps[0]:
                                    phase_steps[0].pop(0)()
                        pending = cur
                consume(pending)
                pending = None
                def make_phase_steps(ic_, hp_):
                    steps = []
                    def s_wait_ga(e=None):
                        p.wait(V, m_TA)
                    steps.append(s_wait_ga)
                    for h in range(2):
                        def s_norm1(h=h):
                            vf(lambda e: e.tensor_scalar(
                                out=stage_t[0:1, 0:512], in0=ps_ga[64:65, h * 512:(h + 1) * 512],
                                scalar1=ATT_EPS, scalar2=None, op0=ALU.add))
                            vf(lambda e: e.reciprocal(
                                out=stage_t[0:1, 512:1024], in_=stage_t[0:1, 0:512]))
                        def s_bcast(h=h):
                            p.wait(T, m_V)
                            p.emit(T, lambda e: e.matmul(
                                ps_b[0:64, 0:512],
                                onesrow_t[0:1, 0:64], stage_t[0:1, 512:1024],
                                start=True, stop=True), inc=m_T)
                            p.wait(V, m_T)
                            vf(lambda e: e.tensor_copy(
                                out=recB_t[0:64, 0:512], in_=ps_b[0:64, 0:512]))
                        def s_y(h=h, ic_=ic_):
                            lsl = ic_ * 512
                            vf(lambda e, h=h: e.tensor_tensor(
                                out=yT_t[h * 64:h * 64 + 64, 0:512],
                                in0=ps_ga[0:64, h * 512:(h + 1) * 512],
                                in1=recB_t[0:64, 0:512], op=ALU.mult))
                            vf(lambda e, h=h, lsl=lsl: e.tensor_tensor(
                                out=yT_t[h * 64:h * 64 + 64, 0:512],
                                in0=yT_t[h * 64:h * 64 + 64, 0:512],
                                in1=localT_t[h * 64:h * 64 + 64, lsl:lsl + 512], op=ALU.add))
                        steps += [s_norm1, s_bcast, s_y]
                    for ob in range(8):
                        def s_proj(ob=ob, ic_=ic_):
                            p.wait(T, m_V)
                            p.emit(T, lambda e, ob=ob: e.matmul(
                                ps_a[:, :], wt3_t[:, ob * 128:(ob + 1) * 128],
                                yT_t[:, 0:512], start=True, stop=True), inc=m_T)
                            p.wait(V, m_T)
                            p.emit(V, lambda e, ob=ob, ic_=ic_: e.tensor_tensor(
                                out=R_t[:, ob * S + ic_ * 512: ob * S + (ic_ + 1) * 512],
                                in0=R_t[:, ob * S + ic_ * 512: ob * S + (ic_ + 1) * 512],
                                in1=ps_a[:, 0:512], op=ALU.add), inc=m_V)
                        steps.append(s_proj)
                    return steps

                phase_steps[0] = make_phase_steps(ic, hp)
                if ic == 3:
                    while phase_steps[0]:
                        phase_steps[0].pop(0)()


        barrier()
        prod_t = dbuf_t
        gu_t = kern_t
        selwB_t = recB_t
        ps_g = lambda: ps_sig[:, 0:512]
        ps_u = lambda: ps_sig[:, 512:1024]
        ps_d = lambda: ps_sig[:, 1024:1536]

        FFN_STAGE = (sqt_t, xTg_t)

        def slab_dma(buf, src_dram, row0, ncols):
            p.wait(Y, m_T)
            p.emit(Y, lambda e, buf=buf, src_dram=src_dram, row0=row0, ncols=ncols: e.dma_start(
                out=buf[:, 0:ncols], in_=src_dram[row0:row0 + 128, 0:ncols]),
                inc=m_dma, inc_n=16)
            dv = p.val(m_dma)
            p.wait(Y, m_dma, dv)
            p._qbound[id(m_dma)] = dv
            return dv

        def slab_transpose(buf, dst_off, nblk, dma_val):
            p.wait(T, m_dma, dma_val)
            p.wait(T, m_V)
            for grp in range((nblk + 3) // 4):
                psl = ps_a if grp % 2 == 0 else ps_b
                n_in_grp = min(4, nblk - grp * 4)
                for q in range(n_in_grp):
                    dc = grp * 4 + q
                    p.emit(T, lambda e, dc=dc, q=q, psl=psl, buf=buf: e.transpose(
                        psl[0:128, q * 128:(q + 1) * 128],
                        buf[:, dc * 128:(dc + 1) * 128], ident_t[:, :]),
                        inc=(m_T if q == n_in_grp - 1 else None))
                p.wait(V, m_T)
                p.emit(V, lambda e, grp=grp, n_in_grp=n_in_grp, psl=psl, dst_off=dst_off: e.tensor_copy(
                    out=wt3_t[:, dst_off + grp * 512:dst_off + grp * 512 + n_in_grp * 128],
                    in_=psl[:, 0:n_in_grp * 128]), inc=m_V)
                p.wait(T, m_V)

        def ffn_block(gw_d, uw_d, dw_d, gw_row0, dw_row0, n_sib, dw_ncols, selw_e):
            def emit_xnt(tq):
                p.wait(Y, m_T)
                for dc in range(8):
                    p.emit(Y, lambda e, tq=tq, dc=dc: e.dma_start(
                        out=bass.AP(xnt2_t, dc * 512, [[4096, 128], [1, 512]]),
                        in_=bass.AP(xnT_hbm.tensor, dc * 128 * S + tq * 512, [[S, 128], [1, 512]])),
                        inc=m_dma, inc_n=16)

            def emit_selw(tq):
                p.wait(T, m_dma)
                p.wait(T, m_V)
                p.wait(T, m_gdma)
                p.emit(T, lambda e, tq=tq: e.matmul(
                    ps_sig[:, 1536:2048], onesrow_t[0:1, :],
                    localT_t[0:1, tq * 512:(tq + 1) * 512],
                    start=True, stop=True), inc=m_T)
                p.wait(V, m_T)
                p.emit(V, lambda e: e.tensor_copy(
                    out=selwB_t[:, 512:1024], in_=ps_sig[:, 1536:2048]), inc=m_V)

            sched = []
            for tq in range(4):
                for sib in range(n_sib):
                    sched.append(("g", gw_d, gw_row0 + sib * 128, D, 8, 0, sib, tq))
                    sched.append(("u", uw_d, gw_row0 + sib * 128, D, 8, 1024, sib, tq))
                for cb in range(8):
                    sched.append(("d", dw_d, dw_row0 + cb * 128, dw_ncols, n_sib, 2048, cb, tq))
            per_tq = len(sched) // 4
            emit_xnt(0)
            dv = slab_dma(FFN_STAGE[0], sched[0][1], sched[0][2], sched[0][3])
            for k, (kind, dram_, row0, ncols, nblk, dst, idx, tq) in enumerate(sched):
                if selw_e is not None and k % per_tq == 0:
                    emit_selw(tq)
                dv_cur = dv
                if k + 1 < len(sched):
                    nx = sched[k + 1]
                    dv = slab_dma(FFN_STAGE[(k + 1) % 2], nx[1], nx[2], nx[3])
                slab_transpose(FFN_STAGE[k % 2], dst, nblk, dv_cur)
                if True:
                    if kind == "g":
                        p.wait(T, m_V)
                        for dc in range(8):
                            p.emit(T, lambda e, dc=dc: e.matmul(
                                ps_g(), wt3_t[:, dc * 128:(dc + 1) * 128],
                                xnt2_t[:, dc * 512:(dc + 1) * 512],
                                start=(dc == 0), stop=(dc == 7)), inc=(m_T if dc == 7 else None))
                        p.wait(Sc, m_T)
                        p.wait(Sc, m_V)
                        p.emit(Sc, lambda e: e.activation(
                            out=stage_t[:, 0:512], in_=ps_g(), func=ACT.Sigmoid), inc=m_S)
                    elif kind == "u":
                        p.wait(T, m_V)
                        for dc in range(8):
                            p.emit(T, lambda e, dc=dc: e.matmul(
                                ps_u(), wt3_t[:, 1024 + dc * 128:1024 + (dc + 1) * 128],
                                xnt2_t[:, dc * 512:(dc + 1) * 512],
                                start=(dc == 0), stop=(dc == 7)), inc=(m_T if dc == 7 else None))
                        p.wait(V, m_S)
                        p.wait(V, m_T)
                        vf(lambda e: e.tensor_tensor(
                            out=gu_t[:, 0:512], in0=ps_g(), in1=stage_t[:, 0:512], op=ALU.mult))
                        vf(lambda e, sib=idx: e.tensor_tensor(
                            out=prod_t[:, sib * 512:(sib + 1) * 512], in0=gu_t[:, 0:512],
                            in1=ps_u(), op=ALU.mult))
                    else:
                        p.wait(T, m_V)
                        for sic in range(n_sib):
                            p.emit(T, lambda e, sic=sic: e.matmul(
                                ps_d(), wt3_t[:, 2048 + sic * 128:2048 + (sic + 1) * 128],
                                prod_t[:, sic * 512:(sic + 1) * 512],
                                start=(sic == 0), stop=(sic == n_sib - 1)),
                                inc=(m_T if sic == n_sib - 1 else None))
                        p.wait(V, m_T)
                        cb = idx
                        if selw_e is None:
                            p.emit(V, lambda e, cb=cb, tq=tq: e.tensor_tensor(
                                out=R_t[:, cb * S + tq * 512: cb * S + (tq + 1) * 512],
                                in0=R_t[:, cb * S + tq * 512: cb * S + (tq + 1) * 512],
                                in1=ps_d(), op=ALU.add), inc=m_V)
                        else:
                            vf(lambda e: e.tensor_tensor(
                                out=stage_t[:, 512:1024], in0=ps_d(), in1=selwB_t[:, 512:1024],
                                op=ALU.mult))
                            p.emit(V, lambda e, cb=cb, tq=tq: e.tensor_tensor(
                                out=R_t[:, cb * S + tq * 512: cb * S + (tq + 1) * 512],
                                in0=R_t[:, cb * S + tq * 512: cb * S + (tq + 1) * 512],
                                in1=stage_t[:, 512:1024], op=ALU.add), inc=m_V)
                if tq < 3 and k % per_tq == 2 * n_sib - 1:
                    emit_xnt(tq + 1)

        ffn_block(sh_gw, sh_uw, sh_dw, 0, 0, 8, SI, None)
        for ex in range(E):
            p.wait(G, m_T)
            p.wait(G, m_V)
            for tb16 in range(16):
                p.emit(G, lambda e, ex=ex, tb16=tb16: e.dma_start(
                    out=localT_t[0:1, tb16 * 128:(tb16 + 1) * 128],
                    in_=selwT_t[E * tb16 + ex:E * tb16 + ex + 1, 0:128]), inc=m_gdma, inc_n=16)
            ffn_block(exp_gw, exp_uw, exp_dw, ex * I_EXP, ex * D, 4, I_EXP, ex)

        p.wait(T, m_V)
        for tb in range(16):
            p.wait(V, m_dma, p.val(m_dma))
            for half in range(2):
                psl = ps_a if half == 0 else ps_b
                p.wait(T, m_V)
                for q in range(4):
                    cb = half * 4 + q
                    p.emit(T, lambda e, cb=cb, q=q, tb=tb, psl=psl: e.transpose(
                        psl[0:128, q * 128:(q + 1) * 128],
                        R_t[:, cb * S + tb * 128: cb * S + (tb + 1) * 128],
                        ident_t[:, :]), inc=(m_T if q == 3 else None))
                p.wait(V, m_T)
                p.emit(V, lambda e, half=half, psl=psl: e.tensor_copy(
                    out=stage_t[:, half * 512:(half + 1) * 512], in_=psl[:, 0:512]), inc=m_V)
            p.wait(Y, m_V)
            p.emit(Y, lambda e, tb=tb: e.dma_start(
                out=out[tb * 128:(tb + 1) * 128, :], in_=stage_t[:, :]), inc=m_dma, inc_n=16)
        p.wait(G, m_dma)
        p.wait(G, m_gdma)

        p.finalize(block)

    return nc, dbg


def _inputs_common(L0_attn_qkv_w, L0_attn_qkv_b, L0_attn_proj_w, L0_attn_proj_b,
                   L0_attn_dwc_w, L0_attn_dwc_b, L0_ffn_gate_w, L0_ffn_lb_bias,
                   L0_ffn_expert_gate_w, L0_ffn_expert_up_w, L0_ffn_expert_down_w,
                   L0_ffn_shared_gate_w, L0_ffn_shared_up_w, L0_ffn_shared_down_w):
    f = np.float32
    return {
        "qkv_w": np.ascontiguousarray(L0_attn_qkv_w, dtype=f),
        "qkv_b": np.ascontiguousarray(np.asarray(L0_attn_qkv_b, dtype=f).reshape(3 * D, 1)),
        "proj_w": np.ascontiguousarray(L0_attn_proj_w, dtype=f),
        "proj_b": np.ascontiguousarray(np.asarray(L0_attn_proj_b, dtype=f).reshape(D, 1)),
        "dwc_w": np.ascontiguousarray(np.asarray(L0_attn_dwc_w, dtype=f).reshape(D, 3)),
        "dwc_b": np.ascontiguousarray(np.asarray(L0_attn_dwc_b, dtype=f).reshape(D, 1)),
        "gate_w": np.ascontiguousarray(L0_ffn_gate_w, dtype=f),
        "lb_bias": np.ascontiguousarray(np.asarray(L0_ffn_lb_bias, dtype=f).reshape(E, 1)),
        "exp_gw": np.ascontiguousarray(np.asarray(L0_ffn_expert_gate_w, dtype=f).reshape(E * I_EXP, D)),
        "exp_uw": np.ascontiguousarray(np.asarray(L0_ffn_expert_up_w, dtype=f).reshape(E * I_EXP, D)),
        "exp_dw": np.ascontiguousarray(np.asarray(L0_ffn_expert_down_w, dtype=f).reshape(E * D, I_EXP)),
        "sh_gw": np.ascontiguousarray(L0_ffn_shared_gate_w, dtype=f),
        "sh_uw": np.ascontiguousarray(L0_ffn_shared_up_w, dtype=f),
        "sh_dw": np.ascontiguousarray(L0_ffn_shared_down_w, dtype=f),
    }


def kernel(L0_attn_x, **kw):
    x = np.ascontiguousarray(np.asarray(L0_attn_x, dtype=np.float32))
    common = _inputs_common(**kw)
    common["xf"] = np.ascontiguousarray(x.reshape(N_ALL, D))
    nc, _ = build()
    in_maps = [dict(common, xo=np.ascontiguousarray(x[b])) for b in range(B)]
    res = run_bass_kernel_spmd(nc, in_maps, core_ids=list(range(B)))
    return np.stack([res.results[b]["out"] for b in range(B)], axis=0)
